# version 04
# baseline (speedup 1.0000x reference)
import math
import numpy as np
import ml_dtypes
import concourse.bass as bass
import concourse.mybir as mybir
from concourse.bass_utils import run_bass_kernel_spmd

F32 = mybir.dt.float32
BF16 = mybir.dt.bfloat16
AF = mybir.ActivationFunctionType
ALU = mybir.AluOpType
AX = mybir.AxisListType

NCORES = 8

FULL_CFG = dict(D=4096, S=8192, H=32, F=11008, QB=256, PN=256, ON=256,
                DN=256, G=6, MT_HALF=4, debug=False)

EPS_RMS = 1e-6
EPS_LN = 1e-5
INV_SQRT_DH = 1.0 / math.sqrt(128.0)
M_PAD = 0.25

SEM_NAMES = [
    "sX", "sGrow", "sConst", "sSL0", "sSL1", "sKst0", "sKst1", "sVst", "sKn",
    "sQH0", "sQH1", "sKH0", "sKH1", "sVH0", "sVH1", "sAst", "sAh",
    "sXp0", "sXp1", "sYst", "sYbf", "sYfm", "sUG0", "sUG1", "sDN0", "sDN1",
    "sDPst", "sDP0", "sDP1", "sR2st", "sP60", "sP61", "sOUT",
    "sMq2", "sKnl", "sCC", "sPE", "sACT", "sDVE",
]


class Plan:

    def __init__(self):
        self.ops = {k: [] for k in ("tensor", "scalar", "vector", "gpsimd", "sync")}
        self.led = {}

    def inc(self, sem, amt):
        self.led[sem] = self.led.get(sem, 0) + amt
        return self.led[sem]

    def val(self, sem):
        return self.led.get(sem, 0)

    def op(self, eng, fn):
        if eng == "vector":
            prior = self.val("sDVE")
            if prior:
                self.ops[eng].append(
                    lambda E, S, v=prior: E["vector"].wait_ge(S["sDVE"], v))
            self.inc("sDVE", 1)
            self.ops[eng].append(
                lambda E, S, fn=fn: fn(E, S).then_inc(S["sDVE"], 1))
        else:
            self.ops[eng].append(fn)

    def wait(self, eng, sem, v):
        if not v:
            return
        self.ops[eng].append(lambda E, S, sem=sem, v=v: E[eng].wait_ge(S[sem], v))


def build_program(cfg, part="all"):
    D = cfg["D"]; S = cfg["S"]; H = cfg["H"]; F = cfg["F"]
    T = S // NCORES
    NKT = D // 128
    NMT = T // 128
    QB = min(cfg["QB"], T); NQB = T // QB; QTB = QB // 128
    PN = cfg["PN"]; NPN = D // PN
    ON = cfg["ON"]; NON = D // ON
    DN = cfg["DN"]; NDN = D // DN
    G = cfg["G"]
    SK = S // 128
    NFT = F // 128
    FT_PASS = (NFT + 1) // 2
    MT_HALF = min(cfg["MT_HALF"], NMT)
    N_HALVES = NMT // MT_HALF
    TH = T // N_HALVES
    NG = (SK + G - 1) // G
    debug = cfg.get("debug", False)
    do_A = part in ("all", "A")
    do_B = part in ("all", "B")
    fused = part == "all"

    nc = bass.Bass("TRN2", target_bir_lowering=False)

    bias_d = {}
    if do_A:
        x_fm = nc.dram_tensor("x_fm", [D, T], F32, kind="ExternalInput").ap()
        wq = nc.dram_tensor("wq", [D, D], F32, kind="ExternalInput").ap()
        wk = nc.dram_tensor("wk", [D, D], F32, kind="ExternalInput").ap()
        wv = nc.dram_tensor("wv", [D, D], F32, kind="ExternalInput").ap()
        for p in ("q", "k", "v"):
            bias_d[p] = nc.dram_tensor("b" + p, [1, D], F32, kind="ExternalInput").ap()
        gq_d = nc.dram_tensor("gq", [1, D], F32, kind="ExternalInput").ap()
        gk_d = nc.dram_tensor("gk", [1, D], F32, kind="ExternalInput").ap()
    if do_B:
        x_tm = nc.dram_tensor("x_tm", [T, D], F32, kind="ExternalInput").ap()
        wo = nc.dram_tensor("wo", [D, D], F32, kind="ExternalInput").ap()
        wupT = nc.dram_tensor("wupT", [D, F], F32, kind="ExternalInput").ap()
        wgateT = nc.dram_tensor("wgateT", [D, F], F32, kind="ExternalInput").ap()
        wdownT = nc.dram_tensor("wdownT", [F, D], F32, kind="ExternalInput").ap()
        bias_d["o"] = nc.dram_tensor("bo", [1, D], F32, kind="ExternalInput").ap()
        onesf_d = nc.dram_tensor("onesf_in", [1, 128], F32, kind="ExternalInput").ap()
        out_d = nc.dram_tensor("out", [T, D], F32, kind="ExternalOutput").ap()
    ones_d = nc.dram_tensor("ones_in", [1, 128], BF16, kind="ExternalInput").ap()
    identf_d = nc.dram_tensor("identf_in", [128, 128], F32, kind="ExternalInput").ap()

    spill_kind = "ExternalOutput" if part == "A" else "Internal"
    if do_A:
        ktm_sh = nc.dram_tensor("ktm_sh", [T, D], BF16, kind=spill_kind).ap()
        vtm_sh = nc.dram_tensor("vtm_sh", [T, D], BF16, kind=spill_kind).ap()
        qtm_sp = nc.dram_tensor("qtm_sp", [T, D], BF16, kind=spill_kind).ap()
        kn_sh = nc.dram_tensor("kn_sh", [H, 1], F32, kind=spill_kind).ap()
        qn_sh = nc.dram_tensor("qn_sh", [H, 1], F32, kind=spill_kind).ap()
    if do_B:
        if part == "B":
            qtm_sp = nc.dram_tensor("qtm_in", [T, D], BF16, kind="ExternalInput").ap()
            ktm_all = nc.dram_tensor("ktm_all", [S, D], BF16,
                                     kind="ExternalInput").ap()
            vtm_all = nc.dram_tensor("vtm_all", [S, D], BF16,
                                     kind="ExternalInput").ap()
            kn_all = nc.dram_tensor("kn_all", [NCORES * H, 1], F32,
                                    kind="ExternalInput").ap()
            qn_in = nc.dram_tensor("qn_in", [H, 1], F32, kind="ExternalInput").ap()
        else:
            ktm_all = nc.dram_tensor("ktm_all", [S, D], BF16).ap()
            vtm_all = nc.dram_tensor("vtm_all", [S, D], BF16).ap()
            kn_all = nc.dram_tensor("kn_all", [NCORES * H, 1], F32).ap()
        attntm_d = nc.dram_tensor("attntm_d", [T, D], BF16).ap()
        y_d = nc.dram_tensor("y_d", [T, D], F32).ap()
        ybf_d = nc.dram_tensor("ybf_d", [T, D], BF16).ap()
        dpart_d = nc.dram_tensor("dpart_d", [T, D], F32).ap()
        r2_d = nc.dram_tensor("r2_d", [T, D], F32).ap()
    dbg = {}
    if debug:
        items = []
        if do_B:
            items += [("dbg_attntm", [T, D], BF16), ("dbg_y", [T, D], F32),
                      ("dbg_r2", [T, D], F32), ("dbg_negm", [1, H], F32)]
        if fused:
            items += [("dbg_qtm", [T, D], BF16), ("dbg_ktm_all", [S, D], BF16),
                      ("dbg_vtm_all", [S, D], BF16)]
        for nm, shp, dt in items:
            dbg[nm] = nc.dram_tensor(nm, shp, dt, kind="ExternalOutput").ap()

    from contextlib import ExitStack
    with ExitStack() as stk:
        A = stk.enter_context(nc.sbuf_tensor("arenaA", [128, 33024], BF16))
        B_SZ = max(NMT * D, 8 * D, 2 * MT_HALF * D, NKT * T, 2 * FT_PASS * DN)
        B = stk.enter_context(nc.sbuf_tensor("arenaB", [128, B_SZ], BF16))
        C_BUF = 8448
        C = stk.enter_context(nc.sbuf_tensor("arenaC", [128, 2 * C_BUF], BF16))
        Dd = stk.enter_context(nc.sbuf_tensor("arenaD", [128, 12288], BF16))
        gq_b = stk.enter_context(nc.sbuf_tensor("gq_b", [128, D], BF16))
        gk_b = stk.enter_context(nc.sbuf_tensor("gk_b", [128, D], BF16))
        ones_sb = stk.enter_context(nc.sbuf_tensor("ones_sb", [1, 128], BF16))
        onesf_sb = stk.enter_context(nc.sbuf_tensor("onesf_sb", [1, 128], F32))
        bias_pc = stk.enter_context(nc.sbuf_tensor("bias_pc", [1, 2 * PN], BF16))
        identf = stk.enter_context(nc.sbuf_tensor("identf", [128, 128], F32))
        negm = stk.enter_context(nc.sbuf_tensor("negm", [128, H], F32))
        ssqp = stk.enter_context(nc.sbuf_tensor("ssqp", [128, NMT * NPN], F32))
        stats = stk.enter_context(nc.sbuf_tensor("stats", [128, 64], F32))
        nrm = stk.enter_context(nc.sbuf_tensor("nrm", [128, 5 * H + 3 + NCORES], F32))
        nrow = stk.enter_context(nc.sbuf_tensor("nrow", [1, H], F32))
        PS = stk.enter_context(nc.psum_tensor("PS", [128, 4096], F32))

        sems = {s: stk.enter_context(nc.semaphore(s)) for s in SEM_NAMES}
        P = Plan()

        def bank(b):
            return PS[:, 512 * b:512 * (b + 1)]

        X_bf = A[:, 0:NKT * T].rearrange("p (n t) -> p n t", t=T)
        PT = A[:, 0:SK * QB].rearrange("p (k q) -> p k q", q=QB)
        VH_OFF = SK * QB
        VH = [A[:, VH_OFF + i * (SK * 129):VH_OFF + (i + 1) * (SK * 129)]
              .rearrange("p (k j) -> p k j", j=129) for i in range(2)]
        attn_half = A[:, 0:NKT * TH].rearrange("p (n t) -> p n t", t=TH)
        xpiece = [A[:, NKT * TH + 512 + i * 512:NKT * TH + 512 + (i + 1) * 512]
                  .bitcast(F32) for i in range(2)]
        h1 = A[:, 0:min(32, FT_PASS) * T].rearrange("p (n t) -> p n t", t=T)
        Af32 = A.ap().bitcast(F32)

        attn_tm = B[:, 0:NMT * D].rearrange("p (m d) -> p m d", d=D)
        r_half = B.ap().bitcast(F32)[:, 0:MT_HALF * D].rearrange(
            "p (m d) -> p m d", d=D)
        yfm_sb = B[:, 0:NKT * T].rearrange("p (n t) -> p n t", t=T)
        dn_slab = [B[:, i * (FT_PASS * DN):(i + 1) * (FT_PASS * DN)]
                   .rearrange("p (f d) -> p f d", d=DN) for i in range(2)]
        Bf32 = B.ap().bitcast(F32)

        slab = [C[:, i * C_BUF:i * C_BUF + NKT * PN]
                .rearrange("p (n m) -> p n m", m=PN) for i in range(2)]
        KH = [C[:, i * C_BUF:i * C_BUF + S] for i in range(2)]
        ug_slab = [C[:, i * C_BUF:i * C_BUF + 2 * NKT * 128] for i in range(2)]


        khat = [Dd[:, 0:D], Dd[:, D:2 * D]]
        dump = Dd[:, 2 * D:3 * D]
        dump2 = A[:, 32768:33024]
        QH = [Dd[:, i * T:(i + 1) * T] for i in range(2)]
        y_tm_f32 = Dd[:, 0:2 * D].bitcast(F32)
        p4dump = Dd[:, 2 * D:3 * D]
        h2 = (Dd[:, 0:(FT_PASS - 32) * T].rearrange("p (n t) -> p n t", t=T)
              if FT_PASS > 32 else None)
        gt_stage = Dd[:, 11264:12288]
        dpin = [C[:, 0:1024].bitcast(F32)[:, i * 256:(i + 1) * 256] for i in range(2)]
        dstg = [C[:, C_BUF:C_BUF + 1024].bitcast(F32)[:, i * 256:(i + 1) * 256]
                for i in range(2)]

        def hview(fl):
            if fl < 32:
                return h1[:, fl, :]
            return h2[:, fl - 32, :]

        sct = [0]

        def stat_slot():
            sl = sct[0] % 4; sct[0] += 1
            base = sl * 12
            return {i: stats[:, base + i:base + i + 1] for i in range(12)}

        s_rs = [stats[:, 48 + i:49 + i] for i in range(4)]
        c_eps_rms = stats[:, 52:53]
        c_eps_ln = stats[:, 53:54]
        c_zero = stats[:, 54:55]
        n2s = nrm[:, 0:H]
        kmaxAB = (nrm[:, H:2 * H], nrm[:, 2 * H:3 * H])
        qmaxAB = (nrm[:, 3 * H:4 * H], nrm[:, 4 * H:5 * H])
        mk2 = nrm[:, 5 * H:5 * H + 1]
        mq2 = nrm[:, 5 * H + 1:5 * H + 2]
        mcol = nrm[:, 5 * H + 2:5 * H + 3]
        knall_sb = nrm[:, 5 * H + 3:5 * H + 3 + NCORES]

        x_loaded = 0
        if do_A:
            P.op("gpsimd", lambda E, S: E["gpsimd"].dma_start(
                out=X_bf, in_=x_fm.rearrange("(n p) t -> p n t", p=128))
                .then_inc(S["sX"], 16))
            x_loaded = P.inc("sX", 16)
            P.op("gpsimd", lambda E, S: E["gpsimd"].dma_start(
                out=gq_b[0:1, :], in_=gq_d).then_inc(S["sGrow"], 16))
            P.inc("sGrow", 16)
            P.op("gpsimd", lambda E, S: E["gpsimd"].dma_start(
                out=gk_b[0:1, :], in_=gk_d).then_inc(S["sGrow"], 16))
            grow_ld = P.inc("sGrow", 16)
        const_list = [(ones_sb, ones_d), (identf, identf_d)]
        if do_B:
            const_list.append((onesf_sb, onesf_d))
        for dst, srcc in const_list:
            P.op("sync", lambda E, S, dst=dst, srcc=srcc: E["sync"].dma_start(
                out=dst[:, :], in_=srcc).then_inc(S["sConst"], 16))
            P.inc("sConst", 16)
        const_loaded = P.val("sConst")
        P.op("vector", lambda E, S: E["vector"].memset(c_eps_rms, EPS_RMS))
        P.op("vector", lambda E, S: E["vector"].memset(c_eps_ln, EPS_LN))
        P.op("vector", lambda E, S: E["vector"].memset(c_zero, 0.0))
        if do_A:
            P.op("vector", lambda E, S: E["vector"].memset(kmaxAB[1][:, :], -1e30))
            P.op("vector", lambda E, S: E["vector"].memset(qmaxAB[1][:, :], -1e30))
        memsets_done = P.val("sDVE")

        gains_ready = 0
        if do_A:
            P.wait("tensor", "sGrow", grow_ld)
            P.wait("tensor", "sConst", const_loaded)
            g_copy_at = {}
            gi = 0
            for gb in (gq_b, gk_b):
                for ntb in range(D // 512):
                    off = 2048 + (gi % 4) * 512
                    if gi >= 4:
                        P.wait("tensor", "sACT", g_copy_at[gi - 4])
                    P.op("tensor", lambda E, S, gb=gb, ntb=ntb, off=off:
                         E["tensor"].matmul(
                             PS[:, off:off + 512], ones_sb[:, :],
                             gb[0:1, ntb * 512:(ntb + 1) * 512],
                             start=True, stop=True).then_inc(S["sPE"], 1))
                    bc_mm = P.inc("sPE", 1)
                    P.wait("scalar", "sPE", bc_mm)
                    P.op("scalar", lambda E, S, gb=gb, ntb=ntb, off=off:
                         E["scalar"].copy(
                             out=gb[:, ntb * 512:(ntb + 1) * 512],
                             in_=PS[:, off:off + 512]).then_inc(S["sACT"], 1))
                    g_copy_at[gi] = P.inc("sACT", 1)
                    gi += 1
            gains_ready = P.val("sACT")

        w_of = {"k": wk, "v": wv, "q": wq} if do_A else {}
        g_of = {"k": gk_b, "q": gq_b}
        spill_of = {"k": ktm_sh, "q": qtm_sp} if do_A else {}
        nmax_of = {"k": kmaxAB, "q": qmaxAB}

        slab_done_at = {}
        evac_done_at = {}
        proj_mm_done = {"q": 0}
        norm_done = {}
        stt_at = {}
        khat_store_at = {}
        nmax_at = {}
        sqrtdone_at = {}
        kn_stored = 0
        qn_stored = 0
        red_at = {}
        mk2_reduce_done = 0
        vst_total = 0
        mk2_copy_done = 0

        projs = ("q", "k", "v") if do_A else ()
        for pi, p in enumerate(projs):
            wsrc = w_of[p].rearrange("(n p2) m -> p2 n m", p2=128)
            for nt in range(NPN):
                li = pi * NPN + nt
                buf = li % 2
                sl_sem = "sSL%d" % buf
                if li >= 2:
                    P.wait("gpsimd", "sPE", slab_done_at[li - 2])
                P.op("gpsimd", lambda E, S, b=buf, wsrc=wsrc, nt=nt, sl=sl_sem:
                     E["gpsimd"].dma_start(
                         out=slab[b], in_=wsrc[:, :, nt * PN:(nt + 1) * PN])
                     .then_inc(S[sl], 16))
                P.inc(sl_sem, 16)
                P.op("gpsimd", lambda E, S, b=buf, p=p, nt=nt, sl=sl_sem:
                     E["gpsimd"].dma_start(
                         out=bias_pc[:, b * PN:(b + 1) * PN],
                         in_=bias_d[p][:, nt * PN:(nt + 1) * PN]).then_inc(S[sl], 16))
                slab_ld = P.inc(sl_sem, 16)
                for mt in range(NMT):
                    bk_i = mt % 8
                    P.wait("tensor", sl_sem, slab_ld)
                    if li < 2:
                        P.wait("tensor", "sConst", const_loaded)
                        P.wait("tensor", "sACT", gains_ready)
                        P.wait("tensor", "sX", x_loaded)
                    prev = evac_done_at.get((pi, nt - 1, mt))
                    if prev:
                        P.wait("tensor", "sACT", prev)
                    if pi > 0 and nt == 0:
                        P.wait("tensor", "sACT", norm_done.get(projs[pi - 1], 0))
                    if pi == 2 and nt == 0 and mt == 2 and mk2_copy_done:
                        P.wait("tensor", "sACT", mk2_copy_done)

                    def mm_group(E, S, nt=nt, mt=mt, buf=buf, bk_i=bk_i):
                        for kt in range(NKT):
                            E["tensor"].matmul(
                                bank(bk_i)[:, 0:PN],
                                X_bf[:, kt, mt * 128:(mt + 1) * 128],
                                slab[buf][:, kt, :], start=(kt == 0),
                                stop=False).then_inc(S["sPE"], 1)
                        E["tensor"].matmul(
                            bank(bk_i)[:, 0:PN], ones_sb[:, :],
                            bias_pc[:, buf * PN:(buf + 1) * PN],
                            start=False, stop=True).then_inc(S["sPE"], 1)
                    P.op("tensor", mm_group)
                    mm_done = P.inc("sPE", NKT + 1)
                    if mt == NMT - 1:
                        slab_done_at[li] = mm_done
                    P.wait("scalar", "sPE", mm_done)
                    if li < 2 and mt == 0:
                        P.wait("scalar", "sDVE", memsets_done)
                    if p == "v":
                        if nt == 0:
                            P.wait("scalar", "sDVE", stt_at[("k", mt)])
                        P.op("scalar", lambda E, S, mt=mt, nt=nt, bk_i=bk_i:
                             E["scalar"].copy(
                                 out=attn_tm[:, mt, nt * PN:(nt + 1) * PN],
                                 in_=bank(bk_i)[:, 0:PN]).then_inc(S["sACT"], 1))
                        evac_done_at[(pi, nt, mt)] = P.inc("sACT", 1)
                    else:
                        if p == "k" and nt == 0:
                            P.wait("scalar", "sDVE", max(stt_at[("q", mt)],
                                                         red_at[("q", mt)]))
                        P.op("scalar", lambda E, S, mt=mt, nt=nt, bk_i=bk_i:
                             E["scalar"].copy(
                                 out=attn_tm[:, mt, nt * PN:(nt + 1) * PN],
                                 in_=bank(bk_i)[:, 0:PN]).then_inc(S["sACT"], 1))
                        P.op("scalar", lambda E, S, mt=mt, nt=nt, bk_i=bk_i:
                             E["scalar"].activation(
                                 out=dump2[:, 0:PN], in_=bank(bk_i)[:, 0:PN],
                                 func=AF.Square, bias=c_zero,
                                 accum_out=ssqp[:, mt * NPN + nt:mt * NPN + nt + 1])
                             .then_inc(S["sACT"], 1))
                        evac_done_at[(pi, nt, mt)] = P.inc("sACT", 2)
            proj_mm_done[p] = P.val("sPE")

            if p == "v":
                for mt in range(NMT):
                    P.wait("sync", "sACT", evac_done_at[(pi, NPN - 1, mt)])
                    P.op("sync", lambda E, S, mt=mt: E["sync"].dma_start(
                        out=vtm_sh[mt * 128:(mt + 1) * 128, :], in_=attn_tm[:, mt, :])
                        .then_inc(S["sVst"], 16))
                    P.inc("sVst", 16)
                vst_total = P.val("sVst")
                norm_done[p] = P.val("sACT")
                continue

            gbc = g_of[p]
            maxA, maxB = nmax_of[p]
            for mt in range(NMT):
                ev = evac_done_at[(pi, NPN - 1, mt)]
                ss = stat_slot()
                P.wait("vector", "sACT", ev)
                if mt == 0:
                    P.wait("vector", "sACT", gains_ready)
                    P.wait("vector", "sDVE", memsets_done)
                prevsq = sqrtdone_at.get((p, mt - 4))
                if prevsq:
                    P.wait("vector", "sACT", prevsq)
                P.op("vector", lambda E, S, mt=mt, ss=ss: E["vector"].reduce_sum(
                    ss[0], ssqp[:, mt * NPN:(mt + 1) * NPN], axis=AX.X))
                red_done = P.val("sDVE")
                red_at[(p, mt)] = red_done
                P.wait("scalar", "sDVE", red_done)
                P.op("scalar", lambda E, S, ss=ss: E["scalar"].activation(
                    out=ss[1], in_=ss[0], func=AF.Sqrt, scale=1.0 / D,
                    bias=c_eps_rms).then_inc(S["sACT"], 1))
                sq_done = P.inc("sACT", 1)
                sqrtdone_at[(p, mt)] = sq_done
                P.wait("vector", "sACT", sq_done)
                P.op("vector", lambda E, S, ss=ss: E["vector"].reciprocal(ss[2], ss[1]))
                if p == "q":
                    P.op("vector", lambda E, S, ss=ss: E["vector"].tensor_scalar(
                        out=ss[3], in0=ss[2], scalar1=INV_SQRT_DH, scalar2=None,
                        op0=ALU.mult))
                    rstd_i = 3
                else:
                    rstd_i = 2
                kh = khat[mt % 2]
                ks_sem = "sKst%d" % (mt % 2)
                prev_store = khat_store_at.get((p, mt - 2))
                if prev_store:
                    P.wait("vector", ks_sem, prev_store)
                if p == "k" and mt <= 1:
                    cands = [v for (pp, m2), v in khat_store_at.items()
                             if pp == "q" and m2 % 2 == mt % 2]
                    if cands:
                        P.wait("vector", ks_sem, max(cands))
                P.op("vector", lambda E, S, mt=mt, ss=ss, r=rstd_i, gbc=gbc, kh=kh:
                     E["vector"].scalar_tensor_tensor(
                         out=kh[:, :], in0=attn_tm[:, mt, :], scalar=ss[r],
                         in1=gbc[:, :], op0=ALU.mult, op1=ALU.mult))
                stt_at[(p, mt)] = P.val("sDVE")
                P.wait("sync", "sDVE", stt_at[(p, mt)])
                P.op("sync", lambda E, S, mt=mt, kh=kh, p=p, ks=ks_sem:
                     E["sync"].dma_start(
                         out=spill_of[p][mt * 128:(mt + 1) * 128, :], in_=kh[:, :])
                     .then_inc(S[ks], 16))
                khat_store_at[(p, mt)] = P.inc(ks_sem, 16)
                P.wait("scalar", "sDVE", stt_at[(p, mt)])
                pn = nmax_at.get((p, mt - 1))
                if pn:
                    P.wait("scalar", "sDVE", pn)
                P.op("scalar", lambda E, S, kh=kh: E["scalar"].activation(
                    out=dump[:, :], in_=kh[:, :], func=AF.Square, bias=c_zero)
                    .then_inc(S["sACT"], 1))
                sqr_done = P.inc("sACT", 1)
                P.wait("vector", "sACT", sqr_done)
                P.op("vector", lambda E, S: E["vector"].tensor_reduce(
                    out=n2s[:, :], in_=dump.rearrange("p (h j) -> p h j", j=128),
                    axis=AX.X, op=ALU.add))
                srcm, dstm = (maxB, maxA) if mt % 2 == 0 else (maxA, maxB)
                P.op("vector", lambda E, S, s=srcm, d=dstm: E["vector"].tensor_tensor(
                    out=d[:, :], in0=s[:, :], in1=n2s[:, :], op=ALU.max))
                nmax_at[(p, mt)] = P.val("sDVE")
            norm_done[p] = P.val("sACT")

            if p == "k":
                fin = kmaxAB[0] if NMT % 2 == 1 else kmaxAB[1]
                P.wait("tensor", "sDVE", nmax_at[("k", NMT - 1)])
                P.op("tensor", lambda E, S, fin=fin: E["tensor"].transpose(
                    PS[0:H, 1024:1152], fin[:, :], identf[:, :]).then_inc(S["sPE"], 1))
                t_done = P.inc("sPE", 1)
                P.wait("scalar", "sPE", t_done)
                P.op("scalar", lambda E, S: E["scalar"].copy(
                    out=dump.bitcast(F32)[0:H, 0:128], in_=PS[0:H, 1024:1152])
                    .then_inc(S["sACT"], 1))
                c_done = P.inc("sACT", 1)
                mk2_copy_done = c_done
                P.wait("vector", "sACT", c_done)
                P.op("vector", lambda E, S: E["vector"].reduce_max(
                    mk2[0:H, :], dump.bitcast(F32)[0:H, 0:128], axis=AX.X))
                mk_done = mk2_reduce_done = P.val("sDVE")
                P.wait("sync", "sDVE", mk_done)
                P.op("sync", lambda E, S: E["sync"].dma_start(
                    out=kn_sh, in_=mk2[0:H, :]).then_inc(S["sKn"], 16))
                kn_stored = P.inc("sKn", 16)

        if do_A:
            finq = qmaxAB[0] if NMT % 2 == 1 else qmaxAB[1]
            P.wait("tensor", "sDVE", nmax_at[("q", NMT - 1)])
            P.op("tensor", lambda E, S: E["tensor"].transpose(
                PS[0:H, 1024:1152], finq[:, :], identf[:, :]).then_inc(S["sPE"], 1))
            tq_done = P.inc("sPE", 1)
            P.wait("scalar", "sPE", tq_done)
            if mk2_reduce_done:
                P.wait("scalar", "sDVE", mk2_reduce_done)
            P.op("scalar", lambda E, S: E["scalar"].copy(
                out=dump.bitcast(F32)[0:H, 0:128], in_=PS[0:H, 1024:1152])
                .then_inc(S["sACT"], 1))
            cq_done = P.inc("sACT", 1)
            P.wait("vector", "sACT", cq_done)
            P.op("vector", lambda E, S: E["vector"].reduce_max(
                mq2[0:H, :], dump.bitcast(F32)[0:H, 0:128], axis=AX.X))
            mqx_done = P.val("sDVE")
            P.wait("sync", "sDVE", mqx_done)
            P.wait("sync", "sKn", kn_stored)
            P.op("sync", lambda E, S: E["sync"].dma_start(
                out=qn_sh, in_=mq2[0:H, :]).then_inc(S["sKn"], 16))
            qn_stored = P.inc("sKn", 16)
        if part == "A":
            P.wait("sync", "sKst0", P.val("sKst0"))
            P.wait("sync", "sKst1", P.val("sKst1"))
            P.wait("sync", "sVst", P.val("sVst"))
            P.wait("sync", "sKn", P.val("sKn"))

        agn_done = agv_done = 0
        if fused:
            P.wait("gpsimd", "sKst0", P.val("sKst0"))
            P.wait("gpsimd", "sKst1", P.val("sKst1"))
            P.op("gpsimd", lambda E, S: E["gpsimd"].collective_compute(
                "AllGather", ALU.bypass, replica_groups=[list(range(NCORES))],
                ins=[ktm_sh.opt()], outs=[ktm_all.opt()]).then_inc(S["sCC"], 16))
            P.inc("sCC", 16)
            P.wait("gpsimd", "sKn", qn_stored)
            P.op("gpsimd", lambda E, S: E["gpsimd"].collective_compute(
                "AllGather", ALU.bypass, replica_groups=[list(range(NCORES))],
                ins=[kn_sh.opt()], outs=[kn_all.opt()]).then_inc(S["sCC"], 16))
            agn_done = P.inc("sCC", 16)
            P.wait("gpsimd", "sVst", vst_total)
            P.op("gpsimd", lambda E, S: E["gpsimd"].collective_compute(
                "AllGather", ALU.bypass, replica_groups=[list(range(NCORES))],
                ins=[vtm_sh.opt()], outs=[vtm_all.opt()]).then_inc(S["sCC"], 16))
            agv_done = P.inc("sCC", 16)

        if do_B:
            if part == "B":
                P.op("sync", lambda E, S: E["sync"].dma_start(
                    out=mq2[0:H, :], in_=qn_in).then_inc(S["sMq2"], 16))
                mq2_ld = P.inc("sMq2", 16)
                P.wait("vector", "sMq2", mq2_ld)
            if fused:
                P.wait("sync", "sCC", agn_done)
            P.op("sync", lambda E, S: E["sync"].dma_start(
                out=knall_sb[0:H, :],
                in_=kn_all.rearrange("(r h) o -> h (r o)", r=NCORES),
                allow_slow_non_contiguous=True).then_inc(S["sKnl"], 16))
            knall_ld = P.inc("sKnl", 16)
            P.wait("vector", "sKnl", knall_ld)
            P.wait("vector", "sDVE", memsets_done)
            P.op("vector", lambda E, S: E["vector"].reduce_max(
                mk2[0:H, :], knall_sb[0:H, :], axis=AX.X))
            P.op("vector", lambda E, S: E["vector"].tensor_tensor(
                out=mcol[0:H, :], in0=mq2[0:H, :], in1=mk2[0:H, :], op=ALU.mult))
            m2_done = P.val("sDVE")
            P.wait("scalar", "sDVE", m2_done)
            P.op("scalar", lambda E, S: E["scalar"].activation(
                out=mcol[0:H, :], in_=mcol[0:H, :], func=AF.Sqrt,
                bias=c_zero[0:H, :]).then_inc(S["sACT"], 1))
            msq_done = P.inc("sACT", 1)
            P.wait("vector", "sACT", msq_done)
            P.op("vector", lambda E, S: E["vector"].tensor_scalar(
                out=mcol[0:H, :], in0=mcol[0:H, :], scalar1=M_PAD, scalar2=-1.0,
                op0=ALU.add, op1=ALU.mult))
            mneg_done = P.val("sDVE")
            P.wait("tensor", "sDVE", mneg_done)
            if part == "B":
                P.wait("tensor", "sConst", const_loaded)
            P.op("tensor", lambda E, S: E["tensor"].transpose(
                PS[0:1, 1024:1024 + H], mcol[0:H, :], identf[0:H, 0:H])
                .then_inc(S["sPE"], 1))
            tm_done = P.inc("sPE", 1)
            P.wait("scalar", "sPE", tm_done)
            P.op("scalar", lambda E, S: E["scalar"].copy(
                out=nrow[:, :], in_=PS[0:1, 1024:1024 + H]).then_inc(S["sACT"], 1))
            nrow_done = P.inc("sACT", 1)
            P.wait("tensor", "sACT", nrow_done)
            P.op("tensor", lambda E, S: E["tensor"].matmul(
                PS[:, 1536:1536 + H], onesf_sb[:, :], nrow[:, :],
                start=True, stop=True).then_inc(S["sPE"], 1))
            nbc_mm = P.inc("sPE", 1)
            P.wait("scalar", "sPE", nbc_mm)
            P.op("scalar", lambda E, S: E["scalar"].copy(
                out=negm[:, :], in_=PS[:, 1536:1536 + H]).then_inc(S["sACT"], 1))
            negm_ready = P.inc("sACT", 1)

            P.wait("vector", "sPE", proj_mm_done["q"])
            for i in range(2):
                P.op("vector", lambda E, S, i=i: E["vector"].memset(
                    VH[i][:, :, 128:129], 1.0))
            vones_ready = P.val("sDVE")

            kv_ld_at = {}
            head_pe_done = {}
            vtm_all_r = vtm_all.rearrange("(k p) (n j) -> n p k j", p=128, j=128)

            def emit_head_loads(h):
                buf = h % 2
                qs, ks, vs = "sQH%d" % buf, "sKH%d" % buf, "sVH%d" % buf
                if h >= 2:
                    P.wait("sync", "sPE", head_pe_done[h - 2])
                else:
                    P.wait("sync", "sPE", proj_mm_done["q"])
                    P.wait("sync", "sDVE", vones_ready)
                    if fused:
                        P.wait("sync", "sCC", agv_done)
                        P.wait("sync", "sKst0", P.val("sKst0"))
                        P.wait("sync", "sKst1", P.val("sKst1"))
                P.op("sync", lambda E, S, h=h, buf=buf, qs=qs:
                     E["sync"].dma_start_transpose(
                         out=QH[buf][:, :], in_=qtm_sp[:, h * 128:(h + 1) * 128])
                     .then_inc(S[qs], 16))
                q_ld = P.inc(qs, 16)
                P.op("sync", lambda E, S, h=h, buf=buf, ks=ks:
                     E["sync"].dma_start_transpose(
                         out=KH[buf][:, :], in_=ktm_all[:, h * 128:(h + 1) * 128])
                     .then_inc(S[ks], 16))
                k_ld = P.inc(ks, 16)
                P.op("sync", lambda E, S, h=h, buf=buf, vs=vs: E["sync"].dma_start(
                    out=VH[buf][:, :, 0:128], in_=vtm_all_r[h]).then_inc(S[vs], 16))
                v_ld = P.inc(vs, 16)
                kv_ld_at[h] = (qs, q_ld, ks, k_ld, vs, v_ld)

            PV0 = 2 * G * QB
            assert PV0 + QTB * 512 <= 4096
            exp_done = {}
            pv_done = {}
            attn_written = {}

            emit_head_loads(0)
            if H > 1:
                emit_head_loads(1)
            for h in range(H):
                buf = h % 2
                for qb in range(NQB):
                    pv_g = None
                    for g in range(NG):
                        kts = list(range(g * G, min((g + 1) * G, SK)))
                        sco = (g % 2) * (G * QB)
                        if g == 0 and qb == 0:
                            qs, qv, ks, kv2, vs, vv = kv_ld_at[h]
                            P.wait("tensor", qs, qv)
                            P.wait("tensor", ks, kv2)
                            P.wait("tensor", vs, vv)
                        if h == 0 and qb == 0 and g <= 1:
                            P.wait("tensor", "sACT", negm_ready)
                        prev_exp = exp_done.get((h, qb, g - 2))
                        if not prev_exp and g - 2 + NG >= 0:
                            if qb > 0:
                                prev_exp = exp_done.get((h, qb - 1, g - 2 + NG))
                            elif h > 0:
                                prev_exp = exp_done.get((h - 1, NQB - 1, g - 2 + NG))
                        if prev_exp:
                            P.wait("tensor", "sACT", prev_exp)

                        def smm(E, S, kts=kts, sco=sco, qb=qb, buf=buf):
                            for i, kt in enumerate(kts):
                                E["tensor"].matmul(
                                    PS[:, sco + i * QB:sco + (i + 1) * QB],
                                    KH[buf][:, kt * 128:(kt + 1) * 128],
                                    QH[buf][:, qb * QB:(qb + 1) * QB],
                                    start=True, stop=True).then_inc(S["sPE"], 1)
                        P.op("tensor", smm)
                        sc_done = P.inc("sPE", len(kts))
                        P.wait("scalar", "sPE", sc_done)
                        prev_pv = pv_done.get((h, qb - 1)) if qb > 0 else (
                            pv_done.get((h - 1, NQB - 1)) if h > 0 else None)
                        if prev_pv and g == 0:
                            P.wait("scalar", "sPE", prev_pv)
                        if h == 0 and qb == 0 and g == 0:
                            P.wait("scalar", "sACT", negm_ready)

                        def expg(E, S, kts=kts, sco=sco, h=h):
                            E["scalar"].activation(
                                out=PT[:, kts[0]:kts[0] + len(kts), :],
                                in_=PS[:, sco:sco + len(kts) * QB],
                                func=AF.Exp, bias=negm[:, h:h + 1], scale=1.0)\
                                .then_inc(S["sACT"], 1)
                        P.op("scalar", expg)
                        exp_done[(h, qb, g)] = P.inc("sACT", 1)
                        for gg in ([g - 1] if g >= 1 else []) + \
                                ([g] if g == NG - 1 else []):
                            kts2 = list(range(gg * G, min((gg + 1) * G, SK)))
                            P.wait("tensor", "sACT", exp_done[(h, qb, gg)])
                            if gg == 0:
                                prev_norm = attn_written.get((h, qb - 1)) if qb > 0 \
                                    else (attn_written.get((h - 1, NQB - 1))
                                          if h > 0 else None)
                                if prev_norm:
                                    P.wait("tensor", "sDVE", prev_norm)

                            def pvg(E, S, kts2=kts2, buf=buf):
                                for kt in kts2:
                                    for qt in range(QTB):
                                        E["tensor"].matmul(
                                            PS[:, PV0 + qt * 512:
                                               PV0 + qt * 512 + 129],
                                            PT[:, kt, qt * 128:(qt + 1) * 128],
                                            VH[buf][:, kt, :],
                                            start=(kt == 0), stop=(kt == SK - 1),
                                            skip_group_check=True)\
                                            .then_inc(S["sPE"], 1)
                            P.op("tensor", pvg)
                            pv_g = P.inc("sPE", len(kts2) * QTB)
                    pv_done[(h, qb)] = pv_g
                    P.wait("vector", "sPE", pv_done[(h, qb)])
                    for qt in range(QTB):
                        tt = qb * QTB + qt
                        P.op("vector", lambda E, S, qt=qt: E["vector"].reciprocal(
                            s_rs[qt],
                            PS[:, PV0 + qt * 512 + 128:PV0 + qt * 512 + 129]))
                        P.op("vector", lambda E, S, qt=qt, tt=tt, h=h:
                             E["vector"].tensor_scalar(
                                 out=attn_tm[:, tt, h * 128:(h + 1) * 128],
                                 in0=PS[:, PV0 + qt * 512:PV0 + qt * 512 + 128],
                                 scalar1=s_rs[qt], scalar2=None, op0=ALU.mult))
                    attn_written[(h, qb)] = P.val("sDVE")
                head_pe_done[h] = pv_done[(h, NQB - 1)]
                if h + 2 < H:
                    emit_head_loads(h + 2)
            for qb in range(NQB):
                for qt in range(QTB):
                    tt = qb * QTB + qt
                    P.wait("sync", "sDVE", attn_written[(H - 1, qb)])
                    P.op("sync", lambda E, S, tt=tt: E["sync"].dma_start(
                        out=attntm_d[tt * 128:(tt + 1) * 128, :],
                        in_=attn_tm[:, tt, :]).then_inc(S["sAst"], 16))
                    P.inc("sAst", 16)
            attntm_stored = P.val("sAst")

            wo_r = wo.rearrange("(n p2) m -> p2 n m", p2=128)
            o_slab_idx = 0
            o_slab_free = {}
            o_evac_at = {}
            ybf_store_at = {}
            first_dve_p4 = True
            prev_half_yst = prev_half_ybf = 0
            pending_ybf = []
            for half in range(N_HALVES):
                P.wait("sync", "sAst", attntm_stored)
                P.wait("scalar", "sAst", attntm_stored)
                if half > 0:
                    P.wait("sync", "sPE", o_slab_free[o_slab_idx - 1])
                    P.wait("scalar", "sPE", o_slab_free[o_slab_idx - 1])
                for kt in range(NKT):
                    P.op("sync", lambda E, S, kt=kt, half=half:
                         E["sync"].dma_start_transpose(
                             out=attn_half[:, kt, :],
                             in_=attntm_d[half * TH:(half + 1) * TH,
                                          kt * 128:(kt + 1) * 128])
                         .then_inc(S["sAh"], 16))
                    P.inc("sAh", 16)
                ah_ld = P.val("sAh")
                for nt in range(NON):
                    li = o_slab_idx; o_slab_idx += 1
                    buf = li % 2
                    sl_sem = "sSL%d" % buf
                    if li >= 2:
                        P.wait("gpsimd", "sPE", o_slab_free[li - 2])
                    else:
                        P.wait("gpsimd", "sPE", head_pe_done[H - 1])
                    P.op("gpsimd", lambda E, S, buf=buf, nt=nt, sl=sl_sem:
                         E["gpsimd"].dma_start(
                             out=slab[buf][:, :, 0:ON],
                             in_=wo_r[:, :, nt * ON:(nt + 1) * ON])
                         .then_inc(S[sl], 16))
                    P.inc(sl_sem, 16)
                    P.op("gpsimd", lambda E, S, buf=buf, nt=nt, sl=sl_sem:
                         E["gpsimd"].dma_start(
                             out=bias_pc[:, buf * PN:buf * PN + ON],
                             in_=bias_d["o"][:, nt * ON:(nt + 1) * ON])
                         .then_inc(S[sl], 16))
                    sl_ld = P.inc(sl_sem, 16)
                    if nt == 1 and pending_ybf:
                        for fn in pending_ybf:
                            P.ops["gpsimd"].append(fn)
                        pending_ybf = []
                    for mt in range(MT_HALF):
                        mtg = half * MT_HALF + mt
                        bk_i = mt % 4 + 4 * (nt % 2)
                        li2 = nt * MT_HALF + mt
                        xp = xpiece[li2 % 2]
                        xp_sem = "sXp%d" % (li2 % 2)
                        prev_x = o_evac_at.get(("x", half, li2 - 2))
                        if prev_x:
                            P.wait("sync", "sDVE", prev_x)
                        P.op("sync", lambda E, S, xp=xp, mtg=mtg, nt=nt, xs=xp_sem:
                             E["sync"].dma_start(
                                 out=xp, in_=x_tm[mtg * 128:(mtg + 1) * 128,
                                                  nt * ON:(nt + 1) * ON])
                             .then_inc(S[xs], 16))
                        xp_ld = P.inc(xp_sem, 16)
                        P.wait("tensor", sl_sem, sl_ld)
                        if li < 2:
                            P.wait("tensor", "sAh", ah_ld)
                            P.wait("tensor", "sConst", const_loaded)
                        if half > 0 and nt == 0:
                            P.wait("tensor", "sAh", ah_ld)
                        prev_ev = o_evac_at.get((half, nt - 2, mt))
                        if prev_ev:
                            P.wait("tensor", "sDVE", prev_ev)

                        def omm(E, S, nt=nt, mt=mt, buf=buf, bk_i=bk_i):
                            for kt in range(NKT):
                                E["tensor"].matmul(
                                    bank(bk_i)[:, 0:ON],
                                    attn_half[:, kt, mt * 128:(mt + 1) * 128],
                                    slab[buf][:, kt, 0:ON],
                                    start=(kt == 0), stop=False)\
                                    .then_inc(S["sPE"], 1)
                            E["tensor"].matmul(
                                bank(bk_i)[:, 0:ON], ones_sb[:, :],
                                bias_pc[:, buf * PN:buf * PN + ON],
                                start=False, stop=True).then_inc(S["sPE"], 1)
                        P.op("tensor", omm)
                        mm_done = P.inc("sPE", NKT + 1)
                        if mt == MT_HALF - 1:
                            o_slab_free[li] = mm_done
                        P.wait("vector", "sPE", mm_done)
                        P.wait("vector", xp_sem, xp_ld)
                        if first_dve_p4:
                            P.wait("vector", "sAst", attntm_stored)
                            first_dve_p4 = False
                        if half > 0 and nt == 0 and mt == 0:
                            P.wait("vector", "sYst", prev_half_yst)
                            P.wait("vector", "sYbf", prev_half_ybf)
                        P.op("vector", lambda E, S, mt=mt, nt=nt, bk_i=bk_i, xp=xp:
                             E["vector"].tensor_tensor(
                                 out=r_half[:, mt, nt * ON:(nt + 1) * ON],
                                 in0=bank(bk_i)[:, 0:ON], in1=xp, op=ALU.add))
                        ev = P.val("sDVE")
                        o_evac_at[(half, nt, mt)] = ev
                        o_evac_at[("x", half, li2)] = ev
                while sct[0] % 4:
                    sct[0] += 1
                sss = [stat_slot() for _ in range(4)]

                def scol(j):
                    v = stats[:, 0:48].rearrange("p (s c) -> p s c", c=12)
                    return v[:, 0:MT_HALF, j]

                sq_last = 0
                for mt in range(MT_HALF):
                    ss = sss[mt % 4]
                    last_ev = o_evac_at[(half, NON - 1, mt)]
                    P.wait("vector", "sDVE", last_ev)
                    P.op("vector", lambda E, S, mt=mt, ss=ss: E["vector"].reduce_sum(
                        ss[0], r_half[:, mt, :], axis=AX.X, negate=True))
                    P.wait("scalar", "sDVE", last_ev)
                    P.op("scalar", lambda E, S, mt=mt, ss=ss: E["scalar"].activation(
                        out=p4dump[:, :], in_=r_half[:, mt, :], func=AF.Square,
                        bias=c_zero, accum_out=ss[3]).then_inc(S["sACT"], 1))
                    sq_last = P.inc("sACT", 1)
                P.wait("vector", "sACT", sq_last)
                P.op("vector", lambda E, S: E["vector"].tensor_scalar(
                    out=scol(1), in0=scol(0), scalar1=1.0 / D, scalar2=None,
                    op0=ALU.mult))
                P.op("vector", lambda E, S: E["vector"].tensor_tensor(
                    out=scol(2), in0=scol(1), in1=scol(1), op=ALU.mult))
                P.op("vector", lambda E, S: E["vector"].tensor_scalar(
                    out=scol(4), in0=scol(3), scalar1=1.0 / D, scalar2=None,
                    op0=ALU.mult))
                P.op("vector", lambda E, S: E["vector"].tensor_tensor(
                    out=scol(5), in0=scol(4), in1=scol(2), op=ALU.subtract))
                var_done = P.val("sDVE")
                P.wait("scalar", "sDVE", var_done)
                P.op("scalar", lambda E, S: E["scalar"].activation(
                    out=scol(6), in_=scol(5), func=AF.Sqrt, scale=1.0,
                    bias=c_eps_ln).then_inc(S["sACT"], 1))
                sd_done = P.inc("sACT", 1)
                P.wait("vector", "sACT", sd_done)
                P.op("vector", lambda E, S: E["vector"].reciprocal(
                    scol(7), scol(6)))
                P.op("vector", lambda E, S: E["vector"].tensor_tensor(
                    out=scol(8), in0=scol(1), in1=scol(7), op=ALU.mult))
                rstd_done = P.val("sDVE")
                P.wait("scalar", "sDVE", rstd_done)
                yst_at = {}
                for mt in range(MT_HALF):
                    mtg = half * MT_HALF + mt
                    ss = sss[mt % 4]
                    ysl = y_tm_f32[:, 0:D]
                    prev = yst_at.get(mt - 1)
                    if prev:
                        P.wait("scalar", "sYst", prev)
                    P.op("scalar", lambda E, S, mt=mt, ss=ss, ysl=ysl:
                         E["scalar"].activation(
                             out=ysl, in_=r_half[:, mt, :], func=AF.Identity,
                             bias=ss[8], scale=ss[7]).then_inc(S["sACT"], 1))
                    y_done = P.inc("sACT", 1)
                    P.wait("scalar", "sACT", y_done)
                    P.op("scalar", lambda E, S, mtg=mtg, ysl=ysl:
                         E["scalar"].dma_start(
                             out=y_d[mtg * 128:(mtg + 1) * 128, :], in_=ysl)
                         .then_inc(S["sYst"], 16))
                    ys = P.inc("sYst", 16)
                    yst_at[mt] = ys
                    yb = P.inc("sYbf", 16)

                    def ybf_store(E, S, mtg=mtg, ys=ys):
                        E["gpsimd"].wait_ge(S["sYst"], ys)
                        E["gpsimd"].dma_start(
                            out=ybf_d[mtg * 128:(mtg + 1) * 128, :],
                            in_=y_d[mtg * 128:(mtg + 1) * 128, :])                            .then_inc(S["sYbf"], 16)
                    pending_ybf.append(ybf_store)
                prev_half_yst = P.val("sYst")
                prev_half_ybf = P.val("sYbf")
            for fn in pending_ybf:
                P.ops["gpsimd"].append(fn)
            pending_ybf = []
            ybf_all_stored = P.val("sYbf")
            y_all_stored = P.val("sYst")
            p4_pe_done = P.val("sPE")
            p4_act_done = P.val("sACT")
            p4_dve_done = P.val("sDVE")

            wup_r = wupT.rearrange("(n p2) f -> p2 n f", p2=128)
            wgate_r = wgateT.rearrange("(n p2) f -> p2 n f", p2=128)
            wdn_r = wdownT.rearrange("(f p2) d -> p2 f d", p2=128)
            ug_idx = 0
            ug_free_at = {}
            h_done_at = {}
            dn_idx = 0
            dn_free_at = {}
            dn_evac_at = {}
            dn_st_at = {}
            dpst_total = 0
            r2_stored = 0
            for fpass in range(2):
                ft0 = fpass * FT_PASS
                nft_p = min(FT_PASS, NFT - ft0)
                if fpass == 0:
                    P.wait("sync", "sACT", p4_act_done)
                    P.wait("sync", "sDVE", p4_dve_done)
                    P.wait("sync", "sYbf", ybf_all_stored)
                    P.wait("sync", "sYst", y_all_stored)
                    P.wait("scalar", "sYbf", ybf_all_stored)
                    P.wait("scalar", "sYst", y_all_stored)
                else:
                    P.wait("sync", "sPE", dn_free_at[("pass", fpass - 1)])
                    P.wait("scalar", "sPE", dn_free_at[("pass", fpass - 1)])
                for kt in range(NKT):
                    P.op("sync", lambda E, S, kt=kt: E["sync"].dma_start_transpose(
                        out=yfm_sb[:, kt, :],
                        in_=ybf_d[:, kt * 128:(kt + 1) * 128])
                        .then_inc(S["sYfm"], 16))
                    P.inc("sYfm", 16)
                yfm_ld = P.val("sYfm")
                for fl in range(nft_p):
                    ft = ft0 + fl
                    li = ug_idx; ug_idx += 1
                    buf = li % 2
                    ug_sem = "sUG%d" % buf
                    if li >= 2:
                        P.wait("gpsimd", "sPE", ug_free_at[li - 2])
                    elif fpass == 0:
                        P.wait("gpsimd", "sPE", p4_pe_done)
                    if fpass == 1 and fl < 2:
                        P.wait("gpsimd", "sDPst", dpst_total)
                    P.op("gpsimd", lambda E, S, buf=buf, ft=ft, us=ug_sem:
                         E["gpsimd"].dma_start(
                             out=ug_slab[buf][:, 0:NKT * 128]
                             .rearrange("p (n f) -> p n f", f=128),
                             in_=wup_r[:, :, ft * 128:(ft + 1) * 128])
                         .then_inc(S[us], 16))
                    P.inc(ug_sem, 16)
                    P.op("gpsimd", lambda E, S, buf=buf, ft=ft, us=ug_sem:
                         E["gpsimd"].dma_start(
                             out=ug_slab[buf][:, NKT * 128:2 * NKT * 128]
                             .rearrange("p (n f) -> p n f", f=128),
                             in_=wgate_r[:, :, ft * 128:(ft + 1) * 128])
                         .then_inc(S[us], 16))
                    ug_ld = P.inc(ug_sem, 16)
                    po = (fl % 2) * 2048
                    P.wait("tensor", ug_sem, ug_ld)
                    if fl == 0:
                        P.wait("tensor", "sYfm", yfm_ld)
                    prev_h = h_done_at.get((fpass, fl - 2))
                    if prev_h:
                        P.wait("tensor", "sDVE", prev_h)
                    TCH = min(512, T)

                    def ugmm(E, S, buf=buf, po=po, TCH=TCH):
                        us = ug_slab[buf][:, 0:NKT * 128].rearrange(
                            "p (n f) -> p n f", f=128)
                        gs = ug_slab[buf][:, NKT * 128:2 * NKT * 128].rearrange(
                            "p (n f) -> p n f", f=128)
                        for hf, sl in ((0, us), (1, gs)):
                            for tc in range(T // TCH):
                                for kt in range(NKT):
                                    E["tensor"].matmul(
                                        PS[:, po + hf * 1024 + tc * TCH:
                                           po + hf * 1024 + (tc + 1) * TCH],
                                        sl[:, kt, :],
                                        yfm_sb[:, kt, tc * TCH:(tc + 1) * TCH],
                                        start=(kt == 0), stop=(kt == NKT - 1))\
                                        .then_inc(S["sPE"], 1)
                    P.op("tensor", ugmm)
                    ug_done = P.inc("sPE", 2 * (T // TCH) * NKT)
                    ug_free_at[li] = ug_done
                    P.wait("scalar", "sPE", ug_done)
                    prev_hh = h_done_at.get((fpass, fl - 1))
                    if prev_hh:
                        P.wait("scalar", "sDVE", prev_hh)
                    P.op("scalar", lambda E, S, po=po: E["scalar"].activation(
                        out=gt_stage[:, 0:T], in_=PS[:, po + 1024:po + 1024 + T],
                        func=AF.Silu, bias=c_zero).then_inc(S["sACT"], 1))
                    silu_done = P.inc("sACT", 1)
                    P.wait("vector", "sACT", silu_done)
                    if fpass == 1 and fl < 2:
                        P.wait("vector", "sPE", dn_free_at[("pass", 0)])
                    P.op("vector", lambda E, S, fl=fl, po=po:
                         E["vector"].tensor_tensor(
                             out=hview(fl), in0=gt_stage[:, 0:T],
                             in1=PS[:, po:po + T], op=ALU.mult))
                    h_done_at[(fpass, fl)] = P.val("sDVE")
                h_all = P.val("sDVE")
                for nt in range(NDN):
                    li = dn_idx; dn_idx += 1
                    buf = li % 2
                    dn_sem = "sDN%d" % buf
                    if li >= 2:
                        P.wait("gpsimd", "sPE", dn_free_at[li - 2])
                    P.wait("gpsimd", "sPE", ug_free_at[ug_idx - 1])
                    P.op("gpsimd", lambda E, S, buf=buf, nt=nt, ft0=ft0,
                         nft_p=nft_p, ds=dn_sem: E["gpsimd"].dma_start(
                             out=dn_slab[buf][:, 0:nft_p, :],
                             in_=wdn_r[:, ft0:ft0 + nft_p, nt * DN:(nt + 1) * DN])
                         .then_inc(S[ds], 16))
                    dn_ld = P.inc(dn_sem, 16)
                    for mt in range(NMT):
                        bk_i = mt % 8
                        P.wait("tensor", dn_sem, dn_ld)
                        if nt == 0 and mt == 0:
                            P.wait("tensor", "sDVE", h_all)
                        prev_ev = dn_evac_at.get((fpass, nt - 1, mt))
                        if prev_ev:
                            P.wait("tensor", "sACT" if fpass == 0 else "sDVE",
                                   prev_ev)

                        def dnmm(E, S, nt=nt, mt=mt, buf=buf, bk_i=bk_i,
                                 nft_p=nft_p):
                            for fl in range(nft_p):
                                E["tensor"].matmul(
                                    bank(bk_i)[:, 0:DN],
                                    hview(fl)[:, mt * 128:(mt + 1) * 128],
                                    dn_slab[buf][:, fl, :],
                                    start=(fl == 0), stop=(fl == nft_p - 1))\
                                    .then_inc(S["sPE"], 1)
                        P.op("tensor", dnmm)
                        mm_done = P.inc("sPE", nft_p)
                        if mt == NMT - 1:
                            dn_free_at[li] = mm_done
                        li2 = nt * NMT + mt
                        stg = dstg[li2 % 2]
                        if fpass == 0:
                            P.wait("scalar", "sPE", mm_done)
                            prev_st = dn_st_at.get((fpass, li2 - 2))
                            if prev_st:
                                P.wait("scalar", "sDPst", prev_st)
                            P.op("scalar", lambda E, S, stg=stg, bk_i=bk_i:
                                 E["scalar"].copy(
                                     out=stg, in_=bank(bk_i)[:, 0:DN])
                                 .then_inc(S["sACT"], 1))
                            ev = P.inc("sACT", 1)
                            dn_evac_at[(fpass, nt, mt)] = ev
                            P.wait("scalar", "sACT", ev)
                            P.op("scalar", lambda E, S, stg=stg, mt=mt, nt=nt:
                                 E["scalar"].dma_start(
                                     out=dpart_d[mt * 128:(mt + 1) * 128,
                                                 nt * DN:(nt + 1) * DN], in_=stg)
                                 .then_inc(S["sDPst"], 16))
                            dn_st_at[(fpass, li2)] = P.inc("sDPst", 16)
                        else:
                            dp = dpin[li2 % 2]
                            dp_sem = "sDP%d" % (li2 % 2)
                            prev_dp = dn_evac_at.get(("dp", li2 - 2))
                            if prev_dp:
                                P.wait("sync", "sDVE", prev_dp)
                            if li2 == 0:
                                P.wait("sync", "sDPst", dpst_total)
                                P.wait("sync", "sPE", ug_free_at[ug_idx - 1])
                            P.op("sync", lambda E, S, dp=dp, mt=mt, nt=nt,
                                 ds=dp_sem: E["sync"].dma_start(
                                     out=dp, in_=dpart_d[mt * 128:(mt + 1) * 128,
                                                         nt * DN:(nt + 1) * DN])
                                 .then_inc(S[ds], 16))
                            dp_ld = P.inc(dp_sem, 16)
                            P.wait("vector", "sPE", mm_done)
                            P.wait("vector", dp_sem, dp_ld)
                            prev_st = dn_st_at.get((fpass, li2 - 2))
                            if prev_st:
                                P.wait("vector", "sR2st", prev_st)
                            P.op("vector", lambda E, S, stg=stg, dp=dp, bk_i=bk_i:
                                 E["vector"].tensor_tensor(
                                     out=stg, in0=bank(bk_i)[:, 0:DN], in1=dp,
                                     op=ALU.add))
                            ev = P.val("sDVE")
                            dn_evac_at[(fpass, nt, mt)] = ev
                            dn_evac_at[("dp", li2)] = ev
                            P.wait("scalar", "sDVE", ev)
                            P.op("scalar", lambda E, S, stg=stg, mt=mt, nt=nt:
                                 E["scalar"].dma_start(
                                     out=r2_d[mt * 128:(mt + 1) * 128,
                                              nt * DN:(nt + 1) * DN], in_=stg)
                                 .then_inc(S["sR2st"], 16))
                            dn_st_at[(fpass, li2)] = P.inc("sR2st", 16)
                            r2_stored = dn_st_at[(fpass, li2)]
                dn_free_at[("pass", fpass)] = P.val("sPE")
                if fpass == 0:
                    dpst_total = P.val("sDPst")
            p5_pe_done = P.val("sPE")
            p5_dve_done = P.val("sDVE")

            out_at = {}
            ln2_act_at = {}
            for mt in range(NMT):
                ss = stat_slot()
                p6_sem = "sP6%d" % (mt % 2)
                if mt % 2 == 0:
                    r2s = Bf32[:, 0:D]; yss = Bf32[:, D:2 * D]
                else:
                    r2s = Af32[:, 0:D]; yss = Af32[:, D:2 * D]
                rsum = Bf32[:, 2 * D:3 * D]
                dmp = Bf32[:, 3 * D:4 * D]
                if mt == 0:
                    P.wait("sync", "sPE", p5_pe_done)
                    P.wait("sync", "sDVE", p5_dve_done)
                    P.wait("sync", "sR2st", r2_stored)
                    P.wait("sync", "sYst", y_all_stored)
                prev = out_at.get(mt - 2)
                if prev:
                    P.wait("sync", "sOUT", prev)
                P.op("sync", lambda E, S, r2s=r2s, mt=mt, ps=p6_sem:
                     E["sync"].dma_start(
                         out=r2s, in_=r2_d[mt * 128:(mt + 1) * 128, :])
                     .then_inc(S[ps], 16))
                P.inc(p6_sem, 16)
                P.op("sync", lambda E, S, yss=yss, mt=mt, ps=p6_sem:
                     E["sync"].dma_start(
                         out=yss, in_=y_d[mt * 128:(mt + 1) * 128, :])
                     .then_inc(S[ps], 16))
                ld = P.inc(p6_sem, 16)
                P.wait("vector", p6_sem, ld)
                pa = ln2_act_at.get(mt - 1)
                if pa:
                    P.wait("vector", "sACT", pa)
                P.op("vector", lambda E, S, r2s=r2s, yss=yss, rsum=rsum:
                     E["vector"].tensor_tensor(out=rsum, in0=r2s, in1=yss,
                                               op=ALU.add))
                P.op("vector", lambda E, S, rsum=rsum, ss=ss:
                     E["vector"].reduce_sum(ss[0], rsum, axis=AX.X, negate=True))
                P.op("vector", lambda E, S, ss=ss: E["vector"].tensor_scalar(
                    out=ss[1], in0=ss[0], scalar1=1.0 / D, scalar2=None,
                    op0=ALU.mult))
                P.op("vector", lambda E, S, ss=ss: E["vector"].tensor_tensor(
                    out=ss[2], in0=ss[1], in1=ss[1], op=ALU.mult))
                mu_done = P.val("sDVE")
                P.wait("scalar", "sDVE", mu_done)
                po6 = out_at.get(mt - 1)
                if po6:
                    P.wait("scalar", "sOUT", po6)
                P.op("scalar", lambda E, S, rsum=rsum, dmp=dmp, ss=ss:
                     E["scalar"].activation(out=dmp, in_=rsum, func=AF.Square,
                                            bias=c_zero, accum_out=ss[3])
                     .then_inc(S["sACT"], 1))
                sq_done = P.inc("sACT", 1)
                P.wait("vector", "sACT", sq_done)
                P.op("vector", lambda E, S, ss=ss: E["vector"].tensor_scalar(
                    out=ss[4], in0=ss[3], scalar1=1.0 / D, scalar2=ss[2],
                    op0=ALU.mult, op1=ALU.subtract))
                var_done = P.val("sDVE")
                P.wait("scalar", "sDVE", var_done)
                P.op("scalar", lambda E, S, ss=ss: E["scalar"].activation(
                    out=ss[5], in_=ss[4], func=AF.Sqrt, scale=1.0, bias=c_eps_ln)
                    .then_inc(S["sACT"], 1))
                sd_done = P.inc("sACT", 1)
                P.wait("vector", "sACT", sd_done)
                P.op("vector", lambda E, S, ss=ss: E["vector"].reciprocal(
                    ss[6], ss[5]))
                P.op("vector", lambda E, S, ss=ss: E["vector"].tensor_tensor(
                    out=ss[7], in0=ss[1], in1=ss[6], op=ALU.mult))
                rstd_done = P.val("sDVE")
                P.wait("scalar", "sDVE", rstd_done)
                P.op("scalar", lambda E, S, rsum=rsum, dmp=dmp, ss=ss:
                     E["scalar"].activation(out=dmp, in_=rsum, func=AF.Identity,
                                            bias=ss[7], scale=ss[6])
                     .then_inc(S["sACT"], 1))
                o_done = P.inc("sACT", 1)
                ln2_act_at[mt] = o_done
                P.wait("sync", "sACT", o_done)
                P.op("sync", lambda E, S, dmp=dmp, mt=mt: E["sync"].dma_start(
                    out=out_d[mt * 128:(mt + 1) * 128, :], in_=dmp)
                    .then_inc(S["sOUT"], 16))
                out_at[mt] = P.inc("sOUT", 16)

            if debug:
                P.wait("sync", "sACT", negm_ready)
                P.op("sync", lambda E, S: E["sync"].dma_start(
                    out=dbg["dbg_negm"], in_=negm[0:1, :]).then_inc(S["sOUT"], 16))
                P.inc("sOUT", 16)
                dump_list = [("dbg_attntm", attntm_d), ("dbg_y", y_d),
                             ("dbg_r2", r2_d)]
                if fused:
                    dump_list += [("dbg_qtm", qtm_sp), ("dbg_ktm_all", ktm_all),
                                  ("dbg_vtm_all", vtm_all)]
                for dname, srcd in dump_list:
                    P.op("sync", lambda E, S, d=dbg[dname], s=srcd:
                         E["sync"].dma_start(out=d, in_=s).then_inc(S["sOUT"], 16))
                    P.inc("sOUT", 16)
            P.wait("sync", "sOUT", P.val("sOUT"))

        with nc.Block() as blk:
            E = {}

            @blk.tensor
            def _(e):
                E["tensor"] = e
                for f in P.ops["tensor"]:
                    f(E, sems)

            @blk.scalar
            def _(e):
                E["scalar"] = e
                for f in P.ops["scalar"]:
                    f(E, sems)

            @blk.vector
            def _(e):
                E["vector"] = e
                for f in P.ops["vector"]:
                    f(E, sems)

            @blk.gpsimd
            def _(e):
                E["gpsimd"] = e
                for f in P.ops["gpsimd"]:
                    f(E, sems)

            @blk.sync
            def _(e):
                E["sync"] = e
                for f in P.ops["sync"]:
                    f(E, sems)

    return nc


def make_in_maps_A(cfg, inputs):
    D = cfg["D"]; S = cfg["S"]; T = S // NCORES
    x = np.asarray(inputs["L0_attn_hidden_states"]).reshape(S, D)
    common = dict(
        wq=np.asarray(inputs["L0_attn_Wq"]), wk=np.asarray(inputs["L0_attn_Wk"]),
        wv=np.asarray(inputs["L0_attn_Wv"]),
        bq=np.asarray(inputs["L0_attn_bq"]).reshape(1, D),
        bk=np.asarray(inputs["L0_attn_bk"]).reshape(1, D),
        bv=np.asarray(inputs["L0_attn_bv"]).reshape(1, D),
        gq=np.asarray(inputs["L0_attn_gq"]).reshape(1, D),
        gk=np.asarray(inputs["L0_attn_gk"]).reshape(1, D),
        ones_in=np.ones((1, 128), dtype=ml_dtypes.bfloat16),
        identf_in=np.eye(128, dtype=np.float32),
    )
    maps = []
    for c in range(NCORES):
        m = dict(common)
        m["x_fm"] = np.ascontiguousarray(x[c * T:(c + 1) * T, :].T)
        maps.append(m)
    return maps


def make_in_maps_B(cfg, inputs, resA):
    D = cfg["D"]; S = cfg["S"]; T = S // NCORES
    x = np.asarray(inputs["L0_attn_hidden_states"]).reshape(S, D)
    ktm_all = np.concatenate([np.asarray(r["ktm_sh"]) for r in resA], axis=0)
    vtm_all = np.concatenate([np.asarray(r["vtm_sh"]) for r in resA], axis=0)
    kn_all = np.concatenate([np.asarray(r["kn_sh"]) for r in resA], axis=0)
    common = dict(
        wo=np.asarray(inputs["L0_attn_Wo"]),
        wupT=np.ascontiguousarray(np.asarray(inputs["L0_ffn_W_up"]).T),
        wgateT=np.ascontiguousarray(np.asarray(inputs["L0_ffn_W_gate"]).T),
        wdownT=np.ascontiguousarray(np.asarray(inputs["L0_ffn_W_down"]).T),
        bo=np.asarray(inputs["L0_attn_bo"]).reshape(1, D),
        ones_in=np.ones((1, 128), dtype=ml_dtypes.bfloat16),
        onesf_in=np.ones((1, 128), dtype=np.float32),
        identf_in=np.eye(128, dtype=np.float32),
        ktm_all=ktm_all, vtm_all=vtm_all, kn_all=kn_all,
    )
    maps = []
    for c in range(NCORES):
        m = dict(common)
        m["x_tm"] = np.ascontiguousarray(x[c * T:(c + 1) * T, :])
        m["qtm_in"] = np.asarray(resA[c]["qtm_sp"])
        m["qn_in"] = np.asarray(resA[c]["qn_sh"])
        maps.append(m)
    return maps


_NC_CACHE = {}


def _get_programs(cfg):
    key = (id(cfg), cfg.get("debug", False))
    if key not in _NC_CACHE:
        _NC_CACHE[key] = (build_program(cfg, part="A"), build_program(cfg, part="B"))
    return _NC_CACHE[key]


def kernel(**inputs):
    cfg = FULL_CFG
    S = cfg["S"]; D = cfg["D"]
    ncA, ncB = _get_programs(cfg)
    resA = run_bass_kernel_spmd(ncA, make_in_maps_A(cfg, inputs),
                                core_ids=list(range(NCORES))).results
    resB = run_bass_kernel_spmd(ncB, make_in_maps_B(cfg, inputs, resA),
                                core_ids=list(range(NCORES))).results
    out = np.concatenate([np.asarray(r["out"]) for r in resB], axis=0)
    return out.reshape(1, S, D).astype(np.float32)


# version 07
# speedup vs baseline: 1.0116x; 1.0116x over previous
import math
import numpy as np
import ml_dtypes
import concourse.bass as bass
import concourse.mybir as mybir
from concourse.bass_utils import run_bass_kernel_spmd

F32 = mybir.dt.float32
BF16 = mybir.dt.bfloat16
AF = mybir.ActivationFunctionType
ALU = mybir.AluOpType
AX = mybir.AxisListType

NCORES = 8

FULL_CFG = dict(D=4096, S=8192, H=32, F=11008, QB=256, PN=256, ON=256,
                DN=256, G=6, MT_HALF=4, debug=False)

EPS_RMS = 1e-6
EPS_LN = 1e-5
INV_SQRT_DH = 1.0 / math.sqrt(128.0)
M_PAD = 0.25

SEM_NAMES = [
    "sX", "sGrow", "sConst", "sSL0", "sSL1", "sKst0", "sKst1", "sVst0",
    "sVst1", "sKn",
    "sQH0", "sQH1", "sKH0", "sKH1", "sVH0", "sVH1", "sAst", "sAh",
    "sXp0", "sXp1", "sYst", "sYbf", "sYfm", "sUG0", "sUG1", "sDN0", "sDN1",
    "sDPst0", "sDPst1", "sDP0", "sDP1", "sR2st0", "sR2st1", "sP60", "sP61",
    "sOUT",
    "sMq2", "sKnl", "sCC", "sPE", "sACT", "sDVE",
]


class Plan:

    def __init__(self):
        self.ops = {k: [] for k in ("tensor", "scalar", "vector", "gpsimd", "sync")}
        self.led = {}

    def inc(self, sem, amt):
        self.led[sem] = self.led.get(sem, 0) + amt
        return self.led[sem]

    def val(self, sem):
        return self.led.get(sem, 0)

    def op(self, eng, fn):
        if eng == "vector":
            prior = self.val("sDVE")
            if prior:
                self.ops[eng].append(
                    lambda E, S, v=prior: E["vector"].wait_ge(S["sDVE"], v))
            self.inc("sDVE", 1)
            self.ops[eng].append(
                lambda E, S, fn=fn: fn(E, S).then_inc(S["sDVE"], 1))
        else:
            self.ops[eng].append(fn)

    def wait(self, eng, sem, v):
        if not v:
            return
        self.ops[eng].append(lambda E, S, sem=sem, v=v: E[eng].wait_ge(S[sem], v))


def build_program(cfg, part="all"):
    D = cfg["D"]; S = cfg["S"]; H = cfg["H"]; F = cfg["F"]
    T = S // NCORES
    NKT = D // 128
    NMT = T // 128
    QB = min(cfg["QB"], T); NQB = T // QB; QTB = QB // 128
    PN = cfg["PN"]; NPN = D // PN
    ON = cfg["ON"]; NON = D // ON
    DN = cfg["DN"]; NDN = D // DN
    G = cfg["G"]
    SK = S // 128
    NFT = F // 128
    FT_PASS = (NFT + 1) // 2
    MT_HALF = min(cfg["MT_HALF"], NMT)
    N_HALVES = NMT // MT_HALF
    TH = T // N_HALVES
    NG = (SK + G - 1) // G
    debug = cfg.get("debug", False)
    do_A = part in ("all", "A")
    do_B = part in ("all", "B")
    fused = part == "all"

    nc = bass.Bass("TRN2", target_bir_lowering=False)

    bias_d = {}
    if do_A:
        x_fm = nc.dram_tensor("x_fm", [D, T], F32, kind="ExternalInput").ap()
        wq = nc.dram_tensor("wq", [D, D], F32, kind="ExternalInput").ap()
        wk = nc.dram_tensor("wk", [D, D], F32, kind="ExternalInput").ap()
        wv = nc.dram_tensor("wv", [D, D], F32, kind="ExternalInput").ap()
        for p in ("q", "k", "v"):
            bias_d[p] = nc.dram_tensor("b" + p, [1, D], F32, kind="ExternalInput").ap()
        gq_d = nc.dram_tensor("gq", [1, D], F32, kind="ExternalInput").ap()
        gk_d = nc.dram_tensor("gk", [1, D], F32, kind="ExternalInput").ap()
    if do_B:
        x_tm = nc.dram_tensor("x_tm", [T, D], F32, kind="ExternalInput").ap()
        wo = nc.dram_tensor("wo", [D, D], F32, kind="ExternalInput").ap()
        wupT = nc.dram_tensor("wupT", [D, F], F32, kind="ExternalInput").ap()
        wgateT = nc.dram_tensor("wgateT", [D, F], F32, kind="ExternalInput").ap()
        wdownT = nc.dram_tensor("wdownT", [F, D], F32, kind="ExternalInput").ap()
        bias_d["o"] = nc.dram_tensor("bo", [1, D], F32, kind="ExternalInput").ap()
        onesf_d = nc.dram_tensor("onesf_in", [1, 128], F32, kind="ExternalInput").ap()
        out_d = nc.dram_tensor("out", [T, D], F32, kind="ExternalOutput").ap()
    ones_d = nc.dram_tensor("ones_in", [1, 128], BF16, kind="ExternalInput").ap()
    identf_d = nc.dram_tensor("identf_in", [128, 128], F32, kind="ExternalInput").ap()

    spill_kind = "ExternalOutput" if part == "A" else "Internal"
    if do_A:
        ktm_sh = nc.dram_tensor("ktm_sh", [T, D], BF16, kind=spill_kind).ap()
        vtm_sh = nc.dram_tensor("vtm_sh", [T, D], BF16, kind=spill_kind).ap()
        qtm_sp = nc.dram_tensor("qtm_sp", [T, D], BF16, kind=spill_kind).ap()
        kn_sh = nc.dram_tensor("kn_sh", [H, 1], F32, kind=spill_kind).ap()
        qn_sh = nc.dram_tensor("qn_sh", [H, 1], F32, kind=spill_kind).ap()
    if do_B:
        if part == "B":
            qtm_sp = nc.dram_tensor("qtm_in", [T, D], BF16, kind="ExternalInput").ap()
            ktm_all = nc.dram_tensor("ktm_all", [S, D], BF16,
                                     kind="ExternalInput").ap()
            vtm_all = nc.dram_tensor("vtm_all", [S, D], BF16,
                                     kind="ExternalInput").ap()
            kn_all = nc.dram_tensor("kn_all", [NCORES * H, 1], F32,
                                    kind="ExternalInput").ap()
            qn_in = nc.dram_tensor("qn_in", [H, 1], F32, kind="ExternalInput").ap()
        else:
            ktm_all = nc.dram_tensor("ktm_all", [S, D], BF16).ap()
            vtm_all = nc.dram_tensor("vtm_all", [S, D], BF16).ap()
            kn_all = nc.dram_tensor("kn_all", [NCORES * H, 1], F32).ap()
        attntm_d = nc.dram_tensor("attntm_d", [T, D], BF16).ap()
        y_d = nc.dram_tensor("y_d", [T, D], F32).ap()
        ybf_d = nc.dram_tensor("ybf_d", [T, D], BF16).ap()
        dpart_d = nc.dram_tensor("dpart_d", [T, D], F32).ap()
        r2_d = nc.dram_tensor("r2_d", [T, D], F32).ap()
    dbg = {}
    if debug:
        items = []
        if do_B:
            items += [("dbg_attntm", [T, D], BF16), ("dbg_y", [T, D], F32),
                      ("dbg_r2", [T, D], F32), ("dbg_negm", [1, H], F32)]
        if fused:
            items += [("dbg_qtm", [T, D], BF16), ("dbg_ktm_all", [S, D], BF16),
                      ("dbg_vtm_all", [S, D], BF16)]
        for nm, shp, dt in items:
            dbg[nm] = nc.dram_tensor(nm, shp, dt, kind="ExternalOutput").ap()

    from contextlib import ExitStack
    with ExitStack() as stk:
        A = stk.enter_context(nc.sbuf_tensor("arenaA", [128, 33024], BF16))
        B_SZ = max(NMT * D, 8 * D, 2 * MT_HALF * D, NKT * T, 2 * FT_PASS * DN)
        B = stk.enter_context(nc.sbuf_tensor("arenaB", [128, B_SZ], BF16))
        C_BUF = 8448
        C = stk.enter_context(nc.sbuf_tensor("arenaC", [128, 2 * C_BUF], BF16))
        Dd = stk.enter_context(nc.sbuf_tensor("arenaD", [128, 12288], BF16))
        gq_b = stk.enter_context(nc.sbuf_tensor("gq_b", [128, D], BF16))
        gk_b = stk.enter_context(nc.sbuf_tensor("gk_b", [128, D], BF16))
        ones_sb = stk.enter_context(nc.sbuf_tensor("ones_sb", [1, 128], BF16))
        onesf_sb = stk.enter_context(nc.sbuf_tensor("onesf_sb", [1, 128], F32))
        bias_pc = stk.enter_context(nc.sbuf_tensor("bias_pc", [1, 2 * PN], BF16))
        identf = stk.enter_context(nc.sbuf_tensor("identf", [128, 128], F32))
        negm = stk.enter_context(nc.sbuf_tensor("negm", [128, H], F32))
        ssqp = stk.enter_context(nc.sbuf_tensor("ssqp", [128, NMT * NPN], F32))
        stats = stk.enter_context(nc.sbuf_tensor("stats", [128, 104], F32))
        nrm = stk.enter_context(nc.sbuf_tensor("nrm", [128, 5 * H + 3 + NCORES + 2 * H], F32))
        nrm2 = stk.enter_context(nc.sbuf_tensor("nrm2", [128, NMT * H], F32))
        nrow = stk.enter_context(nc.sbuf_tensor("nrow", [1, H], F32))
        PS = stk.enter_context(nc.psum_tensor("PS", [128, 4096], F32))

        sems = {s: stk.enter_context(nc.semaphore(s)) for s in SEM_NAMES}
        P = Plan()

        def bank(b):
            return PS[:, 512 * b:512 * (b + 1)]

        X_bf = A[:, 0:NKT * T].rearrange("p (n t) -> p n t", t=T)
        PT = A[:, 0:SK * QB].rearrange("p (k q) -> p k q", q=QB)
        VH_OFF = SK * QB
        VH = [A[:, VH_OFF + i * (SK * 129):VH_OFF + (i + 1) * (SK * 129)]
              .rearrange("p (k j) -> p k j", j=129) for i in range(2)]
        attn_half = A[:, 0:NKT * TH].rearrange("p (n t) -> p n t", t=TH)
        xpiece = [A[:, NKT * TH + 512 + i * 512:NKT * TH + 512 + (i + 1) * 512]
                  .bitcast(F32) for i in range(2)]
        h1 = A[:, 0:min(32, FT_PASS) * T].rearrange("p (n t) -> p n t", t=T)
        Af32 = A.ap().bitcast(F32)

        attn_tm = B[:, 0:NMT * D].rearrange("p (m d) -> p m d", d=D)
        r_half = B.ap().bitcast(F32)[:, 0:MT_HALF * D].rearrange(
            "p (m d) -> p m d", d=D)
        yfm_sb = B[:, 0:NKT * T].rearrange("p (n t) -> p n t", t=T)
        dn_slab = [B[:, i * (FT_PASS * DN):(i + 1) * (FT_PASS * DN)]
                   .rearrange("p (f d) -> p f d", d=DN) for i in range(2)]
        Bf32 = B.ap().bitcast(F32)

        slab = [C[:, i * C_BUF:i * C_BUF + NKT * PN]
                .rearrange("p (n m) -> p n m", m=PN) for i in range(2)]
        KH = [C[:, i * C_BUF:i * C_BUF + S] for i in range(2)]
        ug_slab = [C[:, i * C_BUF:i * C_BUF + 2 * NKT * 128] for i in range(2)]


        khat = [Dd[:, 0:D], Dd[:, D:2 * D]]
        dump = Dd[:, 2 * D:3 * D]
        dump2 = [dump[:, 0:PN], dump[:, PN:2 * PN]]
        QH = [Dd[:, i * T:(i + 1) * T] for i in range(2)]
        y_tm_f32 = Dd[:, 0:2 * D].bitcast(F32)
        p4dump = Dd[:, 2 * D:3 * D]
        h2 = (Dd[:, 0:(FT_PASS - 32) * T].rearrange("p (n t) -> p n t", t=T)
              if FT_PASS > 32 else None)
        gt_stage = Dd[:, 11264:12288]
        vstage = [C[:, 8192:8448], C[:, C_BUF + 8192:C_BUF + 8448]]
        dpin = [C[:, 0:1024].bitcast(F32)[:, i * 256:(i + 1) * 256] for i in range(2)]
        dstg = [C[:, C_BUF:C_BUF + 1024].bitcast(F32)[:, i * 256:(i + 1) * 256]
                for i in range(2)]

        def hview(fl):
            if fl < 32:
                return h1[:, fl, :]
            return h2[:, fl - 32, :]

        sct = [0]

        N_SLOTS = 8

        def stat_slot():
            sl = sct[0] % N_SLOTS; sct[0] += 1
            base = sl * 12
            return {i: stats[:, base + i:base + i + 1] for i in range(12)}

        s_rs = [stats[:, 96 + i:97 + i] for i in range(4)]
        c_eps_rms = stats[:, 100:101]
        c_eps_ln = stats[:, 101:102]
        c_zero = stats[:, 102:103]
        n2s = nrm[:, 0:H]
        kmaxAB = (nrm[:, H:2 * H], nrm[:, 2 * H:3 * H])
        qmaxAB = (nrm[:, 3 * H:4 * H], nrm[:, 4 * H:5 * H])
        mk2 = nrm[:, 5 * H:5 * H + 1]
        mq2 = nrm[:, 5 * H + 1:5 * H + 2]
        mcol = nrm[:, 5 * H + 2:5 * H + 3]
        knall_sb = nrm[:, 5 * H + 3:5 * H + 3 + NCORES]
        g2m = nrm[:, 5 * H + 3 + NCORES:6 * H + 3 + NCORES]
        g2max2 = nrm[:, 6 * H + 3 + NCORES:7 * H + 3 + NCORES]

        x_loaded = 0
        if do_A:
            P.op("gpsimd", lambda E, S: E["gpsimd"].dma_start(
                out=X_bf, in_=x_fm.rearrange("(n p) t -> p n t", p=128))
                .then_inc(S["sX"], 16))
            x_loaded = P.inc("sX", 16)
            P.op("gpsimd", lambda E, S: E["gpsimd"].dma_start(
                out=gq_b[0:1, :], in_=gq_d).then_inc(S["sGrow"], 16))
            P.inc("sGrow", 16)
            P.op("gpsimd", lambda E, S: E["gpsimd"].dma_start(
                out=gk_b[0:1, :], in_=gk_d).then_inc(S["sGrow"], 16))
            grow_ld = P.inc("sGrow", 16)
        const_list = [(ones_sb, ones_d), (identf, identf_d)]
        if do_B:
            const_list.append((onesf_sb, onesf_d))
        for dst, srcc in const_list:
            P.op("sync", lambda E, S, dst=dst, srcc=srcc: E["sync"].dma_start(
                out=dst[:, :], in_=srcc).then_inc(S["sConst"], 16))
            P.inc("sConst", 16)
        const_loaded = P.val("sConst")
        P.op("vector", lambda E, S: E["vector"].memset(c_eps_rms, EPS_RMS))
        P.op("vector", lambda E, S: E["vector"].memset(c_eps_ln, EPS_LN))
        P.op("vector", lambda E, S: E["vector"].memset(c_zero, 0.0))
        if do_A:
            P.op("vector", lambda E, S: E["vector"].memset(kmaxAB[1][:, :], -1e30))
            P.op("vector", lambda E, S: E["vector"].memset(qmaxAB[1][:, :], -1e30))
        memsets_done = P.val("sDVE")

        gains_ready = 0
        if do_A:
            P.wait("tensor", "sGrow", grow_ld)
            P.wait("tensor", "sConst", const_loaded)
            g_done = {}
            gi = 0
            for gb in (gq_b, gk_b):
                for ntb in range(D // 512):
                    off = 2048 + (gi % 4) * 512
                    if gi >= 4:
                        eng, cnt = g_done[gi - 4]
                        P.wait("tensor", eng, cnt)
                    P.op("tensor", lambda E, S, gb=gb, ntb=ntb, off=off:
                         E["tensor"].matmul(
                             PS[:, off:off + 512], ones_sb[:, :],
                             gb[0:1, ntb * 512:(ntb + 1) * 512],
                             start=True, stop=True).then_inc(S["sPE"], 1))
                    bc_mm = P.inc("sPE", 1)
                    if gb is gq_b:
                        P.wait("scalar", "sPE", bc_mm)
                        P.op("scalar", lambda E, S, ntb=ntb, off=off:
                             E["scalar"].copy(
                                 out=gq_b[:, ntb * 512:(ntb + 1) * 512],
                                 in_=PS[:, off:off + 512]).then_inc(S["sACT"], 1))
                        g_done[gi] = ("sACT", P.inc("sACT", 1))
                    else:
                        P.wait("vector", "sPE", bc_mm)
                        P.wait("vector", g_done[ntb][0], g_done[ntb][1])
                        P.op("vector", lambda E, S, ntb=ntb, off=off:
                             E["vector"].tensor_tensor(
                                 out=gk_b[:, ntb * 512:(ntb + 1) * 512],
                                 in0=PS[:, off:off + 512],
                                 in1=gq_b[:, ntb * 512:(ntb + 1) * 512],
                                 op=ALU.mult))
                        g_done[gi] = ("sDVE", P.val("sDVE"))
                    gi += 1
            gains_ready = P.val("sACT")
            P.op("vector", lambda E, S: E["vector"].reduce_max(
                g2m[:, :], gk_b.ap().rearrange("p (h j) -> p h j", j=128),
                axis=AX.X, apply_absolute_value=True))
            P.op("vector", lambda E, S: E["vector"].tensor_tensor(
                out=g2max2[:, :], in0=g2m[:, :], in1=g2m[:, :], op=ALU.mult))
            gains_dve = P.val("sDVE")

        w_of = {"k": wk, "v": wv, "q": wq} if do_A else {}
        g_of = {"k": gk_b, "q": gq_b}
        spill_of = {"k": ktm_sh, "q": qtm_sp} if do_A else {}
        nmax_of = {"k": kmaxAB, "q": qmaxAB}

        slab_done_at = {}
        evac_done_at = {}
        proj_mm_done = {"q": 0}
        norm_done = {}
        stt_at = {}
        khat_store_at = {}
        nmax_at = {}
        sqrtdone_at = {}
        kn_stored = 0
        qn_stored = 0
        vpiece_st = {}
        raw_done_at = {}
        raw_li2 = {}
        red_at = {}
        mk2_reduce_done = 0
        vst_total = (0, 0)
        mk2_copy_done = 0

        projs = ("q", "k", "v") if do_A else ()
        for pi, p in enumerate(projs):
            wsrc = w_of[p].rearrange("(n p2) m -> p2 n m", p2=128)
            for nt in range(NPN):
                li = pi * NPN + nt
                buf = li % 2
                sl_sem = "sSL%d" % buf
                if li >= 2:
                    P.wait("gpsimd", "sPE", slab_done_at[li - 2])
                P.op("gpsimd", lambda E, S, b=buf, wsrc=wsrc, nt=nt, sl=sl_sem:
                     E["gpsimd"].dma_start(
                         out=slab[b], in_=wsrc[:, :, nt * PN:(nt + 1) * PN])
                     .then_inc(S[sl], 16))
                P.inc(sl_sem, 16)
                P.op("gpsimd", lambda E, S, b=buf, p=p, nt=nt, sl=sl_sem:
                     E["gpsimd"].dma_start(
                         out=bias_pc[:, b * PN:(b + 1) * PN],
                         in_=bias_d[p][:, nt * PN:(nt + 1) * PN]).then_inc(S[sl], 16))
                slab_ld = P.inc(sl_sem, 16)
                for mt in range(NMT):
                    bk_i = mt % 8
                    P.wait("tensor", sl_sem, slab_ld)
                    if li < 2:
                        P.wait("tensor", "sConst", const_loaded)
                        P.wait("tensor", "sACT", gains_ready)
                        P.wait("tensor", "sX", x_loaded)
                    prev = evac_done_at.get((pi, nt - 1, mt))
                    if prev:
                        P.wait("tensor", "sACT", prev)
                    if pi > 0 and nt == 0:
                        P.wait("tensor", "sACT",
                               evac_done_at[(pi - 1, NPN - 1, mt)])
                    if pi == 2 and nt == 0 and mt == 2 and mk2_copy_done:
                        P.wait("tensor", "sACT", mk2_copy_done)

                    def mm_group(E, S, nt=nt, mt=mt, buf=buf, bk_i=bk_i):
                        for kt in range(NKT):
                            E["tensor"].matmul(
                                bank(bk_i)[:, 0:PN],
                                X_bf[:, kt, mt * 128:(mt + 1) * 128],
                                slab[buf][:, kt, :], start=(kt == 0),
                                stop=False).then_inc(S["sPE"], 1)
                        E["tensor"].matmul(
                            bank(bk_i)[:, 0:PN], ones_sb[:, :],
                            bias_pc[:, buf * PN:(buf + 1) * PN],
                            start=False, stop=True).then_inc(S["sPE"], 1)
                    P.op("tensor", mm_group)
                    mm_done = P.inc("sPE", NKT + 1)
                    if mt == NMT - 1:
                        slab_done_at[li] = mm_done
                    P.wait("scalar", "sPE", mm_done)
                    if li < 2 and mt == 0:
                        P.wait("scalar", "sDVE", memsets_done)
                    if p == "v":
                        li2 = nt * NMT + mt
                        vs = vstage[li2 % 2]
                        vs_sem = "sVst%d" % (li2 % 2)
                        pv_st = vpiece_st.get(li2 - 2)
                        if pv_st:
                            P.wait("scalar", vs_sem, pv_st)
                        P.op("scalar", lambda E, S, vs=vs, bk_i=bk_i:
                             E["scalar"].copy(
                                 out=vs, in_=bank(bk_i)[:, 0:PN])
                             .then_inc(S["sACT"], 1))
                        ev = P.inc("sACT", 1)
                        evac_done_at[(pi, nt, mt)] = ev
                        P.wait("scalar", "sACT", ev)
                        P.op("scalar", lambda E, S, vs=vs, mt=mt, nt=nt,
                             vs_sem=vs_sem: E["scalar"].dma_start(
                                 out=vtm_sh[mt * 128:(mt + 1) * 128,
                                            nt * PN:(nt + 1) * PN], in_=vs)
                             .then_inc(S[vs_sem], 16))
                        vpiece_st[li2] = P.inc(vs_sem, 16)
                    else:
                        li2 = nt * NMT + mt
                        if p == "k" and nt == 0:
                            P.wait("scalar", "sDVE", max(stt_at[("q", mt)],
                                                         red_at[("q", mt)]))
                        P.op("scalar", lambda E, S, mt=mt, nt=nt, bk_i=bk_i:
                             E["scalar"].copy(
                                 out=attn_tm[:, mt, nt * PN:(nt + 1) * PN],
                                 in_=bank(bk_i)[:, 0:PN]).then_inc(S["sACT"], 1))
                        gidx = pi * NPN * NMT + li2
                        pg = raw_li2.get(gidx - 2)
                        if pg:
                            P.wait("scalar", "sDVE", pg)
                        P.op("scalar", lambda E, S, mt=mt, nt=nt, bk_i=bk_i,
                             li2=li2: E["scalar"].activation(
                                 out=dump2[li2 % 2], in_=bank(bk_i)[:, 0:PN],
                                 func=AF.Square, bias=c_zero,
                                 accum_out=ssqp[:, mt * NPN + nt:mt * NPN + nt + 1])
                             .then_inc(S["sACT"], 1))
                        ev2 = P.inc("sACT", 2)
                        evac_done_at[(pi, nt, mt)] = ev2
                        P.wait("vector", "sACT", ev2)
                        HPP = PN // 128
                        P.op("vector", lambda E, S, mt=mt, nt=nt, li2=li2, HPP=HPP:
                             E["vector"].tensor_reduce(
                                 out=nrm2[:, mt * H + nt * HPP:
                                          mt * H + (nt + 1) * HPP],
                                 in_=dump2[li2 % 2].rearrange(
                                     "p (h j) -> p h j", j=128),
                                 axis=AX.X, op=ALU.add))
                        raw_done_at[(pi, nt, mt)] = P.val("sDVE")
                        raw_li2[pi * NPN * NMT + li2] = raw_done_at[(pi, nt, mt)]
            proj_mm_done[p] = P.val("sPE")

            if p == "v":
                vst_total = (P.val("sVst0"), P.val("sVst1"))
                norm_done[p] = P.val("sACT")
                continue

            gbc = g_of[p]
            maxA, maxB = nmax_of[p]
            slots = {}
            for mt in range(NMT):
                ev = evac_done_at[(pi, NPN - 1, mt)]
                ss = stat_slot()
                slots[mt] = ss
                P.wait("vector", "sACT", ev)
                if mt == 0:
                    if p == "k":
                        P.wait("vector", "sACT", gains_ready)
                        P.wait("vector", "sDVE", gains_dve)
                    P.wait("vector", "sDVE", memsets_done)
                prevsq = sqrtdone_at.get((p, mt - 8))
                if prevsq:
                    P.wait("vector", "sACT", prevsq)
                P.op("vector", lambda E, S, mt=mt, ss=ss: E["vector"].reduce_sum(
                    ss[0], ssqp[:, mt * NPN:(mt + 1) * NPN], axis=AX.X))
                red_at[(p, mt)] = P.val("sDVE")
            for mt in range(NMT):
                ss = slots[mt]
                P.wait("scalar", "sDVE", red_at[(p, mt)])
                P.op("scalar", lambda E, S, ss=ss: E["scalar"].activation(
                    out=ss[1], in_=ss[0], func=AF.Sqrt, scale=1.0 / D,
                    bias=c_eps_rms).then_inc(S["sACT"], 1))
                sq_done = P.inc("sACT", 1)
                sqrtdone_at[(p, mt)] = sq_done
                P.wait("vector", "sACT", sq_done)
                P.op("vector", lambda E, S, ss=ss: E["vector"].reciprocal(ss[2], ss[1]))
                if p == "q":
                    P.op("vector", lambda E, S, ss=ss: E["vector"].tensor_scalar(
                        out=ss[3], in0=ss[2], scalar1=INV_SQRT_DH, scalar2=None,
                        op0=ALU.mult))
                    rstd_i = 3
                else:
                    rstd_i = 2
                kh = khat[mt % 2]
                ks_sem = "sKst%d" % (mt % 2)
                prev_store = khat_store_at.get((p, mt - 2))
                if prev_store:
                    P.wait("vector", ks_sem, prev_store)
                if p == "k" and mt <= 1:
                    cands = [v for (pp, m2), v in khat_store_at.items()
                             if pp == "q" and m2 % 2 == mt % 2]
                    if cands:
                        P.wait("vector", ks_sem, max(cands))
                if p == "q":
                    P.op("vector", lambda E, S, mt=mt, ss=ss, r=rstd_i, kh=kh:
                         E["vector"].tensor_scalar(
                             out=kh[:, :], in0=attn_tm[:, mt, :], scalar1=ss[r],
                             scalar2=None, op0=ALU.mult))
                else:
                    P.op("vector", lambda E, S, mt=mt, ss=ss, r=rstd_i, gbc=gbc,
                         kh=kh: E["vector"].scalar_tensor_tensor(
                             out=kh[:, :], in0=attn_tm[:, mt, :], scalar=ss[r],
                             in1=gbc[:, :], op0=ALU.mult, op1=ALU.mult))
                stt_at[(p, mt)] = P.val("sDVE")
                P.wait("sync" if mt % 2 == 0 else "scalar", "sDVE",
                       stt_at[(p, mt)])
                P.op("sync" if mt % 2 == 0 else "scalar",
                     lambda E, S, mt=mt, kh=kh, p=p, ks=ks_sem,
                     q=("sync" if mt % 2 == 0 else "scalar"): E[q].dma_start(
                         out=spill_of[p][mt * 128:(mt + 1) * 128, :], in_=kh[:, :])
                     .then_inc(S[ks], 16))
                khat_store_at[(p, mt)] = P.inc(ks_sem, 16)
                P.op("vector", lambda E, S, ss=ss, r=rstd_i:
                     E["vector"].tensor_tensor(
                         out=ss[5], in0=ss[r], in1=ss[r], op=ALU.mult))
                if p == "q":
                    P.op("vector", lambda E, S, mt=mt, ss=ss:
                         E["vector"].tensor_scalar(
                             out=n2s[:, :], in0=nrm2[:, mt * H:(mt + 1) * H],
                             scalar1=ss[5], scalar2=None, op0=ALU.mult))
                else:
                    P.op("vector", lambda E, S, mt=mt, ss=ss:
                         E["vector"].scalar_tensor_tensor(
                             out=n2s[:, :], in0=nrm2[:, mt * H:(mt + 1) * H],
                             scalar=ss[5], in1=g2max2[:, :],
                             op0=ALU.mult, op1=ALU.mult))
                srcm, dstm = (maxB, maxA) if mt % 2 == 0 else (maxA, maxB)
                P.op("vector", lambda E, S, s=srcm, d=dstm: E["vector"].tensor_tensor(
                    out=d[:, :], in0=s[:, :], in1=n2s[:, :], op=ALU.max))
                nmax_at[(p, mt)] = P.val("sDVE")
            norm_done[p] = P.val("sACT")

        if do_A:
            fin = kmaxAB[0] if NMT % 2 == 1 else kmaxAB[1]
            P.wait("tensor", "sDVE", nmax_at[("k", NMT - 1)])
            P.wait("tensor", "sACT", evac_done_at[(2, NPN - 1, NMT - 1)])
            P.op("tensor", lambda E, S, fin=fin: E["tensor"].transpose(
                PS[0:H, 1024:1152], fin[:, :], identf[:, :]).then_inc(S["sPE"], 1))
            t_done = P.inc("sPE", 1)
            P.wait("scalar", "sPE", t_done)
            P.wait("scalar", "sDVE", nmax_at[("k", NMT - 1)])
            P.op("scalar", lambda E, S: E["scalar"].copy(
                out=dump.bitcast(F32)[0:H, 0:128], in_=PS[0:H, 1024:1152])
                .then_inc(S["sACT"], 1))
            c_done = P.inc("sACT", 1)
            mk2_copy_done = c_done
            P.wait("vector", "sACT", c_done)
            P.op("vector", lambda E, S: E["vector"].reduce_max(
                mk2[0:H, :], dump.bitcast(F32)[0:H, 0:128], axis=AX.X))
            mk_done = mk2_reduce_done = P.val("sDVE")
            P.wait("sync", "sDVE", mk_done)
            P.op("sync", lambda E, S: E["sync"].dma_start(
                out=kn_sh, in_=mk2[0:H, :]).then_inc(S["sKn"], 16))
            kn_stored = P.inc("sKn", 16)
        if do_A:
            finq = qmaxAB[0] if NMT % 2 == 1 else qmaxAB[1]
            P.wait("tensor", "sDVE", nmax_at[("q", NMT - 1)])
            P.op("tensor", lambda E, S: E["tensor"].transpose(
                PS[0:H, 1280:1408], finq[:, :], identf[:, :]).then_inc(S["sPE"], 1))
            tq_done = P.inc("sPE", 1)
            P.wait("scalar", "sPE", tq_done)
            if mk2_reduce_done:
                P.wait("scalar", "sDVE", mk2_reduce_done)
            P.op("scalar", lambda E, S: E["scalar"].copy(
                out=dump.bitcast(F32)[0:H, 0:128], in_=PS[0:H, 1280:1408])
                .then_inc(S["sACT"], 1))
            cq_done = P.inc("sACT", 1)
            P.wait("vector", "sACT", cq_done)
            P.op("vector", lambda E, S: E["vector"].reduce_max(
                mq2[0:H, :], dump.bitcast(F32)[0:H, 0:128], axis=AX.X))
            mqx_done = P.val("sDVE")
            P.wait("sync", "sDVE", mqx_done)
            P.wait("sync", "sKn", kn_stored)
            P.op("sync", lambda E, S: E["sync"].dma_start(
                out=qn_sh, in_=mq2[0:H, :]).then_inc(S["sKn"], 16))
            qn_stored = P.inc("sKn", 16)
        if part == "A":
            P.wait("sync", "sKst0", P.val("sKst0"))
            P.wait("sync", "sKst1", P.val("sKst1"))
            P.wait("sync", "sVst0", P.val("sVst0"))
            P.wait("sync", "sVst1", P.val("sVst1"))
            P.wait("sync", "sKn", P.val("sKn"))

        agn_done = agv_done = 0
        if fused:
            P.wait("gpsimd", "sKst0", P.val("sKst0"))
            P.wait("gpsimd", "sKst1", P.val("sKst1"))
            P.op("gpsimd", lambda E, S: E["gpsimd"].collective_compute(
                "AllGather", ALU.bypass, replica_groups=[list(range(NCORES))],
                ins=[ktm_sh.opt()], outs=[ktm_all.opt()]).then_inc(S["sCC"], 16))
            P.inc("sCC", 16)
            P.wait("gpsimd", "sKn", qn_stored)
            P.op("gpsimd", lambda E, S: E["gpsimd"].collective_compute(
                "AllGather", ALU.bypass, replica_groups=[list(range(NCORES))],
                ins=[kn_sh.opt()], outs=[kn_all.opt()]).then_inc(S["sCC"], 16))
            agn_done = P.inc("sCC", 16)
            P.wait("gpsimd", "sVst0", vst_total[0])
            P.wait("gpsimd", "sVst1", vst_total[1])
            P.op("gpsimd", lambda E, S: E["gpsimd"].collective_compute(
                "AllGather", ALU.bypass, replica_groups=[list(range(NCORES))],
                ins=[vtm_sh.opt()], outs=[vtm_all.opt()]).then_inc(S["sCC"], 16))
            agv_done = P.inc("sCC", 16)

        if do_B:
            if part == "B":
                P.op("sync", lambda E, S: E["sync"].dma_start(
                    out=mq2[0:H, :], in_=qn_in).then_inc(S["sMq2"], 16))
                mq2_ld = P.inc("sMq2", 16)
                P.wait("vector", "sMq2", mq2_ld)
            if fused:
                P.wait("sync", "sCC", agn_done)
            P.op("sync", lambda E, S: E["sync"].dma_start(
                out=knall_sb[0:H, :],
                in_=kn_all.rearrange("(r h) o -> h (r o)", r=NCORES),
                allow_slow_non_contiguous=True).then_inc(S["sKnl"], 16))
            knall_ld = P.inc("sKnl", 16)
            P.wait("vector", "sKnl", knall_ld)
            P.wait("vector", "sDVE", memsets_done)
            P.op("vector", lambda E, S: E["vector"].reduce_max(
                mk2[0:H, :], knall_sb[0:H, :], axis=AX.X))
            P.op("vector", lambda E, S: E["vector"].tensor_tensor(
                out=mcol[0:H, :], in0=mq2[0:H, :], in1=mk2[0:H, :], op=ALU.mult))
            m2_done = P.val("sDVE")
            P.wait("scalar", "sDVE", m2_done)
            P.op("scalar", lambda E, S: E["scalar"].activation(
                out=mcol[0:H, :], in_=mcol[0:H, :], func=AF.Sqrt,
                bias=c_zero[0:H, :]).then_inc(S["sACT"], 1))
            msq_done = P.inc("sACT", 1)
            P.wait("vector", "sACT", msq_done)
            P.op("vector", lambda E, S: E["vector"].tensor_scalar(
                out=mcol[0:H, :], in0=mcol[0:H, :], scalar1=M_PAD, scalar2=-1.0,
                op0=ALU.add, op1=ALU.mult))
            mneg_done = P.val("sDVE")
            P.wait("tensor", "sDVE", mneg_done)
            if part == "B":
                P.wait("tensor", "sConst", const_loaded)
            P.op("tensor", lambda E, S: E["tensor"].transpose(
                PS[0:1, 1024:1024 + H], mcol[0:H, :], identf[0:H, 0:H])
                .then_inc(S["sPE"], 1))
            tm_done = P.inc("sPE", 1)
            P.wait("scalar", "sPE", tm_done)
            P.op("scalar", lambda E, S: E["scalar"].copy(
                out=nrow[:, :], in_=PS[0:1, 1024:1024 + H]).then_inc(S["sACT"], 1))
            nrow_done = P.inc("sACT", 1)
            P.wait("tensor", "sACT", nrow_done)
            P.op("tensor", lambda E, S: E["tensor"].matmul(
                PS[:, 1536:1536 + H], onesf_sb[:, :], nrow[:, :],
                start=True, stop=True).then_inc(S["sPE"], 1))
            nbc_mm = P.inc("sPE", 1)
            P.wait("scalar", "sPE", nbc_mm)
            P.op("scalar", lambda E, S: E["scalar"].copy(
                out=negm[:, :], in_=PS[:, 1536:1536 + H]).then_inc(S["sACT"], 1))
            negm_ready = P.inc("sACT", 1)

            P.wait("vector", "sPE", proj_mm_done["q"])
            for i in range(2):
                P.op("vector", lambda E, S, i=i: E["vector"].memset(
                    VH[i][:, :, 128:129], 1.0))
            vones_ready = P.val("sDVE")

            kv_ld_at = {}
            head_pe_done = {}
            vtm_all_r = vtm_all.rearrange("(k p) (n j) -> n p k j", p=128, j=128)

            def emit_head_loads(h):
                buf = h % 2
                qs, ks, vs = "sQH%d" % buf, "sKH%d" % buf, "sVH%d" % buf
                if h >= 2:
                    P.wait("sync", "sPE", head_pe_done[h - 2])
                else:
                    P.wait("sync", "sPE", proj_mm_done["q"])
                    P.wait("sync", "sDVE", vones_ready)
                    if fused:
                        P.wait("sync", "sCC", agv_done)
                        P.wait("sync", "sKst0", P.val("sKst0"))
                        P.wait("sync", "sKst1", P.val("sKst1"))
                P.op("sync", lambda E, S, h=h, buf=buf, qs=qs:
                     E["sync"].dma_start_transpose(
                         out=QH[buf][:, :], in_=qtm_sp[:, h * 128:(h + 1) * 128])
                     .then_inc(S[qs], 16))
                q_ld = P.inc(qs, 16)
                P.op("sync", lambda E, S, h=h, buf=buf, ks=ks:
                     E["sync"].dma_start_transpose(
                         out=KH[buf][:, :], in_=ktm_all[:, h * 128:(h + 1) * 128])
                     .then_inc(S[ks], 16))
                k_ld = P.inc(ks, 16)
                P.op("sync", lambda E, S, h=h, buf=buf, vs=vs: E["sync"].dma_start(
                    out=VH[buf][:, :, 0:128], in_=vtm_all_r[h]).then_inc(S[vs], 16))
                v_ld = P.inc(vs, 16)
                kv_ld_at[h] = (qs, q_ld, ks, k_ld, vs, v_ld)

            PV0 = 2 * G * QB
            assert PV0 + QTB * 512 <= 4096
            exp_done = {}
            pv_done = {}
            attn_written = {}

            emit_head_loads(0)
            if H > 1:
                emit_head_loads(1)
            for h in range(H):
                buf = h % 2
                for qb in range(NQB):
                    pv_g = None
                    for g in range(NG):
                        kts = list(range(g * G, min((g + 1) * G, SK)))
                        sco = (g % 2) * (G * QB)
                        if g == 0 and qb == 0:
                            qs, qv, ks, kv2, vs, vv = kv_ld_at[h]
                            P.wait("tensor", qs, qv)
                            P.wait("tensor", ks, kv2)
                            P.wait("tensor", vs, vv)
                        if h == 0 and qb == 0 and g <= 1:
                            P.wait("tensor", "sACT", negm_ready)
                        prev_exp = exp_done.get((h, qb, g - 2))
                        if not prev_exp and g - 2 + NG >= 0:
                            if qb > 0:
                                prev_exp = exp_done.get((h, qb - 1, g - 2 + NG))
                            elif h > 0:
                                prev_exp = exp_done.get((h - 1, NQB - 1, g - 2 + NG))
                        if prev_exp:
                            P.wait("tensor", "sACT", prev_exp)

                        def smm(E, S, kts=kts, sco=sco, qb=qb, buf=buf):
                            for i, kt in enumerate(kts):
                                E["tensor"].matmul(
                                    PS[:, sco + i * QB:sco + (i + 1) * QB],
                                    KH[buf][:, kt * 128:(kt + 1) * 128],
                                    QH[buf][:, qb * QB:(qb + 1) * QB],
                                    start=True, stop=True).then_inc(S["sPE"], 1)
                        P.op("tensor", smm)
                        sc_done = P.inc("sPE", len(kts))
                        P.wait("scalar", "sPE", sc_done)
                        prev_pv = pv_done.get((h, qb - 1)) if qb > 0 else (
                            pv_done.get((h - 1, NQB - 1)) if h > 0 else None)
                        if prev_pv and g == 0:
                            P.wait("scalar", "sPE", prev_pv)
                        if h == 0 and qb == 0 and g == 0:
                            P.wait("scalar", "sACT", negm_ready)

                        def expg(E, S, kts=kts, sco=sco, h=h):
                            E["scalar"].activation(
                                out=PT[:, kts[0]:kts[0] + len(kts), :],
                                in_=PS[:, sco:sco + len(kts) * QB],
                                func=AF.Exp, bias=negm[:, h:h + 1], scale=1.0)\
                                .then_inc(S["sACT"], 1)
                        P.op("scalar", expg)
                        exp_done[(h, qb, g)] = P.inc("sACT", 1)
                        for gg in ([g - 1] if g >= 1 else []) + \
                                ([g] if g == NG - 1 else []):
                            kts2 = list(range(gg * G, min((gg + 1) * G, SK)))
                            P.wait("tensor", "sACT", exp_done[(h, qb, gg)])
                            if gg == 0:
                                prev_norm = attn_written.get((h, qb - 1)) if qb > 0 \
                                    else (attn_written.get((h - 1, NQB - 1))
                                          if h > 0 else None)
                                if prev_norm:
                                    P.wait("tensor", "sDVE", prev_norm)

                            def pvg(E, S, kts2=kts2, buf=buf):
                                for kt in kts2:
                                    for qt in range(QTB):
                                        E["tensor"].matmul(
                                            PS[:, PV0 + qt * 512:
                                               PV0 + qt * 512 + 129],
                                            PT[:, kt, qt * 128:(qt + 1) * 128],
                                            VH[buf][:, kt, :],
                                            start=(kt == 0), stop=(kt == SK - 1),
                                            skip_group_check=True)\
                                            .then_inc(S["sPE"], 1)
                            P.op("tensor", pvg)
                            pv_g = P.inc("sPE", len(kts2) * QTB)
                    pv_done[(h, qb)] = pv_g
                    P.wait("vector", "sPE", pv_done[(h, qb)])
                    for qt in range(QTB):
                        tt = qb * QTB + qt
                        P.op("vector", lambda E, S, qt=qt: E["vector"].reciprocal(
                            s_rs[qt],
                            PS[:, PV0 + qt * 512 + 128:PV0 + qt * 512 + 129]))
                        P.op("vector", lambda E, S, qt=qt, tt=tt, h=h:
                             E["vector"].tensor_scalar(
                                 out=attn_tm[:, tt, h * 128:(h + 1) * 128],
                                 in0=PS[:, PV0 + qt * 512:PV0 + qt * 512 + 128],
                                 scalar1=s_rs[qt], scalar2=None, op0=ALU.mult))
                    attn_written[(h, qb)] = P.val("sDVE")
                head_pe_done[h] = pv_done[(h, NQB - 1)]
                if h + 2 < H:
                    emit_head_loads(h + 2)
            for qb in range(NQB):
                for qt in range(QTB):
                    tt = qb * QTB + qt
                    P.wait("sync", "sDVE", attn_written[(H - 1, qb)])
                    P.op("sync", lambda E, S, tt=tt: E["sync"].dma_start(
                        out=attntm_d[tt * 128:(tt + 1) * 128, :],
                        in_=attn_tm[:, tt, :]).then_inc(S["sAst"], 16))
                    P.inc("sAst", 16)
            attntm_stored = P.val("sAst")

            wo_r = wo.rearrange("(n p2) m -> p2 n m", p2=128)
            o_slab_idx = 0
            o_slab_free = {}
            o_evac_at = {}
            ybf_store_at = {}
            first_dve_p4 = True
            prev_p4sq = 0
            prev_half_yst = prev_half_ybf = 0
            pending_ybf = []
            for half in range(N_HALVES):
                P.wait("sync", "sAst", attntm_stored)
                P.wait("scalar", "sAst", attntm_stored)
                if half > 0:
                    P.wait("sync", "sPE", o_slab_free[o_slab_idx - 1])
                    P.wait("scalar", "sPE", o_slab_free[o_slab_idx - 1])
                for kt in range(NKT):
                    P.op("sync", lambda E, S, kt=kt, half=half:
                         E["sync"].dma_start_transpose(
                             out=attn_half[:, kt, :],
                             in_=attntm_d[half * TH:(half + 1) * TH,
                                          kt * 128:(kt + 1) * 128])
                         .then_inc(S["sAh"], 16))
                    P.inc("sAh", 16)
                ah_ld = P.val("sAh")
                for nt in range(NON):
                    li = o_slab_idx; o_slab_idx += 1
                    buf = li % 2
                    sl_sem = "sSL%d" % buf
                    if li >= 2:
                        P.wait("gpsimd", "sPE", o_slab_free[li - 2])
                    else:
                        P.wait("gpsimd", "sPE", head_pe_done[H - 1])
                    P.op("gpsimd", lambda E, S, buf=buf, nt=nt, sl=sl_sem:
                         E["gpsimd"].dma_start(
                             out=slab[buf][:, :, 0:ON],
                             in_=wo_r[:, :, nt * ON:(nt + 1) * ON])
                         .then_inc(S[sl], 16))
                    P.inc(sl_sem, 16)
                    P.op("gpsimd", lambda E, S, buf=buf, nt=nt, sl=sl_sem:
                         E["gpsimd"].dma_start(
                             out=bias_pc[:, buf * PN:buf * PN + ON],
                             in_=bias_d["o"][:, nt * ON:(nt + 1) * ON])
                         .then_inc(S[sl], 16))
                    sl_ld = P.inc(sl_sem, 16)
                    if nt == 1 and pending_ybf:
                        for fn in pending_ybf:
                            P.ops["gpsimd"].append(fn)
                        pending_ybf = []
                    for mt in range(MT_HALF):
                        mtg = half * MT_HALF + mt
                        bk_i = mt % 4 + 4 * (nt % 2)
                        li2 = nt * MT_HALF + mt
                        xp = xpiece[li2 % 2]
                        xp_sem = "sXp%d" % (li2 % 2)
                        gx = half * (NON * MT_HALF) + li2
                        prev_x = o_evac_at.get(("x", gx - 2))
                        if prev_x:
                            P.wait("sync", "sDVE", prev_x)
                        P.op("sync", lambda E, S, xp=xp, mtg=mtg, nt=nt, xs=xp_sem:
                             E["sync"].dma_start(
                                 out=xp, in_=x_tm[mtg * 128:(mtg + 1) * 128,
                                                  nt * ON:(nt + 1) * ON])
                             .then_inc(S[xs], 16))
                        xp_ld = P.inc(xp_sem, 16)
                        P.wait("tensor", sl_sem, sl_ld)
                        if li < 2:
                            P.wait("tensor", "sAh", ah_ld)
                            P.wait("tensor", "sConst", const_loaded)
                        if half > 0 and nt == 0:
                            P.wait("tensor", "sAh", ah_ld)
                        gn = half * NON + nt
                        prev_ev = o_evac_at.get(("g", gn - 2, mt))
                        if prev_ev:
                            P.wait("tensor", "sDVE", prev_ev)

                        def omm(E, S, nt=nt, mt=mt, buf=buf, bk_i=bk_i):
                            for kt in range(NKT):
                                E["tensor"].matmul(
                                    bank(bk_i)[:, 0:ON],
                                    attn_half[:, kt, mt * 128:(mt + 1) * 128],
                                    slab[buf][:, kt, 0:ON],
                                    start=(kt == 0), stop=False)\
                                    .then_inc(S["sPE"], 1)
                            E["tensor"].matmul(
                                bank(bk_i)[:, 0:ON], ones_sb[:, :],
                                bias_pc[:, buf * PN:buf * PN + ON],
                                start=False, stop=True).then_inc(S["sPE"], 1)
                        P.op("tensor", omm)
                        mm_done = P.inc("sPE", NKT + 1)
                        if mt == MT_HALF - 1:
                            o_slab_free[li] = mm_done
                        P.wait("vector", "sPE", mm_done)
                        P.wait("vector", xp_sem, xp_ld)
                        if first_dve_p4:
                            P.wait("vector", "sAst", attntm_stored)
                            first_dve_p4 = False
                        if half > 0 and nt == 0 and mt == 0:
                            P.wait("vector", "sYst", prev_half_yst)
                            P.wait("vector", "sYbf", prev_half_ybf)
                        P.op("vector", lambda E, S, mt=mt, nt=nt, bk_i=bk_i, xp=xp:
                             E["vector"].tensor_tensor(
                                 out=r_half[:, mt, nt * ON:(nt + 1) * ON],
                                 in0=bank(bk_i)[:, 0:ON], in1=xp, op=ALU.add))
                        ev = P.val("sDVE")
                        o_evac_at[(half, nt, mt)] = ev
                        o_evac_at[("x", gx)] = ev
                        o_evac_at[("g", gn, mt)] = ev
                while sct[0] % 4:
                    sct[0] += 1
                base0 = (sct[0] % N_SLOTS) * 12
                sss = [stat_slot() for _ in range(4)]

                def scol(j, base0=base0):
                    v = stats[:, base0:base0 + 48].rearrange(
                        "p (s c) -> p s c", c=12)
                    return v[:, 0:MT_HALF, j]

                sq_last = 0
                for mt in range(MT_HALF):
                    ss = sss[mt % 4]
                    last_ev = o_evac_at[(half, NON - 1, mt)]
                    P.wait("vector", "sDVE", last_ev)
                    P.op("vector", lambda E, S, mt=mt, ss=ss: E["vector"].reduce_sum(
                        ss[0], r_half[:, mt, :], axis=AX.X, negate=True))
                    P.wait("scalar", "sDVE", last_ev)
                    if prev_p4sq:
                        P.wait("scalar", "sACT", prev_p4sq)
                    P.op("scalar", lambda E, S, mt=mt, ss=ss: E["scalar"].activation(
                        out=p4dump[:, :], in_=r_half[:, mt, :], func=AF.Square,
                        bias=c_zero, accum_out=ss[3]).then_inc(S["sACT"], 1))
                    sq_last = P.inc("sACT", 1)
                    prev_p4sq = sq_last
                P.wait("vector", "sACT", sq_last)
                P.op("vector", lambda E, S, scol=scol: E["vector"].tensor_scalar(
                    out=scol(1), in0=scol(0), scalar1=1.0 / D, scalar2=None,
                    op0=ALU.mult))
                P.op("vector", lambda E, S, scol=scol: E["vector"].tensor_tensor(
                    out=scol(2), in0=scol(1), in1=scol(1), op=ALU.mult))
                P.op("vector", lambda E, S, scol=scol: E["vector"].tensor_scalar(
                    out=scol(4), in0=scol(3), scalar1=1.0 / D, scalar2=None,
                    op0=ALU.mult))
                P.op("vector", lambda E, S, scol=scol: E["vector"].tensor_tensor(
                    out=scol(5), in0=scol(4), in1=scol(2), op=ALU.subtract))
                var_done = P.val("sDVE")
                P.wait("scalar", "sDVE", var_done)
                P.op("scalar", lambda E, S, scol=scol: E["scalar"].activation(
                    out=scol(6), in_=scol(5), func=AF.Sqrt, scale=1.0,
                    bias=c_eps_ln).then_inc(S["sACT"], 1))
                sd_done = P.inc("sACT", 1)
                P.wait("vector", "sACT", sd_done)
                P.op("vector", lambda E, S, scol=scol: E["vector"].reciprocal(
                    scol(7), scol(6)))
                P.op("vector", lambda E, S, scol=scol: E["vector"].tensor_tensor(
                    out=scol(8), in0=scol(1), in1=scol(7), op=ALU.mult))
                rstd_done = P.val("sDVE")
                P.wait("scalar", "sDVE", rstd_done)
                yst_at = {}
                for mt in range(MT_HALF):
                    mtg = half * MT_HALF + mt
                    ss = sss[mt % 4]
                    ysl = y_tm_f32[:, 0:D]
                    prev = yst_at.get(mt - 1)
                    if prev:
                        P.wait("scalar", "sYst", prev)
                    P.op("scalar", lambda E, S, mt=mt, ss=ss, ysl=ysl:
                         E["scalar"].activation(
                             out=ysl, in_=r_half[:, mt, :], func=AF.Identity,
                             bias=ss[8], scale=ss[7]).then_inc(S["sACT"], 1))
                    y_done = P.inc("sACT", 1)
                    P.wait("scalar", "sACT", y_done)
                    P.op("scalar", lambda E, S, mtg=mtg, ysl=ysl:
                         E["scalar"].dma_start(
                             out=y_d[mtg * 128:(mtg + 1) * 128, :], in_=ysl)
                         .then_inc(S["sYst"], 16))
                    ys = P.inc("sYst", 16)
                    yst_at[mt] = ys
                    yb = P.inc("sYbf", 16)

                    def ybf_store(E, S, mtg=mtg, ys=ys):
                        E["gpsimd"].wait_ge(S["sYst"], ys)
                        E["gpsimd"].dma_start(
                            out=ybf_d[mtg * 128:(mtg + 1) * 128, :],
                            in_=y_d[mtg * 128:(mtg + 1) * 128, :])                            .then_inc(S["sYbf"], 16)
                    pending_ybf.append(ybf_store)
                prev_half_yst = P.val("sYst")
                prev_half_ybf = P.val("sYbf")
            for fn in pending_ybf:
                P.ops["gpsimd"].append(fn)
            pending_ybf = []
            ybf_all_stored = P.val("sYbf")
            y_all_stored = P.val("sYst")
            p4_pe_done = P.val("sPE")
            p4_act_done = P.val("sACT")
            p4_dve_done = P.val("sDVE")

            wup_r = wupT.rearrange("(n p2) f -> p2 n f", p2=128)
            wgate_r = wgateT.rearrange("(n p2) f -> p2 n f", p2=128)
            wdn_r = wdownT.rearrange("(f p2) d -> p2 f d", p2=128)
            ug_idx = 0
            ug_free_at = {}
            h_done_at = {}
            dn_idx = 0
            dn_free_at = {}
            dn_evac_at = {}
            dn_st_at = {}
            dpst_total = (0, 0)
            r2_stored = (0, 0)
            for fpass in range(2):
                ft0 = fpass * FT_PASS
                nft_p = min(FT_PASS, NFT - ft0)
                if fpass == 0:
                    P.wait("sync", "sACT", p4_act_done)
                    P.wait("sync", "sDVE", p4_dve_done)
                    P.wait("sync", "sYbf", ybf_all_stored)
                    P.wait("sync", "sYst", y_all_stored)
                    P.wait("scalar", "sYbf", ybf_all_stored)
                    P.wait("scalar", "sYst", y_all_stored)
                else:
                    P.wait("sync", "sPE", dn_free_at[("pass", fpass - 1)])
                    P.wait("scalar", "sPE", dn_free_at[("pass", fpass - 1)])
                for kt in range(NKT):
                    P.op("sync", lambda E, S, kt=kt: E["sync"].dma_start_transpose(
                        out=yfm_sb[:, kt, :],
                        in_=ybf_d[:, kt * 128:(kt + 1) * 128])
                        .then_inc(S["sYfm"], 16))
                    P.inc("sYfm", 16)
                yfm_ld = P.val("sYfm")
                for fl in range(nft_p):
                    ft = ft0 + fl
                    li = ug_idx; ug_idx += 1
                    buf = li % 2
                    ug_sem = "sUG%d" % buf
                    if li >= 2:
                        P.wait("gpsimd", "sPE", ug_free_at[li - 2])
                    elif fpass == 0:
                        P.wait("gpsimd", "sPE", p4_pe_done)
                    if fpass == 1 and fl < 2:
                        P.wait("gpsimd", "sDPst0", dpst_total[0])
                        P.wait("gpsimd", "sDPst1", dpst_total[1])
                    P.op("gpsimd", lambda E, S, buf=buf, ft=ft, us=ug_sem:
                         E["gpsimd"].dma_start(
                             out=ug_slab[buf][:, 0:NKT * 128]
                             .rearrange("p (n f) -> p n f", f=128),
                             in_=wup_r[:, :, ft * 128:(ft + 1) * 128])
                         .then_inc(S[us], 16))
                    P.inc(ug_sem, 16)
                    P.op("gpsimd", lambda E, S, buf=buf, ft=ft, us=ug_sem:
                         E["gpsimd"].dma_start(
                             out=ug_slab[buf][:, NKT * 128:2 * NKT * 128]
                             .rearrange("p (n f) -> p n f", f=128),
                             in_=wgate_r[:, :, ft * 128:(ft + 1) * 128])
                         .then_inc(S[us], 16))
                    ug_ld = P.inc(ug_sem, 16)
                    po = (fl % 2) * 2048
                    P.wait("tensor", ug_sem, ug_ld)
                    if fl == 0:
                        P.wait("tensor", "sYfm", yfm_ld)
                    prev_h = h_done_at.get((fpass, fl - 2))
                    if prev_h:
                        P.wait("tensor", "sDVE", prev_h)
                    TCH = min(512, T)

                    def ugmm(E, S, buf=buf, po=po, TCH=TCH):
                        us = ug_slab[buf][:, 0:NKT * 128].rearrange(
                            "p (n f) -> p n f", f=128)
                        gs = ug_slab[buf][:, NKT * 128:2 * NKT * 128].rearrange(
                            "p (n f) -> p n f", f=128)
                        for hf, sl in ((0, us), (1, gs)):
                            for tc in range(T // TCH):
                                for kt in range(NKT):
                                    E["tensor"].matmul(
                                        PS[:, po + hf * 1024 + tc * TCH:
                                           po + hf * 1024 + (tc + 1) * TCH],
                                        sl[:, kt, :],
                                        yfm_sb[:, kt, tc * TCH:(tc + 1) * TCH],
                                        start=(kt == 0), stop=(kt == NKT - 1))\
                                        .then_inc(S["sPE"], 1)
                    P.op("tensor", ugmm)
                    ug_done = P.inc("sPE", 2 * (T // TCH) * NKT)
                    ug_free_at[li] = ug_done
                    P.wait("scalar", "sPE", ug_done)
                    prev_hh = h_done_at.get((fpass, fl - 1))
                    if prev_hh:
                        P.wait("scalar", "sDVE", prev_hh)
                    P.op("scalar", lambda E, S, po=po: E["scalar"].activation(
                        out=gt_stage[:, 0:T], in_=PS[:, po + 1024:po + 1024 + T],
                        func=AF.Silu, bias=c_zero).then_inc(S["sACT"], 1))
                    silu_done = P.inc("sACT", 1)
                    P.wait("vector", "sACT", silu_done)
                    if fpass == 1 and fl < 2:
                        P.wait("vector", "sPE", dn_free_at[("pass", 0)])
                    P.op("vector", lambda E, S, fl=fl, po=po:
                         E["vector"].tensor_tensor(
                             out=hview(fl), in0=gt_stage[:, 0:T],
                             in1=PS[:, po:po + T], op=ALU.mult))
                    h_done_at[(fpass, fl)] = P.val("sDVE")
                h_all = P.val("sDVE")
                for nt in range(NDN):
                    li = dn_idx; dn_idx += 1
                    buf = li % 2
                    dn_sem = "sDN%d" % buf
                    if li >= 2:
                        P.wait("gpsimd", "sPE", dn_free_at[li - 2])
                    P.wait("gpsimd", "sPE", ug_free_at[ug_idx - 1])
                    P.op("gpsimd", lambda E, S, buf=buf, nt=nt, ft0=ft0,
                         nft_p=nft_p, ds=dn_sem: E["gpsimd"].dma_start(
                             out=dn_slab[buf][:, 0:nft_p, :],
                             in_=wdn_r[:, ft0:ft0 + nft_p, nt * DN:(nt + 1) * DN])
                         .then_inc(S[ds], 16))
                    dn_ld = P.inc(dn_sem, 16)
                    for mt in range(NMT):
                        bk_i = mt % 8
                        P.wait("tensor", dn_sem, dn_ld)
                        if nt == 0 and mt == 0:
                            P.wait("tensor", "sDVE", h_all)
                        prev_ev = dn_evac_at.get((fpass, nt - 1, mt))
                        if prev_ev:
                            P.wait("tensor", "sACT" if fpass == 0 else "sDVE",
                                   prev_ev)

                        def dnmm(E, S, nt=nt, mt=mt, buf=buf, bk_i=bk_i,
                                 nft_p=nft_p):
                            for fl in range(nft_p):
                                E["tensor"].matmul(
                                    bank(bk_i)[:, 0:DN],
                                    hview(fl)[:, mt * 128:(mt + 1) * 128],
                                    dn_slab[buf][:, fl, :],
                                    start=(fl == 0), stop=(fl == nft_p - 1))\
                                    .then_inc(S["sPE"], 1)
                        P.op("tensor", dnmm)
                        mm_done = P.inc("sPE", nft_p)
                        if mt == NMT - 1:
                            dn_free_at[li] = mm_done
                        li2 = nt * NMT + mt
                        stg = dstg[li2 % 2]
                        if fpass == 0:
                            dpst_sem = "sDPst%d" % (li2 % 2)
                            P.wait("scalar", "sPE", mm_done)
                            prev_st = dn_st_at.get((fpass, li2 - 2))
                            if prev_st:
                                P.wait("scalar", dpst_sem, prev_st)
                            P.op("scalar", lambda E, S, stg=stg, bk_i=bk_i:
                                 E["scalar"].copy(
                                     out=stg, in_=bank(bk_i)[:, 0:DN])
                                 .then_inc(S["sACT"], 1))
                            ev = P.inc("sACT", 1)
                            dn_evac_at[(fpass, nt, mt)] = ev
                            P.wait("scalar", "sACT", ev)
                            P.op("scalar", lambda E, S, stg=stg, mt=mt, nt=nt,
                                 ds=dpst_sem: E["scalar"].dma_start(
                                     out=dpart_d[mt * 128:(mt + 1) * 128,
                                                 nt * DN:(nt + 1) * DN], in_=stg)
                                 .then_inc(S[ds], 16))
                            dn_st_at[(fpass, li2)] = P.inc(dpst_sem, 16)
                        else:
                            dp = dpin[li2 % 2]
                            dp_sem = "sDP%d" % (li2 % 2)
                            prev_dp = dn_evac_at.get(("dp", li2 - 2))
                            if prev_dp:
                                P.wait("sync", "sDVE", prev_dp)
                            if li2 == 0:
                                P.wait("sync", "sDPst0", dpst_total[0])
                                P.wait("sync", "sDPst1", dpst_total[1])
                                P.wait("sync", "sPE", ug_free_at[ug_idx - 1])
                            P.op("sync", lambda E, S, dp=dp, mt=mt, nt=nt,
                                 ds=dp_sem: E["sync"].dma_start(
                                     out=dp, in_=dpart_d[mt * 128:(mt + 1) * 128,
                                                         nt * DN:(nt + 1) * DN])
                                 .then_inc(S[ds], 16))
                            dp_ld = P.inc(dp_sem, 16)
                            P.wait("vector", "sPE", mm_done)
                            P.wait("vector", dp_sem, dp_ld)
                            r2_sem = "sR2st%d" % (li2 % 2)
                            prev_st = dn_st_at.get((fpass, li2 - 2))
                            if prev_st:
                                P.wait("vector", r2_sem, prev_st)
                            P.op("vector", lambda E, S, stg=stg, dp=dp, bk_i=bk_i:
                                 E["vector"].tensor_tensor(
                                     out=stg, in0=bank(bk_i)[:, 0:DN], in1=dp,
                                     op=ALU.add))
                            ev = P.val("sDVE")
                            dn_evac_at[(fpass, nt, mt)] = ev
                            dn_evac_at[("dp", li2)] = ev
                            P.wait("scalar", "sDVE", ev)
                            P.op("scalar", lambda E, S, stg=stg, mt=mt, nt=nt,
                                 rs=r2_sem: E["scalar"].dma_start(
                                     out=r2_d[mt * 128:(mt + 1) * 128,
                                              nt * DN:(nt + 1) * DN], in_=stg)
                                 .then_inc(S[rs], 16))
                            dn_st_at[(fpass, li2)] = P.inc(r2_sem, 16)
                dn_free_at[("pass", fpass)] = P.val("sPE")
                if fpass == 0:
                    dpst_total = (P.val("sDPst0"), P.val("sDPst1"))
                else:
                    r2_stored = (P.val("sR2st0"), P.val("sR2st1"))
            p5_pe_done = P.val("sPE")
            p5_dve_done = P.val("sDVE")

            out_at = {}
            ln2_act_at = {}
            for mt in range(NMT):
                ss = stat_slot()
                p6_sem = "sP6%d" % (mt % 2)
                if mt % 2 == 0:
                    r2s = Bf32[:, 0:D]; yss = Bf32[:, D:2 * D]
                else:
                    r2s = Af32[:, 0:D]; yss = Af32[:, D:2 * D]
                rsum = Bf32[:, 2 * D:3 * D]
                dmp = Bf32[:, 3 * D:4 * D]
                if mt == 0:
                    P.wait("sync", "sPE", p5_pe_done)
                    P.wait("sync", "sDVE", p5_dve_done)
                    P.wait("sync", "sR2st0", r2_stored[0])
                    P.wait("sync", "sR2st1", r2_stored[1])
                    P.wait("sync", "sYst", y_all_stored)
                prev = out_at.get(mt - 2)
                if prev:
                    P.wait("sync", "sOUT", prev)
                P.op("sync", lambda E, S, r2s=r2s, mt=mt, ps=p6_sem:
                     E["sync"].dma_start(
                         out=r2s, in_=r2_d[mt * 128:(mt + 1) * 128, :])
                     .then_inc(S[ps], 16))
                P.inc(p6_sem, 16)
                P.op("sync", lambda E, S, yss=yss, mt=mt, ps=p6_sem:
                     E["sync"].dma_start(
                         out=yss, in_=y_d[mt * 128:(mt + 1) * 128, :])
                     .then_inc(S[ps], 16))
                ld = P.inc(p6_sem, 16)
                P.wait("vector", p6_sem, ld)
                pa = ln2_act_at.get(mt - 1)
                if pa:
                    P.wait("vector", "sACT", pa)
                P.op("vector", lambda E, S, r2s=r2s, yss=yss, rsum=rsum:
                     E["vector"].tensor_tensor(out=rsum, in0=r2s, in1=yss,
                                               op=ALU.add))
                P.op("vector", lambda E, S, rsum=rsum, ss=ss:
                     E["vector"].reduce_sum(ss[0], rsum, axis=AX.X, negate=True))
                P.op("vector", lambda E, S, ss=ss: E["vector"].tensor_scalar(
                    out=ss[1], in0=ss[0], scalar1=1.0 / D, scalar2=None,
                    op0=ALU.mult))
                P.op("vector", lambda E, S, ss=ss: E["vector"].tensor_tensor(
                    out=ss[2], in0=ss[1], in1=ss[1], op=ALU.mult))
                mu_done = P.val("sDVE")
                P.wait("scalar", "sDVE", mu_done)
                po6 = out_at.get(mt - 1)
                if po6:
                    P.wait("scalar", "sOUT", po6)
                P.op("scalar", lambda E, S, rsum=rsum, dmp=dmp, ss=ss:
                     E["scalar"].activation(out=dmp, in_=rsum, func=AF.Square,
                                            bias=c_zero, accum_out=ss[3])
                     .then_inc(S["sACT"], 1))
                sq_done = P.inc("sACT", 1)
                P.wait("vector", "sACT", sq_done)
                P.op("vector", lambda E, S, ss=ss: E["vector"].tensor_scalar(
                    out=ss[4], in0=ss[3], scalar1=1.0 / D, scalar2=ss[2],
                    op0=ALU.mult, op1=ALU.subtract))
                var_done = P.val("sDVE")
                P.wait("scalar", "sDVE", var_done)
                P.op("scalar", lambda E, S, ss=ss: E["scalar"].activation(
                    out=ss[5], in_=ss[4], func=AF.Sqrt, scale=1.0, bias=c_eps_ln)
                    .then_inc(S["sACT"], 1))
                sd_done = P.inc("sACT", 1)
                P.wait("vector", "sACT", sd_done)
                P.op("vector", lambda E, S, ss=ss: E["vector"].reciprocal(
                    ss[6], ss[5]))
                P.op("vector", lambda E, S, ss=ss: E["vector"].tensor_tensor(
                    out=ss[7], in0=ss[1], in1=ss[6], op=ALU.mult))
                rstd_done = P.val("sDVE")
                P.wait("scalar", "sDVE", rstd_done)
                P.op("scalar", lambda E, S, rsum=rsum, dmp=dmp, ss=ss:
                     E["scalar"].activation(out=dmp, in_=rsum, func=AF.Identity,
                                            bias=ss[7], scale=ss[6])
                     .then_inc(S["sACT"], 1))
                o_done = P.inc("sACT", 1)
                ln2_act_at[mt] = o_done
                P.wait("sync", "sACT", o_done)
                P.op("sync", lambda E, S, dmp=dmp, mt=mt: E["sync"].dma_start(
                    out=out_d[mt * 128:(mt + 1) * 128, :], in_=dmp)
                    .then_inc(S["sOUT"], 16))
                out_at[mt] = P.inc("sOUT", 16)

            if debug:
                P.wait("sync", "sACT", negm_ready)
                P.op("sync", lambda E, S: E["sync"].dma_start(
                    out=dbg["dbg_negm"], in_=negm[0:1, :]).then_inc(S["sOUT"], 16))
                P.inc("sOUT", 16)
                dump_list = [("dbg_attntm", attntm_d), ("dbg_y", y_d),
                             ("dbg_r2", r2_d)]
                if fused:
                    dump_list += [("dbg_qtm", qtm_sp), ("dbg_ktm_all", ktm_all),
                                  ("dbg_vtm_all", vtm_all)]
                for dname, srcd in dump_list:
                    P.op("sync", lambda E, S, d=dbg[dname], s=srcd:
                         E["sync"].dma_start(out=d, in_=s).then_inc(S["sOUT"], 16))
                    P.inc("sOUT", 16)
            P.wait("sync", "sOUT", P.val("sOUT"))

        with nc.Block() as blk:
            E = {}

            @blk.tensor
            def _(e):
                E["tensor"] = e
                for f in P.ops["tensor"]:
                    f(E, sems)

            @blk.scalar
            def _(e):
                E["scalar"] = e
                for f in P.ops["scalar"]:
                    f(E, sems)

            @blk.vector
            def _(e):
                E["vector"] = e
                for f in P.ops["vector"]:
                    f(E, sems)

            @blk.gpsimd
            def _(e):
                E["gpsimd"] = e
                for f in P.ops["gpsimd"]:
                    f(E, sems)

            @blk.sync
            def _(e):
                E["sync"] = e
                for f in P.ops["sync"]:
                    f(E, sems)

    return nc


def make_in_maps_A(cfg, inputs):
    D = cfg["D"]; S = cfg["S"]; T = S // NCORES
    x = np.asarray(inputs["L0_attn_hidden_states"]).reshape(S, D)
    common = dict(
        wq=np.asarray(inputs["L0_attn_Wq"]), wk=np.asarray(inputs["L0_attn_Wk"]),
        wv=np.asarray(inputs["L0_attn_Wv"]),
        bq=np.asarray(inputs["L0_attn_bq"]).reshape(1, D),
        bk=np.asarray(inputs["L0_attn_bk"]).reshape(1, D),
        bv=np.asarray(inputs["L0_attn_bv"]).reshape(1, D),
        gq=np.asarray(inputs["L0_attn_gq"]).reshape(1, D),
        gk=np.asarray(inputs["L0_attn_gk"]).reshape(1, D),
        ones_in=np.ones((1, 128), dtype=ml_dtypes.bfloat16),
        identf_in=np.eye(128, dtype=np.float32),
    )
    maps = []
    for c in range(NCORES):
        m = dict(common)
        m["x_fm"] = np.ascontiguousarray(x[c * T:(c + 1) * T, :].T)
        maps.append(m)
    return maps


def make_in_maps_B(cfg, inputs, resA):
    D = cfg["D"]; S = cfg["S"]; T = S // NCORES
    x = np.asarray(inputs["L0_attn_hidden_states"]).reshape(S, D)
    ktm_all = np.concatenate([np.asarray(r["ktm_sh"]) for r in resA], axis=0)
    vtm_all = np.concatenate([np.asarray(r["vtm_sh"]) for r in resA], axis=0)
    kn_all = np.concatenate([np.asarray(r["kn_sh"]) for r in resA], axis=0)
    common = dict(
        wo=np.asarray(inputs["L0_attn_Wo"]),
        wupT=np.ascontiguousarray(np.asarray(inputs["L0_ffn_W_up"]).T),
        wgateT=np.ascontiguousarray(np.asarray(inputs["L0_ffn_W_gate"]).T),
        wdownT=np.ascontiguousarray(np.asarray(inputs["L0_ffn_W_down"]).T),
        bo=np.asarray(inputs["L0_attn_bo"]).reshape(1, D),
        ones_in=np.ones((1, 128), dtype=ml_dtypes.bfloat16),
        onesf_in=np.ones((1, 128), dtype=np.float32),
        identf_in=np.eye(128, dtype=np.float32),
        ktm_all=ktm_all, vtm_all=vtm_all, kn_all=kn_all,
    )
    maps = []
    for c in range(NCORES):
        m = dict(common)
        m["x_tm"] = np.ascontiguousarray(x[c * T:(c + 1) * T, :])
        m["qtm_in"] = np.asarray(resA[c]["qtm_sp"])
        m["qn_in"] = np.asarray(resA[c]["qn_sh"])
        maps.append(m)
    return maps


_NC_CACHE = {}


def _get_programs(cfg):
    key = (id(cfg), cfg.get("debug", False))
    if key not in _NC_CACHE:
        _NC_CACHE[key] = (build_program(cfg, part="A"), build_program(cfg, part="B"))
    return _NC_CACHE[key]


def kernel(**inputs):
    cfg = FULL_CFG
    S = cfg["S"]; D = cfg["D"]
    ncA, ncB = _get_programs(cfg)
    resA = run_bass_kernel_spmd(ncA, make_in_maps_A(cfg, inputs),
                                core_ids=list(range(NCORES))).results
    resB = run_bass_kernel_spmd(ncB, make_in_maps_B(cfg, inputs, resA),
                                core_ids=list(range(NCORES))).results
    out = np.concatenate([np.asarray(r["out"]) for r in resB], axis=0)
    return out.reshape(1, S, D).astype(np.float32)


# version 10
# speedup vs baseline: 1.0590x; 1.0469x over previous
import math
import numpy as np
import ml_dtypes
import concourse.bass as bass
import concourse.mybir as mybir
from concourse.bass_utils import run_bass_kernel_spmd

F32 = mybir.dt.float32
BF16 = mybir.dt.bfloat16
AF = mybir.ActivationFunctionType
ALU = mybir.AluOpType
AX = mybir.AxisListType

NCORES = 8

FULL_CFG = dict(D=4096, S=8192, H=32, F=11008, QB=256, PN=256, ON=256,
                DN=256, G=6, MT_HALF=4, debug=False)

EPS_RMS = 1e-6
EPS_LN = 1e-5
INV_SQRT_DH = 1.0 / math.sqrt(128.0)
M_PAD = 0.25

SEM_NAMES = [
    "sX", "sGrow", "sConst", "sSL0", "sSL1", "sKst0", "sKst1", "sVst0",
    "sVst1", "sKn",
    "sQH0", "sQH1", "sKH0", "sKH1", "sVH0", "sVH1", "sAst", "sAh",
    "sXp0", "sXp1", "sYst", "sYbf", "sYfm", "sUG0", "sUG1", "sDN0", "sDN1",
    "sDPst0", "sDPst1", "sDP0", "sDP1", "sR2st0", "sR2st1", "sP60", "sP61",
    "sOUT",
    "sMq2", "sKnl", "sCC", "sPE", "sACT", "sDVE",
]


class Plan:

    def __init__(self):
        self.ops = {k: [] for k in ("tensor", "scalar", "vector", "gpsimd", "sync")}
        self.led = {}

    def inc(self, sem, amt):
        self.led[sem] = self.led.get(sem, 0) + amt
        return self.led[sem]

    def val(self, sem):
        return self.led.get(sem, 0)

    def op(self, eng, fn):
        if eng == "vector":
            prior = self.val("sDVE")
            if prior:
                self.ops[eng].append(
                    lambda E, S, v=prior: E["vector"].wait_ge(S["sDVE"], v))
            self.inc("sDVE", 1)
            self.ops[eng].append(
                lambda E, S, fn=fn: fn(E, S).then_inc(S["sDVE"], 1))
        else:
            self.ops[eng].append(fn)

    def wait(self, eng, sem, v):
        if not v:
            return
        self.ops[eng].append(lambda E, S, sem=sem, v=v: E[eng].wait_ge(S[sem], v))


def build_program(cfg, part="all"):
    D = cfg["D"]; S = cfg["S"]; H = cfg["H"]; F = cfg["F"]
    T = S // NCORES
    NKT = D // 128
    NMT = T // 128
    QB = min(cfg["QB"], T); NQB = T // QB; QTB = QB // 128
    PN = cfg["PN"]; NPN = D // PN
    ON = cfg["ON"]; NON = D // ON
    DN = cfg["DN"]; NDN = D // DN
    G = cfg["G"]
    SK = S // 128
    NFT = F // 128
    FT_PASS = (NFT + 1) // 2
    MT_HALF = min(cfg["MT_HALF"], NMT)
    N_HALVES = NMT // MT_HALF
    TH = T // N_HALVES
    NG = (SK + G - 1) // G
    debug = cfg.get("debug", False)
    do_A = part in ("all", "A")
    do_B = part in ("all", "B")
    fused = part == "all"

    nc = bass.Bass("TRN2", target_bir_lowering=False)

    bias_d = {}
    if do_A:
        x_fm = nc.dram_tensor("x_fm", [D, T], F32, kind="ExternalInput").ap()
        wq = nc.dram_tensor("wq", [D, D], F32, kind="ExternalInput").ap()
        wk = nc.dram_tensor("wk", [D, D], F32, kind="ExternalInput").ap()
        wv = nc.dram_tensor("wv", [D, D], F32, kind="ExternalInput").ap()
        for p in ("q", "k", "v"):
            bias_d[p] = nc.dram_tensor("b" + p, [1, D], F32, kind="ExternalInput").ap()
        gq_d = nc.dram_tensor("gq", [1, D], F32, kind="ExternalInput").ap()
        gk_d = nc.dram_tensor("gk", [1, D], F32, kind="ExternalInput").ap()
    if do_B:
        x_tm = nc.dram_tensor("x_tm", [T, D], F32, kind="ExternalInput").ap()
        wo = nc.dram_tensor("wo", [D, D], F32, kind="ExternalInput").ap()
        wupT = nc.dram_tensor("wupT", [D, F], F32, kind="ExternalInput").ap()
        wgateT = nc.dram_tensor("wgateT", [D, F], F32, kind="ExternalInput").ap()
        wdownT = nc.dram_tensor("wdownT", [F, D], F32, kind="ExternalInput").ap()
        bias_d["o"] = nc.dram_tensor("bo", [1, D], F32, kind="ExternalInput").ap()
        onesf_d = nc.dram_tensor("onesf_in", [1, 128], F32, kind="ExternalInput").ap()
        out_d = nc.dram_tensor("out", [T, D], F32, kind="ExternalOutput").ap()
    ones_d = nc.dram_tensor("ones_in", [1, 128], BF16, kind="ExternalInput").ap()
    identf_d = nc.dram_tensor("identf_in", [128, 128], F32, kind="ExternalInput").ap()

    spill_kind = "ExternalOutput" if part == "A" else "Internal"
    if do_A:
        ktm_sh = nc.dram_tensor("ktm_sh", [T, D], BF16, kind=spill_kind).ap()
        vtm_sh = nc.dram_tensor("vtm_sh", [T, D], BF16, kind=spill_kind).ap()
        qtm_sp = nc.dram_tensor("qtm_sp", [T, D], BF16, kind=spill_kind).ap()
        kn_sh = nc.dram_tensor("kn_sh", [H, 1], F32, kind=spill_kind).ap()
        qn_sh = nc.dram_tensor("qn_sh", [H, 1], F32, kind=spill_kind).ap()
    if do_B:
        if part == "B":
            qtm_sp = nc.dram_tensor("qtm_in", [D, T], BF16, kind="ExternalInput").ap()
            ktm_all = nc.dram_tensor("ktm_all", [D, S], BF16,
                                     kind="ExternalInput").ap()
            vtm_all = nc.dram_tensor("vtm_all", [128, H * SK * 128], BF16,
                                     kind="ExternalInput").ap()
            kn_all = nc.dram_tensor("kn_all", [NCORES * H, 1], F32,
                                    kind="ExternalInput").ap()
            qn_in = nc.dram_tensor("qn_in", [H, 1], F32, kind="ExternalInput").ap()
        else:
            ktm_all = nc.dram_tensor("ktm_all", [S, D], BF16).ap()
            vtm_all = nc.dram_tensor("vtm_all", [S, D], BF16).ap()
            kn_all = nc.dram_tensor("kn_all", [NCORES * H, 1], F32).ap()
        attntm_d = nc.dram_tensor("attntm_d", [T, D], BF16).ap()
        y_d = nc.dram_tensor("y_d", [T, D], F32).ap()
        ybf_d = nc.dram_tensor("ybf_d", [T, D], BF16).ap()
        dpart_d = nc.dram_tensor("dpart_d", [T, D], F32).ap()
        r2_d = nc.dram_tensor("r2_d", [T, D], F32).ap()
    dbg = {}
    if debug:
        items = []
        if do_B:
            items += [("dbg_attntm", [T, D], BF16), ("dbg_y", [T, D], F32),
                      ("dbg_r2", [T, D], F32), ("dbg_negm", [1, H], F32)]
        if fused:
            items += [("dbg_qtm", [T, D], BF16), ("dbg_ktm_all", [S, D], BF16),
                      ("dbg_vtm_all", [S, D], BF16)]
        for nm, shp, dt in items:
            dbg[nm] = nc.dram_tensor(nm, shp, dt, kind="ExternalOutput").ap()

    from contextlib import ExitStack
    with ExitStack() as stk:
        A = stk.enter_context(nc.sbuf_tensor("arenaA", [128, 33024], BF16))
        B_SZ = max(NMT * D, 8 * D, 2 * MT_HALF * D, NKT * T, 2 * FT_PASS * DN)
        B = stk.enter_context(nc.sbuf_tensor("arenaB", [128, B_SZ], BF16))
        C_BUF = 8448
        C = stk.enter_context(nc.sbuf_tensor("arenaC", [128, 2 * C_BUF], BF16))
        Dd = stk.enter_context(nc.sbuf_tensor("arenaD", [128, 12288], BF16))
        gq_b = stk.enter_context(nc.sbuf_tensor("gq_b", [128, D], BF16))
        gk_b = stk.enter_context(nc.sbuf_tensor("gk_b", [128, D], BF16))
        ones_sb = stk.enter_context(nc.sbuf_tensor("ones_sb", [1, 128], BF16))
        onesf_sb = stk.enter_context(nc.sbuf_tensor("onesf_sb", [1, 128], F32))
        bias_pc = stk.enter_context(nc.sbuf_tensor("bias_pc", [1, 2 * PN], BF16))
        identf = stk.enter_context(nc.sbuf_tensor("identf", [128, 128], F32))
        negm = stk.enter_context(nc.sbuf_tensor("negm", [128, H], F32))
        ssqp = stk.enter_context(nc.sbuf_tensor("ssqp", [128, NMT * NPN], F32))
        stats = stk.enter_context(nc.sbuf_tensor("stats", [128, 104], F32))
        nrm = stk.enter_context(nc.sbuf_tensor("nrm", [128, 5 * H + 3 + NCORES + 2 * H], F32))
        nrm2 = stk.enter_context(nc.sbuf_tensor("nrm2", [128, NMT * H], F32))
        nrow = stk.enter_context(nc.sbuf_tensor("nrow", [1, H], F32))
        PS = stk.enter_context(nc.psum_tensor("PS", [128, 4096], F32))

        sems = {s: stk.enter_context(nc.semaphore(s)) for s in SEM_NAMES}
        P = Plan()

        def bank(b):
            return PS[:, 512 * b:512 * (b + 1)]

        X_bf = A[:, 0:NKT * T].rearrange("p (n t) -> p n t", t=T)
        PT = A[:, 0:SK * QB].rearrange("p (k q) -> p k q", q=QB)
        VH_OFF = SK * QB
        VH = [A[:, VH_OFF + i * (SK * 129):VH_OFF + (i + 1) * (SK * 129)]
              .rearrange("p (k j) -> p k j", j=129) for i in range(2)]
        attn_half = A[:, 0:NKT * TH].rearrange("p (n t) -> p n t", t=TH)
        xpiece = [A[:, NKT * TH + 512 + i * 512:NKT * TH + 512 + (i + 1) * 512]
                  .bitcast(F32) for i in range(2)]
        h1 = A[:, 0:min(32, FT_PASS) * T].rearrange("p (n t) -> p n t", t=T)
        Af32 = A.ap().bitcast(F32)

        attn_tm = B[:, 0:NMT * D].rearrange("p (m d) -> p m d", d=D)
        r_half = B.ap().bitcast(F32)[:, 0:MT_HALF * D].rearrange(
            "p (m d) -> p m d", d=D)
        yfm_sb = B[:, 0:NKT * T].rearrange("p (n t) -> p n t", t=T)
        dn_slab = [B[:, i * (FT_PASS * DN):(i + 1) * (FT_PASS * DN)]
                   .rearrange("p (f d) -> p f d", d=DN) for i in range(2)]
        Bf32 = B.ap().bitcast(F32)

        slab = [C[:, i * C_BUF:i * C_BUF + NKT * PN]
                .rearrange("p (n m) -> p n m", m=PN) for i in range(2)]
        KH = [C[:, i * C_BUF:i * C_BUF + S] for i in range(2)]
        ug_slab = [C[:, i * C_BUF:i * C_BUF + 2 * NKT * 128] for i in range(2)]


        khat = [Dd[:, 0:D], Dd[:, D:2 * D]]
        dump = Dd[:, 2 * D:3 * D]
        dump2 = [dump[:, 0:PN], dump[:, PN:2 * PN]]
        QH = [Dd[:, i * T:(i + 1) * T] for i in range(2)]
        y_tm_f32 = Dd[:, 0:2 * D].bitcast(F32)
        p4dump = Dd[:, 2 * D:3 * D]
        h2 = (Dd[:, 0:(FT_PASS - 32) * T].rearrange("p (n t) -> p n t", t=T)
              if FT_PASS > 32 else None)
        gt_stage = Dd[:, 11264:12288]
        vstage = [C[:, 8192:8448], C[:, C_BUF + 8192:C_BUF + 8448]]
        dpin = [C[:, 0:1024].bitcast(F32)[:, i * 256:(i + 1) * 256] for i in range(2)]
        dstg = [C[:, C_BUF:C_BUF + 1024].bitcast(F32)[:, i * 256:(i + 1) * 256]
                for i in range(2)]

        def hview(fl):
            if fl < 32:
                return h1[:, fl, :]
            return h2[:, fl - 32, :]

        sct = [0]

        N_SLOTS = 8

        def stat_slot():
            sl = sct[0] % N_SLOTS; sct[0] += 1
            base = sl * 12
            return {i: stats[:, base + i:base + i + 1] for i in range(12)}

        s_rs = [stats[:, 96 + i:97 + i] for i in range(4)]
        c_eps_rms = stats[:, 100:101]
        c_eps_ln = stats[:, 101:102]
        c_zero = stats[:, 102:103]
        n2s = nrm[:, 0:H]
        kmaxAB = (nrm[:, H:2 * H], nrm[:, 2 * H:3 * H])
        qmaxAB = (nrm[:, 3 * H:4 * H], nrm[:, 4 * H:5 * H])
        mk2 = nrm[:, 5 * H:5 * H + 1]
        mq2 = nrm[:, 5 * H + 1:5 * H + 2]
        mcol = nrm[:, 5 * H + 2:5 * H + 3]
        knall_sb = nrm[:, 5 * H + 3:5 * H + 3 + NCORES]
        g2m = nrm[:, 5 * H + 3 + NCORES:6 * H + 3 + NCORES]
        g2max2 = nrm[:, 6 * H + 3 + NCORES:7 * H + 3 + NCORES]

        x_loaded = 0
        if do_A:
            P.op("gpsimd", lambda E, S: E["gpsimd"].dma_start(
                out=X_bf, in_=x_fm.rearrange("(n p) t -> p n t", p=128))
                .then_inc(S["sX"], 16))
            x_loaded = P.inc("sX", 16)
            P.op("gpsimd", lambda E, S: E["gpsimd"].dma_start(
                out=gq_b[0:1, :], in_=gq_d).then_inc(S["sGrow"], 16))
            P.inc("sGrow", 16)
            P.op("gpsimd", lambda E, S: E["gpsimd"].dma_start(
                out=gk_b[0:1, :], in_=gk_d).then_inc(S["sGrow"], 16))
            grow_ld = P.inc("sGrow", 16)
        const_list = [(ones_sb, ones_d), (identf, identf_d)]
        if do_B:
            const_list.append((onesf_sb, onesf_d))
        for dst, srcc in const_list:
            P.op("sync", lambda E, S, dst=dst, srcc=srcc: E["sync"].dma_start(
                out=dst[:, :], in_=srcc).then_inc(S["sConst"], 16))
            P.inc("sConst", 16)
        const_loaded = P.val("sConst")
        P.op("vector", lambda E, S: E["vector"].memset(c_eps_rms, EPS_RMS))
        P.op("vector", lambda E, S: E["vector"].memset(c_eps_ln, EPS_LN))
        P.op("vector", lambda E, S: E["vector"].memset(c_zero, 0.0))
        if do_A:
            P.op("vector", lambda E, S: E["vector"].memset(kmaxAB[1][:, :], -1e30))
            P.op("vector", lambda E, S: E["vector"].memset(qmaxAB[1][:, :], -1e30))
        memsets_done = P.val("sDVE")

        gains_ready = 0
        if do_A:
            P.wait("tensor", "sGrow", grow_ld)
            P.wait("tensor", "sConst", const_loaded)
            g_done = {}
            gi = 0
            for gb in (gq_b, gk_b):
                for ntb in range(D // 512):
                    off = 2048 + (gi % 4) * 512
                    if gi >= 4:
                        eng, cnt = g_done[gi - 4]
                        P.wait("tensor", eng, cnt)
                    P.op("tensor", lambda E, S, gb=gb, ntb=ntb, off=off:
                         E["tensor"].matmul(
                             PS[:, off:off + 512], ones_sb[:, :],
                             gb[0:1, ntb * 512:(ntb + 1) * 512],
                             start=True, stop=True).then_inc(S["sPE"], 1))
                    bc_mm = P.inc("sPE", 1)
                    if gb is gq_b:
                        P.wait("scalar", "sPE", bc_mm)
                        P.op("scalar", lambda E, S, ntb=ntb, off=off:
                             E["scalar"].copy(
                                 out=gq_b[:, ntb * 512:(ntb + 1) * 512],
                                 in_=PS[:, off:off + 512]).then_inc(S["sACT"], 1))
                        g_done[gi] = ("sACT", P.inc("sACT", 1))
                    else:
                        P.wait("vector", "sPE", bc_mm)
                        P.wait("vector", g_done[ntb][0], g_done[ntb][1])
                        P.op("vector", lambda E, S, ntb=ntb, off=off:
                             E["vector"].tensor_tensor(
                                 out=gk_b[:, ntb * 512:(ntb + 1) * 512],
                                 in0=PS[:, off:off + 512],
                                 in1=gq_b[:, ntb * 512:(ntb + 1) * 512],
                                 op=ALU.mult))
                        g_done[gi] = ("sDVE", P.val("sDVE"))
                    gi += 1
            gains_ready = P.val("sACT")
            P.op("vector", lambda E, S: E["vector"].reduce_max(
                g2m[:, :], gk_b.ap().rearrange("p (h j) -> p h j", j=128),
                axis=AX.X, apply_absolute_value=True))
            P.op("vector", lambda E, S: E["vector"].tensor_tensor(
                out=g2max2[:, :], in0=g2m[:, :], in1=g2m[:, :], op=ALU.mult))
            gains_dve = P.val("sDVE")

        w_of = {"k": wk, "v": wv, "q": wq} if do_A else {}
        g_of = {"k": gk_b, "q": gq_b}
        spill_of = {"k": ktm_sh, "q": qtm_sp} if do_A else {}
        nmax_of = {"k": kmaxAB, "q": qmaxAB}

        slab_done_at = {}
        evac_done_at = {}
        proj_mm_done = {"q": 0}
        norm_done = {}
        stt_at = {}
        khat_store_at = {}
        nmax_at = {}
        sqrtdone_at = {}
        kn_stored = 0
        qn_stored = 0
        vpiece_st = {}
        raw_done_at = {}
        raw_li2 = {}
        red_at = {}
        mk2_reduce_done = 0
        vst_total = (0, 0)
        mk2_copy_done = 0

        projs = ("q", "k", "v") if do_A else ()
        for pi, p in enumerate(projs):
            wsrc = w_of[p].rearrange("(n p2) m -> p2 n m", p2=128)
            for nt in range(NPN):
                li = pi * NPN + nt
                buf = li % 2
                sl_sem = "sSL%d" % buf
                if li >= 2:
                    P.wait("gpsimd", "sPE", slab_done_at[li - 2])
                P.op("gpsimd", lambda E, S, b=buf, wsrc=wsrc, nt=nt, sl=sl_sem:
                     E["gpsimd"].dma_start(
                         out=slab[b], in_=wsrc[:, :, nt * PN:(nt + 1) * PN])
                     .then_inc(S[sl], 16))
                P.inc(sl_sem, 16)
                P.op("gpsimd", lambda E, S, b=buf, p=p, nt=nt, sl=sl_sem:
                     E["gpsimd"].dma_start(
                         out=bias_pc[:, b * PN:(b + 1) * PN],
                         in_=bias_d[p][:, nt * PN:(nt + 1) * PN]).then_inc(S[sl], 16))
                slab_ld = P.inc(sl_sem, 16)
                for mt in range(NMT):
                    bk_i = mt % 8
                    P.wait("tensor", sl_sem, slab_ld)
                    if li < 2:
                        P.wait("tensor", "sConst", const_loaded)
                        P.wait("tensor", "sACT", gains_ready)
                        P.wait("tensor", "sX", x_loaded)
                    prev = evac_done_at.get((pi, nt - 1, mt))
                    if prev:
                        P.wait("tensor", "sACT", prev)
                    if pi > 0 and nt == 0:
                        P.wait("tensor", "sACT",
                               evac_done_at[(pi - 1, NPN - 1, mt)])
                    if pi == 2 and nt == 0 and mt == 2 and mk2_copy_done:
                        P.wait("tensor", "sACT", mk2_copy_done)

                    def mm_group(E, S, nt=nt, mt=mt, buf=buf, bk_i=bk_i):
                        for kt in range(NKT):
                            E["tensor"].matmul(
                                bank(bk_i)[:, 0:PN],
                                X_bf[:, kt, mt * 128:(mt + 1) * 128],
                                slab[buf][:, kt, :], start=(kt == 0),
                                stop=False).then_inc(S["sPE"], 1)
                        E["tensor"].matmul(
                            bank(bk_i)[:, 0:PN], ones_sb[:, :],
                            bias_pc[:, buf * PN:(buf + 1) * PN],
                            start=False, stop=True).then_inc(S["sPE"], 1)
                    P.op("tensor", mm_group)
                    mm_done = P.inc("sPE", NKT + 1)
                    if mt == NMT - 1:
                        slab_done_at[li] = mm_done
                    P.wait("scalar", "sPE", mm_done)
                    if li < 2 and mt == 0:
                        P.wait("scalar", "sDVE", memsets_done)
                    if p == "v":
                        li2 = nt * NMT + mt
                        vs = vstage[li2 % 2]
                        vs_sem = "sVst%d" % (li2 % 2)
                        pv_st = vpiece_st.get(li2 - 2)
                        if pv_st:
                            P.wait("scalar", vs_sem, pv_st)
                        P.op("scalar", lambda E, S, vs=vs, bk_i=bk_i:
                             E["scalar"].copy(
                                 out=vs, in_=bank(bk_i)[:, 0:PN])
                             .then_inc(S["sACT"], 1))
                        ev = P.inc("sACT", 1)
                        evac_done_at[(pi, nt, mt)] = ev
                        P.wait("scalar", "sACT", ev)
                        P.op("scalar", lambda E, S, vs=vs, mt=mt, nt=nt,
                             vs_sem=vs_sem: E["scalar"].dma_start(
                                 out=vtm_sh[mt * 128:(mt + 1) * 128,
                                            nt * PN:(nt + 1) * PN], in_=vs)
                             .then_inc(S[vs_sem], 16))
                        vpiece_st[li2] = P.inc(vs_sem, 16)
                    else:
                        li2 = nt * NMT + mt
                        if p == "k" and nt == 0:
                            P.wait("scalar", "sDVE", max(stt_at[("q", mt)],
                                                         red_at[("q", mt)]))
                        P.op("scalar", lambda E, S, mt=mt, nt=nt, bk_i=bk_i:
                             E["scalar"].copy(
                                 out=attn_tm[:, mt, nt * PN:(nt + 1) * PN],
                                 in_=bank(bk_i)[:, 0:PN]).then_inc(S["sACT"], 1))
                        gidx = pi * NPN * NMT + li2
                        pg = raw_li2.get(gidx - 2)
                        if pg:
                            P.wait("scalar", "sDVE", pg)
                        P.op("scalar", lambda E, S, mt=mt, nt=nt, bk_i=bk_i,
                             li2=li2: E["scalar"].activation(
                                 out=dump2[li2 % 2], in_=bank(bk_i)[:, 0:PN],
                                 func=AF.Square, bias=c_zero,
                                 accum_out=ssqp[:, mt * NPN + nt:mt * NPN + nt + 1])
                             .then_inc(S["sACT"], 1))
                        ev2 = P.inc("sACT", 2)
                        evac_done_at[(pi, nt, mt)] = ev2
                        P.wait("vector", "sACT", ev2)
                        HPP = PN // 128
                        P.op("vector", lambda E, S, mt=mt, nt=nt, li2=li2, HPP=HPP:
                             E["vector"].tensor_reduce(
                                 out=nrm2[:, mt * H + nt * HPP:
                                          mt * H + (nt + 1) * HPP],
                                 in_=dump2[li2 % 2].rearrange(
                                     "p (h j) -> p h j", j=128),
                                 axis=AX.X, op=ALU.add))
                        raw_done_at[(pi, nt, mt)] = P.val("sDVE")
                        raw_li2[pi * NPN * NMT + li2] = raw_done_at[(pi, nt, mt)]
            proj_mm_done[p] = P.val("sPE")

            if p == "v":
                vst_total = (P.val("sVst0"), P.val("sVst1"))
                norm_done[p] = P.val("sACT")
                continue

            gbc = g_of[p]
            maxA, maxB = nmax_of[p]
            slots = {}
            for mt in range(NMT):
                ev = evac_done_at[(pi, NPN - 1, mt)]
                ss = stat_slot()
                slots[mt] = ss
                P.wait("vector", "sACT", ev)
                if mt == 0:
                    if p == "k":
                        P.wait("vector", "sACT", gains_ready)
                        P.wait("vector", "sDVE", gains_dve)
                    P.wait("vector", "sDVE", memsets_done)
                prevsq = sqrtdone_at.get((p, mt - 8))
                if prevsq:
                    P.wait("vector", "sACT", prevsq)
                P.op("vector", lambda E, S, mt=mt, ss=ss: E["vector"].reduce_sum(
                    ss[0], ssqp[:, mt * NPN:(mt + 1) * NPN], axis=AX.X))
                red_at[(p, mt)] = P.val("sDVE")
            for mt in range(NMT):
                ss = slots[mt]
                P.wait("scalar", "sDVE", red_at[(p, mt)])
                P.op("scalar", lambda E, S, ss=ss: E["scalar"].activation(
                    out=ss[1], in_=ss[0], func=AF.Sqrt, scale=1.0 / D,
                    bias=c_eps_rms).then_inc(S["sACT"], 1))
                sq_done = P.inc("sACT", 1)
                sqrtdone_at[(p, mt)] = sq_done
                P.wait("vector", "sACT", sq_done)
                P.op("vector", lambda E, S, ss=ss: E["vector"].reciprocal(ss[2], ss[1]))
                if p == "q":
                    P.op("vector", lambda E, S, ss=ss: E["vector"].tensor_scalar(
                        out=ss[3], in0=ss[2], scalar1=INV_SQRT_DH, scalar2=None,
                        op0=ALU.mult))
                    rstd_i = 3
                else:
                    rstd_i = 2
                kh = khat[mt % 2]
                ks_sem = "sKst%d" % (mt % 2)
                prev_store = khat_store_at.get((p, mt - 2))
                if prev_store:
                    P.wait("vector", ks_sem, prev_store)
                if p == "k" and mt <= 1:
                    cands = [v for (pp, m2), v in khat_store_at.items()
                             if pp == "q" and m2 % 2 == mt % 2]
                    if cands:
                        P.wait("vector", ks_sem, max(cands))
                if p == "q":
                    P.op("vector", lambda E, S, mt=mt, ss=ss, r=rstd_i, kh=kh:
                         E["vector"].tensor_scalar(
                             out=kh[:, :], in0=attn_tm[:, mt, :], scalar1=ss[r],
                             scalar2=None, op0=ALU.mult))
                else:
                    P.op("vector", lambda E, S, mt=mt, ss=ss, r=rstd_i, gbc=gbc,
                         kh=kh: E["vector"].scalar_tensor_tensor(
                             out=kh[:, :], in0=attn_tm[:, mt, :], scalar=ss[r],
                             in1=gbc[:, :], op0=ALU.mult, op1=ALU.mult))
                stt_at[(p, mt)] = P.val("sDVE")
                P.wait("sync" if mt % 2 == 0 else "scalar", "sDVE",
                       stt_at[(p, mt)])
                P.op("sync" if mt % 2 == 0 else "scalar",
                     lambda E, S, mt=mt, kh=kh, p=p, ks=ks_sem,
                     q=("sync" if mt % 2 == 0 else "scalar"): E[q].dma_start(
                         out=spill_of[p][mt * 128:(mt + 1) * 128, :], in_=kh[:, :])
                     .then_inc(S[ks], 16))
                khat_store_at[(p, mt)] = P.inc(ks_sem, 16)
                P.op("vector", lambda E, S, ss=ss, r=rstd_i:
                     E["vector"].tensor_tensor(
                         out=ss[5], in0=ss[r], in1=ss[r], op=ALU.mult))
                if p == "q":
                    P.op("vector", lambda E, S, mt=mt, ss=ss:
                         E["vector"].tensor_scalar(
                             out=n2s[:, :], in0=nrm2[:, mt * H:(mt + 1) * H],
                             scalar1=ss[5], scalar2=None, op0=ALU.mult))
                else:
                    P.op("vector", lambda E, S, mt=mt, ss=ss:
                         E["vector"].scalar_tensor_tensor(
                             out=n2s[:, :], in0=nrm2[:, mt * H:(mt + 1) * H],
                             scalar=ss[5], in1=g2max2[:, :],
                             op0=ALU.mult, op1=ALU.mult))
                srcm, dstm = (maxB, maxA) if mt % 2 == 0 else (maxA, maxB)
                P.op("vector", lambda E, S, s=srcm, d=dstm: E["vector"].tensor_tensor(
                    out=d[:, :], in0=s[:, :], in1=n2s[:, :], op=ALU.max))
                nmax_at[(p, mt)] = P.val("sDVE")
            norm_done[p] = P.val("sACT")

        if do_A:
            fin = kmaxAB[0] if NMT % 2 == 1 else kmaxAB[1]
            P.wait("tensor", "sDVE", nmax_at[("k", NMT - 1)])
            P.wait("tensor", "sACT", evac_done_at[(2, NPN - 1, NMT - 1)])
            P.op("tensor", lambda E, S, fin=fin: E["tensor"].transpose(
                PS[0:H, 1024:1152], fin[:, :], identf[:, :]).then_inc(S["sPE"], 1))
            t_done = P.inc("sPE", 1)
            P.wait("scalar", "sPE", t_done)
            P.wait("scalar", "sDVE", nmax_at[("k", NMT - 1)])
            P.op("scalar", lambda E, S: E["scalar"].copy(
                out=dump.bitcast(F32)[0:H, 0:128], in_=PS[0:H, 1024:1152])
                .then_inc(S["sACT"], 1))
            c_done = P.inc("sACT", 1)
            mk2_copy_done = c_done
            P.wait("vector", "sACT", c_done)
            P.op("vector", lambda E, S: E["vector"].reduce_max(
                mk2[0:H, :], dump.bitcast(F32)[0:H, 0:128], axis=AX.X))
            mk_done = mk2_reduce_done = P.val("sDVE")
            P.wait("sync", "sDVE", mk_done)
            P.op("sync", lambda E, S: E["sync"].dma_start(
                out=kn_sh, in_=mk2[0:H, :]).then_inc(S["sKn"], 16))
            kn_stored = P.inc("sKn", 16)
        if do_A:
            finq = qmaxAB[0] if NMT % 2 == 1 else qmaxAB[1]
            P.wait("tensor", "sDVE", nmax_at[("q", NMT - 1)])
            P.op("tensor", lambda E, S: E["tensor"].transpose(
                PS[0:H, 1280:1408], finq[:, :], identf[:, :]).then_inc(S["sPE"], 1))
            tq_done = P.inc("sPE", 1)
            P.wait("scalar", "sPE", tq_done)
            if mk2_reduce_done:
                P.wait("scalar", "sDVE", mk2_reduce_done)
            P.op("scalar", lambda E, S: E["scalar"].copy(
                out=dump.bitcast(F32)[0:H, 0:128], in_=PS[0:H, 1280:1408])
                .then_inc(S["sACT"], 1))
            cq_done = P.inc("sACT", 1)
            P.wait("vector", "sACT", cq_done)
            P.op("vector", lambda E, S: E["vector"].reduce_max(
                mq2[0:H, :], dump.bitcast(F32)[0:H, 0:128], axis=AX.X))
            mqx_done = P.val("sDVE")
            P.wait("sync", "sDVE", mqx_done)
            P.wait("sync", "sKn", kn_stored)
            P.op("sync", lambda E, S: E["sync"].dma_start(
                out=qn_sh, in_=mq2[0:H, :]).then_inc(S["sKn"], 16))
            qn_stored = P.inc("sKn", 16)
        if part == "A":
            P.wait("sync", "sKst0", P.val("sKst0"))
            P.wait("sync", "sKst1", P.val("sKst1"))
            P.wait("sync", "sVst0", P.val("sVst0"))
            P.wait("sync", "sVst1", P.val("sVst1"))
            P.wait("sync", "sKn", P.val("sKn"))

        agn_done = agv_done = 0
        if fused:
            P.wait("gpsimd", "sKst0", P.val("sKst0"))
            P.wait("gpsimd", "sKst1", P.val("sKst1"))
            P.op("gpsimd", lambda E, S: E["gpsimd"].collective_compute(
                "AllGather", ALU.bypass, replica_groups=[list(range(NCORES))],
                ins=[ktm_sh.opt()], outs=[ktm_all.opt()]).then_inc(S["sCC"], 16))
            P.inc("sCC", 16)
            P.wait("gpsimd", "sKn", qn_stored)
            P.op("gpsimd", lambda E, S: E["gpsimd"].collective_compute(
                "AllGather", ALU.bypass, replica_groups=[list(range(NCORES))],
                ins=[kn_sh.opt()], outs=[kn_all.opt()]).then_inc(S["sCC"], 16))
            agn_done = P.inc("sCC", 16)
            P.wait("gpsimd", "sVst0", vst_total[0])
            P.wait("gpsimd", "sVst1", vst_total[1])
            P.op("gpsimd", lambda E, S: E["gpsimd"].collective_compute(
                "AllGather", ALU.bypass, replica_groups=[list(range(NCORES))],
                ins=[vtm_sh.opt()], outs=[vtm_all.opt()]).then_inc(S["sCC"], 16))
            agv_done = P.inc("sCC", 16)

        if do_B:
            if part == "B":
                P.op("sync", lambda E, S: E["sync"].dma_start(
                    out=mq2[0:H, :], in_=qn_in).then_inc(S["sMq2"], 16))
                mq2_ld = P.inc("sMq2", 16)
                P.wait("vector", "sMq2", mq2_ld)
            if fused:
                P.wait("sync", "sCC", agn_done)
            P.op("sync", lambda E, S: E["sync"].dma_start(
                out=knall_sb[0:H, :],
                in_=kn_all.rearrange("(r h) o -> h (r o)", r=NCORES),
                allow_slow_non_contiguous=True).then_inc(S["sKnl"], 16))
            knall_ld = P.inc("sKnl", 16)
            P.wait("vector", "sKnl", knall_ld)
            P.wait("vector", "sDVE", memsets_done)
            P.op("vector", lambda E, S: E["vector"].reduce_max(
                mk2[0:H, :], knall_sb[0:H, :], axis=AX.X))
            P.op("vector", lambda E, S: E["vector"].tensor_tensor(
                out=mcol[0:H, :], in0=mq2[0:H, :], in1=mk2[0:H, :], op=ALU.mult))
            m2_done = P.val("sDVE")
            P.wait("scalar", "sDVE", m2_done)
            P.op("scalar", lambda E, S: E["scalar"].activation(
                out=mcol[0:H, :], in_=mcol[0:H, :], func=AF.Sqrt,
                bias=c_zero[0:H, :]).then_inc(S["sACT"], 1))
            msq_done = P.inc("sACT", 1)
            P.wait("vector", "sACT", msq_done)
            P.op("vector", lambda E, S: E["vector"].tensor_scalar(
                out=mcol[0:H, :], in0=mcol[0:H, :], scalar1=M_PAD, scalar2=-1.0,
                op0=ALU.add, op1=ALU.mult))
            mneg_done = P.val("sDVE")
            P.wait("tensor", "sDVE", mneg_done)
            if part == "B":
                P.wait("tensor", "sConst", const_loaded)
            P.op("tensor", lambda E, S: E["tensor"].transpose(
                PS[0:1, 1024:1024 + H], mcol[0:H, :], identf[0:H, 0:H])
                .then_inc(S["sPE"], 1))
            tm_done = P.inc("sPE", 1)
            P.wait("scalar", "sPE", tm_done)
            P.op("scalar", lambda E, S: E["scalar"].copy(
                out=nrow[:, :], in_=PS[0:1, 1024:1024 + H]).then_inc(S["sACT"], 1))
            nrow_done = P.inc("sACT", 1)
            P.wait("tensor", "sACT", nrow_done)
            P.op("tensor", lambda E, S: E["tensor"].matmul(
                PS[:, 1536:1536 + H], onesf_sb[:, :], nrow[:, :],
                start=True, stop=True).then_inc(S["sPE"], 1))
            nbc_mm = P.inc("sPE", 1)
            P.wait("scalar", "sPE", nbc_mm)
            P.op("scalar", lambda E, S: E["scalar"].copy(
                out=negm[:, :], in_=PS[:, 1536:1536 + H]).then_inc(S["sACT"], 1))
            negm_ready = P.inc("sACT", 1)

            P.wait("vector", "sPE", proj_mm_done["q"])
            for i in range(2):
                P.op("vector", lambda E, S, i=i: E["vector"].memset(
                    VH[i][:, :, 128:129], 1.0))
            vones_ready = P.val("sDVE")

            kv_ld_at = {}
            head_pe_done = {}
            if fused:
                vtm_all_r = vtm_all.rearrange("(k p) (n j) -> n p k j",
                                              p=128, j=128)
            else:
                vtm_all_r = vtm_all.rearrange("p (h k j) -> h p k j",
                                              k=SK, j=128)

            def emit_head_loads(h):
                buf = h % 2
                qs, ks, vs = "sQH%d" % buf, "sKH%d" % buf, "sVH%d" % buf
                v_eng = "sync" if fused else "gpsimd"
                if h >= 2:
                    P.wait("sync", "sPE", head_pe_done[h - 2])
                    if v_eng != "sync":
                        P.wait(v_eng, "sPE", head_pe_done[h - 2])
                else:
                    P.wait("sync", "sPE", proj_mm_done["q"])
                    P.wait("sync", "sDVE", vones_ready)
                    if v_eng != "sync":
                        P.wait(v_eng, "sDVE", vones_ready)
                    if fused:
                        P.wait("sync", "sCC", agv_done)
                        P.wait("sync", "sKst0", P.val("sKst0"))
                        P.wait("sync", "sKst1", P.val("sKst1"))
                if fused:
                    P.op("sync", lambda E, S, h=h, buf=buf, qs=qs:
                         E["sync"].dma_start_transpose(
                             out=QH[buf][:, :], in_=qtm_sp[:, h * 128:(h + 1) * 128])
                         .then_inc(S[qs], 16))
                else:
                    P.op("sync", lambda E, S, h=h, buf=buf, qs=qs:
                         E["sync"].dma_start(
                             out=QH[buf][:, :], in_=qtm_sp[h * 128:(h + 1) * 128, :])
                         .then_inc(S[qs], 16))
                q_ld = P.inc(qs, 16)
                if fused:
                    P.op("sync", lambda E, S, h=h, buf=buf, ks=ks:
                         E["sync"].dma_start_transpose(
                             out=KH[buf][:, :], in_=ktm_all[:, h * 128:(h + 1) * 128])
                         .then_inc(S[ks], 16))
                else:
                    P.op("sync", lambda E, S, h=h, buf=buf, ks=ks:
                         E["sync"].dma_start(
                             out=KH[buf][:, :], in_=ktm_all[h * 128:(h + 1) * 128, :])
                         .then_inc(S[ks], 16))
                k_ld = P.inc(ks, 16)
                P.op(v_eng, lambda E, S, h=h, buf=buf, vs=vs, v_eng=v_eng:
                     E[v_eng].dma_start(
                         out=VH[buf][:, :, 0:128], in_=vtm_all_r[h])
                     .then_inc(S[vs], 16))
                v_ld = P.inc(vs, 16)
                kv_ld_at[h] = (qs, q_ld, ks, k_ld, vs, v_ld)

            PV0 = 2 * G * QB
            assert PV0 + QTB * 512 <= 4096
            exp_done = {}
            pv_done = {}
            pv_group = {}
            attn_written = {}

            emit_head_loads(0)
            if H > 1:
                emit_head_loads(1)
            for h in range(H):
                buf = h % 2
                for qb in range(NQB):
                    pv_g = None
                    for g in range(NG):
                        kts = list(range(g * G, min((g + 1) * G, SK)))
                        sco = (g % 2) * (G * QB)
                        if g == 0 and qb == 0:
                            qs, qv, ks, kv2, vs, vv = kv_ld_at[h]
                            P.wait("tensor", qs, qv)
                            P.wait("tensor", ks, kv2)
                            P.wait("tensor", vs, vv)
                        if h == 0 and qb == 0 and g <= 1:
                            P.wait("tensor", "sACT", negm_ready)
                        prev_exp = exp_done.get((h, qb, g - 2))
                        if not prev_exp and g - 2 + NG >= 0:
                            if qb > 0:
                                prev_exp = exp_done.get((h, qb - 1, g - 2 + NG))
                            elif h > 0:
                                prev_exp = exp_done.get((h - 1, NQB - 1, g - 2 + NG))
                        if prev_exp:
                            P.wait("tensor", "sACT", prev_exp)

                        def smm(E, S, kts=kts, sco=sco, qb=qb, buf=buf):
                            for i, kt in enumerate(kts):
                                E["tensor"].matmul(
                                    PS[:, sco + i * QB:sco + (i + 1) * QB],
                                    KH[buf][:, kt * 128:(kt + 1) * 128],
                                    QH[buf][:, qb * QB:(qb + 1) * QB],
                                    start=True, stop=True).then_inc(S["sPE"], 1)
                        P.op("tensor", smm)
                        sc_done = P.inc("sPE", len(kts))
                        P.wait("scalar", "sPE", sc_done)
                        prev_pv = pv_group.get((h, qb - 1, g)) if qb > 0 else (
                            pv_group.get((h - 1, NQB - 1, g)) if h > 0 else None)
                        if prev_pv:
                            P.wait("scalar", "sPE", prev_pv)
                        if h == 0 and qb == 0 and g == 0:
                            P.wait("scalar", "sACT", negm_ready)

                        def expg(E, S, kts=kts, sco=sco, h=h):
                            E["scalar"].activation(
                                out=PT[:, kts[0]:kts[0] + len(kts), :],
                                in_=PS[:, sco:sco + len(kts) * QB],
                                func=AF.Exp, bias=negm[:, h:h + 1], scale=1.0)\
                                .then_inc(S["sACT"], 1)
                        P.op("scalar", expg)
                        exp_done[(h, qb, g)] = P.inc("sACT", 1)
                        for gg in ([g - 1] if g >= 1 else []) + \
                                ([g] if g == NG - 1 else []):
                            kts2 = list(range(gg * G, min((gg + 1) * G, SK)))
                            P.wait("tensor", "sACT", exp_done[(h, qb, gg)])
                            if gg == 0:
                                prev_norm = attn_written.get((h, qb - 1)) if qb > 0 \
                                    else (attn_written.get((h - 1, NQB - 1))
                                          if h > 0 else None)
                                if prev_norm:
                                    P.wait("tensor", "sDVE", prev_norm)

                            def pvg(E, S, kts2=kts2, buf=buf):
                                for kt in kts2:
                                    for qt in range(QTB):
                                        E["tensor"].matmul(
                                            PS[:, PV0 + qt * 512:
                                               PV0 + qt * 512 + 129],
                                            PT[:, kt, qt * 128:(qt + 1) * 128],
                                            VH[buf][:, kt, :],
                                            start=(kt == 0), stop=(kt == SK - 1),
                                            skip_group_check=True)\
                                            .then_inc(S["sPE"], 1)
                            P.op("tensor", pvg)
                            pv_g = P.inc("sPE", len(kts2) * QTB)
                            pv_group[(h, qb, gg)] = pv_g
                    pv_done[(h, qb)] = pv_g
                    P.wait("vector", "sPE", pv_done[(h, qb)])
                    for qt in range(QTB):
                        tt = qb * QTB + qt
                        P.op("vector", lambda E, S, qt=qt: E["vector"].reciprocal(
                            s_rs[qt],
                            PS[:, PV0 + qt * 512 + 128:PV0 + qt * 512 + 129]))
                        P.op("vector", lambda E, S, qt=qt, tt=tt, h=h:
                             E["vector"].tensor_scalar(
                                 out=attn_tm[:, tt, h * 128:(h + 1) * 128],
                                 in0=PS[:, PV0 + qt * 512:PV0 + qt * 512 + 128],
                                 scalar1=s_rs[qt], scalar2=None, op0=ALU.mult))
                    attn_written[(h, qb)] = P.val("sDVE")
                head_pe_done[h] = pv_done[(h, NQB - 1)]
                if h + 2 < H:
                    emit_head_loads(h + 2)
            for qb in range(NQB):
                for qt in range(QTB):
                    tt = qb * QTB + qt
                    P.wait("sync", "sDVE", attn_written[(H - 1, qb)])
                    P.op("sync", lambda E, S, tt=tt: E["sync"].dma_start(
                        out=attntm_d[tt * 128:(tt + 1) * 128, :],
                        in_=attn_tm[:, tt, :]).then_inc(S["sAst"], 16))
                    P.inc("sAst", 16)
            attntm_stored = P.val("sAst")

            wo_r = wo.rearrange("(n p2) m -> p2 n m", p2=128)
            o_slab_idx = 0
            o_slab_free = {}
            o_evac_at = {}
            ybf_store_at = {}
            first_dve_p4 = True
            prev_p4sq = 0
            prev_half_yst = prev_half_ybf = 0
            pending_ybf = []
            for half in range(N_HALVES):
                P.wait("sync", "sAst", attntm_stored)
                P.wait("scalar", "sAst", attntm_stored)
                if half > 0:
                    P.wait("sync", "sPE", o_slab_free[o_slab_idx - 1])
                    P.wait("scalar", "sPE", o_slab_free[o_slab_idx - 1])
                for kt in range(NKT):
                    P.op("sync", lambda E, S, kt=kt, half=half:
                         E["sync"].dma_start_transpose(
                             out=attn_half[:, kt, :],
                             in_=attntm_d[half * TH:(half + 1) * TH,
                                          kt * 128:(kt + 1) * 128])
                         .then_inc(S["sAh"], 16))
                    P.inc("sAh", 16)
                ah_ld = P.val("sAh")
                for nt in range(NON):
                    li = o_slab_idx; o_slab_idx += 1
                    buf = li % 2
                    sl_sem = "sSL%d" % buf
                    if li >= 2:
                        P.wait("gpsimd", "sPE", o_slab_free[li - 2])
                    else:
                        P.wait("gpsimd", "sPE", head_pe_done[H - 1])
                    P.op("gpsimd", lambda E, S, buf=buf, nt=nt, sl=sl_sem:
                         E["gpsimd"].dma_start(
                             out=slab[buf][:, :, 0:ON],
                             in_=wo_r[:, :, nt * ON:(nt + 1) * ON])
                         .then_inc(S[sl], 16))
                    P.inc(sl_sem, 16)
                    P.op("gpsimd", lambda E, S, buf=buf, nt=nt, sl=sl_sem:
                         E["gpsimd"].dma_start(
                             out=bias_pc[:, buf * PN:buf * PN + ON],
                             in_=bias_d["o"][:, nt * ON:(nt + 1) * ON])
                         .then_inc(S[sl], 16))
                    sl_ld = P.inc(sl_sem, 16)
                    if nt == 1 and pending_ybf:
                        for fn in pending_ybf:
                            P.ops["gpsimd"].append(fn)
                        pending_ybf = []
                    for mt in range(MT_HALF):
                        mtg = half * MT_HALF + mt
                        bk_i = mt % 4 + 4 * (nt % 2)
                        li2 = nt * MT_HALF + mt
                        xp = xpiece[li2 % 2]
                        xp_sem = "sXp%d" % (li2 % 2)
                        gx = half * (NON * MT_HALF) + li2
                        prev_x = o_evac_at.get(("x", gx - 2))
                        if prev_x:
                            P.wait("sync", "sDVE", prev_x)
                        P.op("sync", lambda E, S, xp=xp, mtg=mtg, nt=nt, xs=xp_sem:
                             E["sync"].dma_start(
                                 out=xp, in_=x_tm[mtg * 128:(mtg + 1) * 128,
                                                  nt * ON:(nt + 1) * ON])
                             .then_inc(S[xs], 16))
                        xp_ld = P.inc(xp_sem, 16)
                        P.wait("tensor", sl_sem, sl_ld)
                        if li < 2:
                            P.wait("tensor", "sAh", ah_ld)
                            P.wait("tensor", "sConst", const_loaded)
                        if half > 0 and nt == 0:
                            P.wait("tensor", "sAh", ah_ld)
                        gn = half * NON + nt
                        prev_ev = o_evac_at.get(("g", gn - 2, mt))
                        if prev_ev:
                            P.wait("tensor", "sDVE", prev_ev)

                        def omm(E, S, nt=nt, mt=mt, buf=buf, bk_i=bk_i):
                            for kt in range(NKT):
                                E["tensor"].matmul(
                                    bank(bk_i)[:, 0:ON],
                                    attn_half[:, kt, mt * 128:(mt + 1) * 128],
                                    slab[buf][:, kt, 0:ON],
                                    start=(kt == 0), stop=False)\
                                    .then_inc(S["sPE"], 1)
                            E["tensor"].matmul(
                                bank(bk_i)[:, 0:ON], ones_sb[:, :],
                                bias_pc[:, buf * PN:buf * PN + ON],
                                start=False, stop=True).then_inc(S["sPE"], 1)
                        P.op("tensor", omm)
                        mm_done = P.inc("sPE", NKT + 1)
                        if mt == MT_HALF - 1:
                            o_slab_free[li] = mm_done
                        P.wait("vector", "sPE", mm_done)
                        P.wait("vector", xp_sem, xp_ld)
                        if first_dve_p4:
                            P.wait("vector", "sAst", attntm_stored)
                            first_dve_p4 = False
                        if half > 0 and nt == 0 and mt == 0:
                            P.wait("vector", "sYst", prev_half_yst)
                            P.wait("vector", "sYbf", prev_half_ybf)
                        P.op("vector", lambda E, S, mt=mt, nt=nt, bk_i=bk_i, xp=xp:
                             E["vector"].tensor_tensor(
                                 out=r_half[:, mt, nt * ON:(nt + 1) * ON],
                                 in0=bank(bk_i)[:, 0:ON], in1=xp, op=ALU.add))
                        ev = P.val("sDVE")
                        o_evac_at[(half, nt, mt)] = ev
                        o_evac_at[("x", gx)] = ev
                        o_evac_at[("g", gn, mt)] = ev
                while sct[0] % 4:
                    sct[0] += 1
                base0 = (sct[0] % N_SLOTS) * 12
                sss = [stat_slot() for _ in range(4)]

                def scol(j, base0=base0):
                    v = stats[:, base0:base0 + 48].rearrange(
                        "p (s c) -> p s c", c=12)
                    return v[:, 0:MT_HALF, j]

                sq_last = 0
                for mt in range(MT_HALF):
                    ss = sss[mt % 4]
                    last_ev = o_evac_at[(half, NON - 1, mt)]
                    P.wait("vector", "sDVE", last_ev)
                    P.op("vector", lambda E, S, mt=mt, ss=ss: E["vector"].reduce_sum(
                        ss[0], r_half[:, mt, :], axis=AX.X, negate=True))
                    P.wait("scalar", "sDVE", last_ev)
                    if prev_p4sq:
                        P.wait("scalar", "sACT", prev_p4sq)
                    P.op("scalar", lambda E, S, mt=mt, ss=ss: E["scalar"].activation(
                        out=p4dump[:, :], in_=r_half[:, mt, :], func=AF.Square,
                        bias=c_zero, accum_out=ss[3]).then_inc(S["sACT"], 1))
                    sq_last = P.inc("sACT", 1)
                    prev_p4sq = sq_last
                P.wait("vector", "sACT", sq_last)
                P.op("vector", lambda E, S, scol=scol: E["vector"].tensor_scalar(
                    out=scol(1), in0=scol(0), scalar1=1.0 / D, scalar2=None,
                    op0=ALU.mult))
                P.op("vector", lambda E, S, scol=scol: E["vector"].tensor_tensor(
                    out=scol(2), in0=scol(1), in1=scol(1), op=ALU.mult))
                P.op("vector", lambda E, S, scol=scol: E["vector"].tensor_scalar(
                    out=scol(4), in0=scol(3), scalar1=1.0 / D, scalar2=None,
                    op0=ALU.mult))
                P.op("vector", lambda E, S, scol=scol: E["vector"].tensor_tensor(
                    out=scol(5), in0=scol(4), in1=scol(2), op=ALU.subtract))
                var_done = P.val("sDVE")
                P.wait("scalar", "sDVE", var_done)
                P.op("scalar", lambda E, S, scol=scol: E["scalar"].activation(
                    out=scol(6), in_=scol(5), func=AF.Sqrt, scale=1.0,
                    bias=c_eps_ln).then_inc(S["sACT"], 1))
                sd_done = P.inc("sACT", 1)
                P.wait("vector", "sACT", sd_done)
                P.op("vector", lambda E, S, scol=scol: E["vector"].reciprocal(
                    scol(7), scol(6)))
                P.op("vector", lambda E, S, scol=scol: E["vector"].tensor_tensor(
                    out=scol(8), in0=scol(1), in1=scol(7), op=ALU.mult))
                rstd_done = P.val("sDVE")
                P.wait("scalar", "sDVE", rstd_done)
                yst_at = {}
                for mt in range(MT_HALF):
                    mtg = half * MT_HALF + mt
                    ss = sss[mt % 4]
                    ysl = y_tm_f32[:, 0:D]
                    prev = yst_at.get(mt - 1)
                    if prev:
                        P.wait("scalar", "sYst", prev)
                    P.op("scalar", lambda E, S, mt=mt, ss=ss, ysl=ysl:
                         E["scalar"].activation(
                             out=ysl, in_=r_half[:, mt, :], func=AF.Identity,
                             bias=ss[8], scale=ss[7]).then_inc(S["sACT"], 1))
                    y_done = P.inc("sACT", 1)
                    P.wait("scalar", "sACT", y_done)
                    P.op("scalar", lambda E, S, mtg=mtg, ysl=ysl:
                         E["scalar"].dma_start(
                             out=y_d[mtg * 128:(mtg + 1) * 128, :], in_=ysl)
                         .then_inc(S["sYst"], 16))
                    ys = P.inc("sYst", 16)
                    yst_at[mt] = ys
                    yb = P.inc("sYbf", 16)

                    def ybf_store(E, S, mtg=mtg, ys=ys):
                        E["gpsimd"].wait_ge(S["sYst"], ys)
                        E["gpsimd"].dma_start(
                            out=ybf_d[mtg * 128:(mtg + 1) * 128, :],
                            in_=y_d[mtg * 128:(mtg + 1) * 128, :])                            .then_inc(S["sYbf"], 16)
                    pending_ybf.append(ybf_store)
                prev_half_yst = P.val("sYst")
                prev_half_ybf = P.val("sYbf")
            for fn in pending_ybf:
                P.ops["gpsimd"].append(fn)
            pending_ybf = []
            ybf_all_stored = P.val("sYbf")
            y_all_stored = P.val("sYst")
            p4_pe_done = P.val("sPE")
            p4_act_done = P.val("sACT")
            p4_dve_done = P.val("sDVE")

            wup_r = wupT.rearrange("(n p2) f -> p2 n f", p2=128)
            wgate_r = wgateT.rearrange("(n p2) f -> p2 n f", p2=128)
            wdn_r = wdownT.rearrange("(f p2) d -> p2 f d", p2=128)
            ug_idx = 0
            ug_free_at = {}
            h_done_at = {}
            dn_idx = 0
            dn_free_at = {}
            dn_evac_at = {}
            dn_st_at = {}
            dpst_total = (0, 0)
            r2_stored = (0, 0)
            for fpass in range(2):
                ft0 = fpass * FT_PASS
                nft_p = min(FT_PASS, NFT - ft0)
                if fpass == 0:
                    P.wait("sync", "sACT", p4_act_done)
                    P.wait("sync", "sDVE", p4_dve_done)
                    P.wait("sync", "sYbf", ybf_all_stored)
                    P.wait("sync", "sYst", y_all_stored)
                    P.wait("scalar", "sDVE", p4_dve_done)
                    P.wait("scalar", "sYbf", ybf_all_stored)
                    P.wait("scalar", "sYst", y_all_stored)
                else:
                    P.wait("sync", "sPE", dn_free_at[("pass", fpass - 1)])
                    P.wait("scalar", "sPE", dn_free_at[("pass", fpass - 1)])
                for kt in range(NKT):
                    P.op("sync", lambda E, S, kt=kt: E["sync"].dma_start_transpose(
                        out=yfm_sb[:, kt, :],
                        in_=ybf_d[:, kt * 128:(kt + 1) * 128])
                        .then_inc(S["sYfm"], 16))
                    P.inc("sYfm", 16)
                yfm_ld = P.val("sYfm")
                for fl in range(nft_p):
                    ft = ft0 + fl
                    li = ug_idx; ug_idx += 1
                    buf = li % 2
                    ug_sem = "sUG%d" % buf
                    if li >= 2:
                        P.wait("gpsimd", "sPE", ug_free_at[li - 2])
                    elif fpass == 0:
                        P.wait("gpsimd", "sPE", p4_pe_done)
                    if fpass == 1 and fl < 2:
                        P.wait("gpsimd", "sDPst0", dpst_total[0])
                        P.wait("gpsimd", "sDPst1", dpst_total[1])
                    P.op("gpsimd", lambda E, S, buf=buf, ft=ft, us=ug_sem:
                         E["gpsimd"].dma_start(
                             out=ug_slab[buf][:, 0:NKT * 128]
                             .rearrange("p (n f) -> p n f", f=128),
                             in_=wup_r[:, :, ft * 128:(ft + 1) * 128])
                         .then_inc(S[us], 16))
                    P.inc(ug_sem, 16)
                    P.op("gpsimd", lambda E, S, buf=buf, ft=ft, us=ug_sem:
                         E["gpsimd"].dma_start(
                             out=ug_slab[buf][:, NKT * 128:2 * NKT * 128]
                             .rearrange("p (n f) -> p n f", f=128),
                             in_=wgate_r[:, :, ft * 128:(ft + 1) * 128])
                         .then_inc(S[us], 16))
                    ug_ld = P.inc(ug_sem, 16)
                    po = (fl % 2) * 2048
                    P.wait("tensor", ug_sem, ug_ld)
                    if fl == 0:
                        P.wait("tensor", "sYfm", yfm_ld)
                    prev_h = h_done_at.get((fpass, fl - 2))
                    if prev_h:
                        P.wait("tensor", "sDVE", prev_h)
                    TCH = min(512, T)

                    def ugmm(E, S, buf=buf, po=po, TCH=TCH):
                        us = ug_slab[buf][:, 0:NKT * 128].rearrange(
                            "p (n f) -> p n f", f=128)
                        gs = ug_slab[buf][:, NKT * 128:2 * NKT * 128].rearrange(
                            "p (n f) -> p n f", f=128)
                        for hf, sl in ((0, us), (1, gs)):
                            for tc in range(T // TCH):
                                for kt in range(NKT):
                                    E["tensor"].matmul(
                                        PS[:, po + hf * 1024 + tc * TCH:
                                           po + hf * 1024 + (tc + 1) * TCH],
                                        sl[:, kt, :],
                                        yfm_sb[:, kt, tc * TCH:(tc + 1) * TCH],
                                        start=(kt == 0), stop=(kt == NKT - 1))\
                                        .then_inc(S["sPE"], 1)
                    P.op("tensor", ugmm)
                    ug_done = P.inc("sPE", 2 * (T // TCH) * NKT)
                    ug_free_at[li] = ug_done
                    P.wait("scalar", "sPE", ug_done)
                    prev_hh = h_done_at.get((fpass, fl - 1))
                    if prev_hh:
                        P.wait("scalar", "sDVE", prev_hh)
                    P.op("scalar", lambda E, S, po=po: E["scalar"].activation(
                        out=gt_stage[:, 0:T], in_=PS[:, po + 1024:po + 1024 + T],
                        func=AF.Silu, bias=c_zero).then_inc(S["sACT"], 1))
                    silu_done = P.inc("sACT", 1)
                    P.wait("vector", "sACT", silu_done)
                    if fpass == 1 and fl < 2:
                        P.wait("vector", "sPE", dn_free_at[("pass", 0)])
                    P.op("vector", lambda E, S, fl=fl, po=po:
                         E["vector"].tensor_tensor(
                             out=hview(fl), in0=gt_stage[:, 0:T],
                             in1=PS[:, po:po + T], op=ALU.mult))
                    h_done_at[(fpass, fl)] = P.val("sDVE")
                h_all = P.val("sDVE")
                for nt in range(NDN):
                    li = dn_idx; dn_idx += 1
                    buf = li % 2
                    dn_sem = "sDN%d" % buf
                    if li >= 2:
                        P.wait("gpsimd", "sPE", dn_free_at[li - 2])
                    P.wait("gpsimd", "sPE", ug_free_at[ug_idx - 1])
                    P.op("gpsimd", lambda E, S, buf=buf, nt=nt, ft0=ft0,
                         nft_p=nft_p, ds=dn_sem: E["gpsimd"].dma_start(
                             out=dn_slab[buf][:, 0:nft_p, :],
                             in_=wdn_r[:, ft0:ft0 + nft_p, nt * DN:(nt + 1) * DN])
                         .then_inc(S[ds], 16))
                    dn_ld = P.inc(dn_sem, 16)
                    for mt in range(NMT):
                        bk_i = mt % 8
                        P.wait("tensor", dn_sem, dn_ld)
                        if nt == 0 and mt == 0:
                            P.wait("tensor", "sDVE", h_all)
                        prev_ev = dn_evac_at.get((fpass, nt - 1, mt))
                        if prev_ev:
                            P.wait("tensor", "sACT" if fpass == 0 else "sDVE",
                                   prev_ev)

                        def dnmm(E, S, nt=nt, mt=mt, buf=buf, bk_i=bk_i,
                                 nft_p=nft_p):
                            for fl in range(nft_p):
                                E["tensor"].matmul(
                                    bank(bk_i)[:, 0:DN],
                                    hview(fl)[:, mt * 128:(mt + 1) * 128],
                                    dn_slab[buf][:, fl, :],
                                    start=(fl == 0), stop=(fl == nft_p - 1))\
                                    .then_inc(S["sPE"], 1)
                        P.op("tensor", dnmm)
                        mm_done = P.inc("sPE", nft_p)
                        if mt == NMT - 1:
                            dn_free_at[li] = mm_done
                        li2 = nt * NMT + mt
                        stg = dstg[li2 % 2]
                        if fpass == 0:
                            dpst_sem = "sDPst%d" % (li2 % 2)
                            P.wait("scalar", "sPE", mm_done)
                            prev_st = dn_st_at.get((fpass, li2 - 2))
                            if prev_st:
                                P.wait("scalar", dpst_sem, prev_st)
                            P.op("scalar", lambda E, S, stg=stg, bk_i=bk_i:
                                 E["scalar"].copy(
                                     out=stg, in_=bank(bk_i)[:, 0:DN])
                                 .then_inc(S["sACT"], 1))
                            ev = P.inc("sACT", 1)
                            dn_evac_at[(fpass, nt, mt)] = ev
                            P.wait("scalar", "sACT", ev)
                            P.op("scalar", lambda E, S, stg=stg, mt=mt, nt=nt,
                                 ds=dpst_sem: E["scalar"].dma_start(
                                     out=dpart_d[mt * 128:(mt + 1) * 128,
                                                 nt * DN:(nt + 1) * DN], in_=stg)
                                 .then_inc(S[ds], 16))
                            dn_st_at[(fpass, li2)] = P.inc(dpst_sem, 16)
                        else:
                            dp = dpin[li2 % 2]
                            dp_sem = "sDP%d" % (li2 % 2)
                            prev_dp = dn_evac_at.get(("dp", li2 - 2))
                            if prev_dp:
                                P.wait("sync", "sDVE", prev_dp)
                            if li2 == 0:
                                P.wait("sync", "sDPst0", dpst_total[0])
                                P.wait("sync", "sDPst1", dpst_total[1])
                                P.wait("sync", "sPE", ug_free_at[ug_idx - 1])
                            P.op("sync", lambda E, S, dp=dp, mt=mt, nt=nt,
                                 ds=dp_sem: E["sync"].dma_start(
                                     out=dp, in_=dpart_d[mt * 128:(mt + 1) * 128,
                                                         nt * DN:(nt + 1) * DN])
                                 .then_inc(S[ds], 16))
                            dp_ld = P.inc(dp_sem, 16)
                            P.wait("vector", "sPE", mm_done)
                            P.wait("vector", dp_sem, dp_ld)
                            r2_sem = "sR2st%d" % (li2 % 2)
                            prev_st = dn_st_at.get((fpass, li2 - 2))
                            if prev_st:
                                P.wait("vector", r2_sem, prev_st)
                            P.op("vector", lambda E, S, stg=stg, dp=dp, bk_i=bk_i:
                                 E["vector"].tensor_tensor(
                                     out=stg, in0=bank(bk_i)[:, 0:DN], in1=dp,
                                     op=ALU.add))
                            ev = P.val("sDVE")
                            dn_evac_at[(fpass, nt, mt)] = ev
                            dn_evac_at[("dp", li2)] = ev
                            P.wait("scalar", "sDVE", ev)
                            P.op("scalar", lambda E, S, stg=stg, mt=mt, nt=nt,
                                 rs=r2_sem: E["scalar"].dma_start(
                                     out=r2_d[mt * 128:(mt + 1) * 128,
                                              nt * DN:(nt + 1) * DN], in_=stg)
                                 .then_inc(S[rs], 16))
                            dn_st_at[(fpass, li2)] = P.inc(r2_sem, 16)
                dn_free_at[("pass", fpass)] = P.val("sPE")
                if fpass == 0:
                    dpst_total = (P.val("sDPst0"), P.val("sDPst1"))
                else:
                    r2_stored = (P.val("sR2st0"), P.val("sR2st1"))
            p5_pe_done = P.val("sPE")
            p5_dve_done = P.val("sDVE")

            out_at = {}
            ln2_act_at = {}
            for mt in range(NMT):
                ss = stat_slot()
                p6_sem = "sP6%d" % (mt % 2)
                if mt % 2 == 0:
                    r2s = Bf32[:, 0:D]; yss = Bf32[:, D:2 * D]
                else:
                    r2s = Af32[:, 0:D]; yss = Af32[:, D:2 * D]
                rsum = Bf32[:, 2 * D:3 * D]
                dmp = Bf32[:, 3 * D:4 * D]
                if mt == 0:
                    P.wait("sync", "sPE", p5_pe_done)
                    P.wait("sync", "sDVE", p5_dve_done)
                    P.wait("sync", "sR2st0", r2_stored[0])
                    P.wait("sync", "sR2st1", r2_stored[1])
                    P.wait("sync", "sYst", y_all_stored)
                    P.wait("scalar", "sPE", p5_pe_done)
                    P.wait("scalar", "sDVE", p5_dve_done)
                    P.wait("scalar", "sYst", y_all_stored)
                prev = out_at.get(mt - 2)
                if prev:
                    P.wait("sync", "sOUT", prev)
                    P.wait("scalar", "sOUT", prev)
                P.op("sync", lambda E, S, r2s=r2s, mt=mt, ps=p6_sem:
                     E["sync"].dma_start(
                         out=r2s, in_=r2_d[mt * 128:(mt + 1) * 128, :])
                     .then_inc(S[ps], 16))
                P.inc(p6_sem, 16)
                P.op("scalar", lambda E, S, yss=yss, mt=mt, ps=p6_sem:
                     E["scalar"].dma_start(
                         out=yss, in_=y_d[mt * 128:(mt + 1) * 128, :])
                     .then_inc(S[ps], 16))
                ld = P.inc(p6_sem, 16)
                P.wait("vector", p6_sem, ld)
                pa = ln2_act_at.get(mt - 1)
                if pa:
                    P.wait("vector", "sACT", pa)
                P.op("vector", lambda E, S, r2s=r2s, yss=yss, rsum=rsum:
                     E["vector"].tensor_tensor(out=rsum, in0=r2s, in1=yss,
                                               op=ALU.add))
                P.op("vector", lambda E, S, rsum=rsum, ss=ss:
                     E["vector"].reduce_sum(ss[0], rsum, axis=AX.X, negate=True))
                P.op("vector", lambda E, S, ss=ss: E["vector"].tensor_scalar(
                    out=ss[1], in0=ss[0], scalar1=1.0 / D, scalar2=None,
                    op0=ALU.mult))
                P.op("vector", lambda E, S, ss=ss: E["vector"].tensor_tensor(
                    out=ss[2], in0=ss[1], in1=ss[1], op=ALU.mult))
                mu_done = P.val("sDVE")
                P.wait("scalar", "sDVE", mu_done)
                po6 = out_at.get(mt - 1)
                if po6:
                    P.wait("scalar", "sOUT", po6)
                P.op("scalar", lambda E, S, rsum=rsum, dmp=dmp, ss=ss:
                     E["scalar"].activation(out=dmp, in_=rsum, func=AF.Square,
                                            bias=c_zero, accum_out=ss[3])
                     .then_inc(S["sACT"], 1))
                sq_done = P.inc("sACT", 1)
                P.wait("vector", "sACT", sq_done)
                P.op("vector", lambda E, S, ss=ss: E["vector"].tensor_scalar(
                    out=ss[4], in0=ss[3], scalar1=1.0 / D, scalar2=ss[2],
                    op0=ALU.mult, op1=ALU.subtract))
                var_done = P.val("sDVE")
                P.wait("scalar", "sDVE", var_done)
                P.op("scalar", lambda E, S, ss=ss: E["scalar"].activation(
                    out=ss[5], in_=ss[4], func=AF.Sqrt, scale=1.0, bias=c_eps_ln)
                    .then_inc(S["sACT"], 1))
                sd_done = P.inc("sACT", 1)
                P.wait("vector", "sACT", sd_done)
                P.op("vector", lambda E, S, ss=ss: E["vector"].reciprocal(
                    ss[6], ss[5]))
                P.op("vector", lambda E, S, ss=ss: E["vector"].tensor_tensor(
                    out=ss[7], in0=ss[1], in1=ss[6], op=ALU.mult))
                rstd_done = P.val("sDVE")
                P.wait("scalar", "sDVE", rstd_done)
                P.op("scalar", lambda E, S, rsum=rsum, dmp=dmp, ss=ss:
                     E["scalar"].activation(out=dmp, in_=rsum, func=AF.Identity,
                                            bias=ss[7], scale=ss[6])
                     .then_inc(S["sACT"], 1))
                o_done = P.inc("sACT", 1)
                ln2_act_at[mt] = o_done
                P.wait("sync", "sACT", o_done)
                P.op("sync", lambda E, S, dmp=dmp, mt=mt: E["sync"].dma_start(
                    out=out_d[mt * 128:(mt + 1) * 128, :], in_=dmp)
                    .then_inc(S["sOUT"], 16))
                out_at[mt] = P.inc("sOUT", 16)

            if debug:
                P.wait("sync", "sACT", negm_ready)
                P.op("sync", lambda E, S: E["sync"].dma_start(
                    out=dbg["dbg_negm"], in_=negm[0:1, :]).then_inc(S["sOUT"], 16))
                P.inc("sOUT", 16)
                dump_list = [("dbg_attntm", attntm_d), ("dbg_y", y_d),
                             ("dbg_r2", r2_d)]
                if fused:
                    dump_list += [("dbg_qtm", qtm_sp), ("dbg_ktm_all", ktm_all),
                                  ("dbg_vtm_all", vtm_all)]
                for dname, srcd in dump_list:
                    P.op("sync", lambda E, S, d=dbg[dname], s=srcd:
                         E["sync"].dma_start(out=d, in_=s).then_inc(S["sOUT"], 16))
                    P.inc("sOUT", 16)
            P.wait("sync", "sOUT", P.val("sOUT"))

        with nc.Block() as blk:
            E = {}

            @blk.tensor
            def _(e):
                E["tensor"] = e
                for f in P.ops["tensor"]:
                    f(E, sems)

            @blk.scalar
            def _(e):
                E["scalar"] = e
                for f in P.ops["scalar"]:
                    f(E, sems)

            @blk.vector
            def _(e):
                E["vector"] = e
                for f in P.ops["vector"]:
                    f(E, sems)

            @blk.gpsimd
            def _(e):
                E["gpsimd"] = e
                for f in P.ops["gpsimd"]:
                    f(E, sems)

            @blk.sync
            def _(e):
                E["sync"] = e
                for f in P.ops["sync"]:
                    f(E, sems)

    return nc


def make_in_maps_A(cfg, inputs):
    D = cfg["D"]; S = cfg["S"]; T = S // NCORES
    x = np.asarray(inputs["L0_attn_hidden_states"]).reshape(S, D)
    common = dict(
        wq=np.asarray(inputs["L0_attn_Wq"]), wk=np.asarray(inputs["L0_attn_Wk"]),
        wv=np.asarray(inputs["L0_attn_Wv"]),
        bq=np.asarray(inputs["L0_attn_bq"]).reshape(1, D),
        bk=np.asarray(inputs["L0_attn_bk"]).reshape(1, D),
        bv=np.asarray(inputs["L0_attn_bv"]).reshape(1, D),
        gq=np.asarray(inputs["L0_attn_gq"]).reshape(1, D),
        gk=np.asarray(inputs["L0_attn_gk"]).reshape(1, D),
        ones_in=np.ones((1, 128), dtype=ml_dtypes.bfloat16),
        identf_in=np.eye(128, dtype=np.float32),
    )
    maps = []
    for c in range(NCORES):
        m = dict(common)
        m["x_fm"] = np.ascontiguousarray(x[c * T:(c + 1) * T, :].T)
        maps.append(m)
    return maps


def make_in_maps_B(cfg, inputs, resA):
    D = cfg["D"]; S = cfg["S"]; T = S // NCORES
    H = cfg["H"]; SK = S // 128
    x = np.asarray(inputs["L0_attn_hidden_states"]).reshape(S, D)
    ktm_cat = np.concatenate([np.asarray(r["ktm_sh"]) for r in resA], axis=0)
    vtm_cat = np.concatenate([np.asarray(r["vtm_sh"]) for r in resA], axis=0)
    ktm_all = np.ascontiguousarray(ktm_cat.T)
    vtm_all = np.ascontiguousarray(
        vtm_cat.reshape(SK, 128, H, 128).transpose(1, 2, 0, 3).reshape(128, -1))
    kn_all = np.concatenate([np.asarray(r["kn_sh"]) for r in resA], axis=0)
    common = dict(
        wo=np.asarray(inputs["L0_attn_Wo"]),
        wupT=np.ascontiguousarray(np.asarray(inputs["L0_ffn_W_up"]).T),
        wgateT=np.ascontiguousarray(np.asarray(inputs["L0_ffn_W_gate"]).T),
        wdownT=np.ascontiguousarray(np.asarray(inputs["L0_ffn_W_down"]).T),
        bo=np.asarray(inputs["L0_attn_bo"]).reshape(1, D),
        ones_in=np.ones((1, 128), dtype=ml_dtypes.bfloat16),
        onesf_in=np.ones((1, 128), dtype=np.float32),
        identf_in=np.eye(128, dtype=np.float32),
        ktm_all=ktm_all, vtm_all=vtm_all, kn_all=kn_all,
    )
    maps = []
    for c in range(NCORES):
        m = dict(common)
        m["x_tm"] = np.ascontiguousarray(x[c * T:(c + 1) * T, :])
        m["qtm_in"] = np.ascontiguousarray(np.asarray(resA[c]["qtm_sp"]).T)
        m["qn_in"] = np.asarray(resA[c]["qn_sh"])
        maps.append(m)
    return maps


_NC_CACHE = {}


def _get_programs(cfg):
    key = (id(cfg), cfg.get("debug", False))
    if key not in _NC_CACHE:
        _NC_CACHE[key] = (build_program(cfg, part="A"), build_program(cfg, part="B"))
    return _NC_CACHE[key]


def kernel(**inputs):
    cfg = FULL_CFG
    S = cfg["S"]; D = cfg["D"]
    ncA, ncB = _get_programs(cfg)
    resA = run_bass_kernel_spmd(ncA, make_in_maps_A(cfg, inputs),
                                core_ids=list(range(NCORES))).results
    resB = run_bass_kernel_spmd(ncB, make_in_maps_B(cfg, inputs, resA),
                                core_ids=list(range(NCORES))).results
    out = np.concatenate([np.asarray(r["out"]) for r in resB], axis=0)
    return out.reshape(1, S, D).astype(np.float32)
